# version 04
# baseline (speedup 1.0000x reference)
import numpy as np
from contextlib import ExitStack
import concourse.bass as bass
import concourse.mybir as mybir
from concourse.bass_isa import ReduceOp

F32 = mybir.dt.float32
F32R = mybir.dt.float32r
BF16 = mybir.dt.bfloat16
F16 = mybir.dt.float16
AF = mybir.ActivationFunctionType
ALU = mybir.AluOpType
AX = mybir.AxisListType

ENGINES = ("v", "s", "t", "g", "y")
NQ = 12

class Prog:

    def __init__(self, nc):
        self.nc = nc
        self.ops = {e: [] for e in ENGINES}
        self.count = {}
        self.sem_names = ["SM_" + e for e in ENGINES]
        self.qpool = {}
        for q in ("y", "s", "g"):
            self.qpool[q] = ["Q%s%d" % (q, i) for i in range(NQ)]
            self.sem_names += self.qpool[q]
        for s_ in self.sem_names:
            self.count[s_] = 0
        self.qidx = {q: 0 for q in self.qpool}
        self.eng_sem = {e: "SM_" + e for e in ENGINES}
        self.wm = {e: {} for e in ENGINES}
        self.sem_ctx = None

    def _wait(self, e, deps):
        own = self.eng_sem[e]
        for d in deps:
            if d is None:
                continue
            sem, val = d
            if val <= 0:
                continue
            if e == "t" and sem == own:
                continue
            cur = self.wm[e].get(sem, 0)
            if val > cur:
                self.wm[e][sem] = val
                self.ops[e].append(("w", sem, val))

    def op(self, e, emit, deps=()):
        self._wait(e, deps)
        sem = self.eng_sem[e]
        self.count[sem] += 1
        self.ops[e].append(("i", emit, sem))
        return (sem, self.count[sem])

    def q_alloc(self, e):
        pool = self.qpool[e]
        sem = pool[self.qidx[e] % NQ]
        self.qidx[e] += 1
        return sem, self.count[sem]

    def dma(self, e, emit, deps=()):
        sem, prior = self.q_alloc(e)
        alldeps = list(deps)
        if prior > 0:
            alldeps.append((sem, prior))
        self._wait(e, alldeps)
        self.count[sem] += 16
        self.ops[e].append(("i", emit, sem))
        return (sem, self.count[sem])

    def raw(self, e, emit, deps=()):
        self._wait(e, deps)
        self.ops[e].append(("r", emit))
        return None

    def barrier(self, e, deps):
        self._wait(e, deps)

    def build_blocks(self, block):
        nc = self.nc
        handles = self.sem_ctx
        engmap = {"v": nc.vector, "s": nc.scalar, "t": nc.tensor, "g": nc.gpsimd, "y": nc.sync}
        def mk(e):
            eng = engmap[e]
            def body(_):
                for item in self.ops[e]:
                    if item[0] == "w":
                        eng.wait_ge(handles[item[1]], item[2])
                    elif item[0] == "i":
                        inst = item[1](eng, nc)
                        inc = 16 if item[2] not in self.eng_sem.values() else 1
                        inst.then_inc(handles[item[2]], inc)
                    else:
                        item[1](eng, nc)
            return body
        for e in ENGINES:
            if self.ops[e]:
                getattr(block, {"v": "vector", "s": "scalar", "t": "tensor", "g": "gpsimd", "y": "sync"}[e])(mk(e))


class TBuf:
    __slots__ = ("wtok", "rtoks")
    def __init__(self):
        self.wtok = None
        self.rtoks = []
    def rdeps(self):
        return [self.wtok] if self.wtok else []
    def wdeps(self):
        d = list(self.rtoks)
        if self.wtok:
            d.append(self.wtok)
        return d
    def setw(self, tok):
        self.wtok = tok
        self.rtoks = []
    def addr(self, tok):
        if tok:
            self.rtoks.append(tok)


def finalize(nc):
    import bass_rust
    import concourse.mybir as _mb
    from concourse.library_config import all_libraries, standard
    mask = {}
    for lib in all_libraries:
        for it in lib.instructions:
            mask[it] = mask.get(it, 0) | (1 << lib.index)
    bass_rust.insert_library_loads(nc, mask, len(all_libraries), standard.index)
    _mb.codegen_inst_isa_subclasses(nc)
    return nc


D = 2048
DT = 16
R_LO = 64
NST = 16
E_XD = 96
H = 5632
HT = 44
L = 8192
LC = 1024
PW = 512
TC = 1024
FW = 256
NFP = LC // FW


def build_fused(wcols, debug=False):
    nc = bass.Bass("TRN2", target_bir_lowering=False, debug=False)
    xTr = nc.dram_tensor("xTr", [D, LC], F32R, kind="ExternalInput").ap()
    xTf = nc.dram_tensor("xTf", [D, LC], F32, kind="ExternalInput").ap()
    wxpT = nc.dram_tensor("wxpT", [D, E_XD], F32R, kind="ExternalInput").ap()
    wdtT = nc.dram_tensor("wdtT", [R_LO, 256], F32, kind="ExternalInput").ap()
    bdt = nc.dram_tensor("bdt", [128, 2], F32, kind="ExternalInput").ap()
    alog = nc.dram_tensor("alog", [128, 32], F32, kind="ExternalInput").ap()
    dpar = nc.dram_tensor("dpar", [128, 2], F32, kind="ExternalInput").ap()
    ones1 = nc.dram_tensor("ones1", [1, 128], F32, kind="ExternalInput").ap()
    onescr = nc.dram_tensor("onescr", [128, 1], F32R, kind="ExternalInput").ap()
    onescf = nc.dram_tensor("onescf", [128, 1], F32, kind="ExternalInput").ap()
    ident = nc.dram_tensor("ident", [128, 128], F32R, kind="ExternalInput").ap()
    sel32 = nc.dram_tensor("sel32", [32, 32 * 128], BF16, kind="ExternalInput").ap()
    consts = nc.dram_tensor("consts", [128, 8], F32, kind="ExternalInput").ap()
    w1a = nc.dram_tensor("w1a", [128, wcols], F32R, kind="ExternalInput").ap()
    w3a = nc.dram_tensor("w3a", [128, wcols], F32R, kind="ExternalInput").ap()
    w2a = nc.dram_tensor("w2a", [128, wcols], F32R, kind="ExternalInput").ap()
    w1T = nc.dram_tensor("w1T", [D, H], F32, kind="ExternalInput").ap()
    w3T = nc.dram_tensor("w3T", [D, H], F32, kind="ExternalInput").ap()
    w2T = nc.dram_tensor("w2T", [H, D], F32, kind="ExternalInput").ap()
    out_o = nc.dram_tensor("outT", [D, LC], F32, kind="ExternalOutput").ap()
    xn1h = nc.dram_tensor("xn1h", [256, L], F32, kind="Internal").ap()
    xdbh = nc.dram_tensor("xdbh", [E_XD, L], F32, kind="Internal").ap()
    x2h = nc.dram_tensor("x2h", [D, LC], F32R, kind="Internal").ap()
    hfh = nc.dram_tensor("hfh", [H, FW], F32, kind="Internal").ap()
    if debug:
        dxn = nc.dram_tensor("dbg_xn1h", [256, L], F32, kind="ExternalOutput").ap()
        dxd = nc.dram_tensor("dbg_xdbh", [E_XD, L], F32, kind="ExternalOutput").ap()
        dyr = nc.dram_tensor("dbg_yrecv", [128, 16384], F32, kind="ExternalOutput").ap()
        dx2 = nc.dram_tensor("dbg_x2h", [D, LC], F32, kind="ExternalOutput").ap()

    with ExitStack() as st:
        E = st.enter_context
        A = E(nc.sbuf_tensor("A", [128, 16384], F32))
        Bp = E(nc.sbuf_tensor("Bq", [128, 16384], F32))
        Cp = E(nc.sbuf_tensor("Cq", [128, 8192], F32))
        SELXQ = E(nc.sbuf_tensor("SELXQ", [128, 4096], F32))
        W2Q = E(nc.sbuf_tensor("W2Qt", [128, 5632], BF16))
        IDENT = E(nc.sbuf_tensor("IDENTt", [128, 128], F32R))
        CON = E(nc.sbuf_tensor("CONt", [128, 8], F32))
        ROWS = E(nc.sbuf_tensor("ROWSt", [1, 2560], F32))
        WQS = E(nc.sbuf_tensor("WQSt", [128, 48], F32))
        WROW = E(nc.sbuf_tensor("WROWt", [1, 16], F32))
        SWB = E(nc.sbuf_tensor("SWBt", [128, 16], F32))
        WS = E(nc.sbuf_tensor("WSt", [128, 8], F32))
        ONES = E(nc.sbuf_tensor("ONESt", [1, 128], F32))
        ONESC = E(nc.sbuf_tensor("ONESCt", [128, 1], F32R))
        ONESCF = E(nc.sbuf_tensor("ONESCFt", [128, 1], F32))
        WXP = E(nc.sbuf_tensor("WXPt", [128, DT * E_XD], F32R))
        WDT = E(nc.sbuf_tensor("WDTt", [R_LO, 256], F32))
        BDT = E(nc.sbuf_tensor("BDTt", [128, 2], F32))
        AT = E(nc.sbuf_tensor("ATt", [128, 32], F32))
        DP = E(nc.sbuf_tensor("DPt", [128, 2], F32))
        STT = E(nc.sbuf_tensor("STTt", [128, 32], F32))
        ps1 = E(nc.psum_tensor("ps1", [128, 1024], F32))
        ps2 = E(nc.psum_tensor("ps2", [128, 1024], F32))
        ps3 = E(nc.psum_tensor("ps3", [128, 1024], F32))
        ps4 = E(nc.psum_tensor("ps4", [128, 1024], F32))

        P = Prog(nc)
        extra = ["RSXN", "RSXD", "RSWS", "RSY", "LSR", "PREP", "BAR1"]
        P.sem_names += extra
        for s_ in extra:
            P.count[s_] = 0
        P.sem_ctx = {name: E(nc.semaphore(name)) for name in P.sem_names}
        SEM = P.sem_ctx
        nc.const_aps.register(mybir.dt.float32, 1536.0, CON[0:128, 2:3])

        Bu = lambda: TBuf()
        store_toks = []

        def row(i):
            return ROWS[0:1, i * 512:(i + 1) * 512]

        def frow(i):
            return ROWS[0:1, i * 256:(i + 1) * 256]

        bCON, bONES, bONESC, bONESCF, bIDENT, bSEL = (Bu() for _ in range(6))
        bWXP, bWDT, bBDT, bAT, bDP, bST = (Bu() for _ in range(6))
        bA, bB, bC = Bu(), Bu(), Bu()
        bROWS = Bu()
        bWQS, bWROW, bSWB, bWS = (Bu() for _ in range(4))
        bW2Q, bXQ = Bu(), Bu()
        bps1, bps2, bps3, bps4 = (Bu() for _ in range(4))
        bXN1H, bXDBH, bX2H, bHFH = (Bu() for _ in range(4))

        bCON.setw(P.dma("y", lambda e, n: e.dma_start(out=CON[:], in_=consts[:])))
        bONES.setw(P.dma("y", lambda e, n: e.dma_start(out=ONES[:], in_=ones1[:])))
        bONESC.setw(P.dma("y", lambda e, n: e.dma_start(out=ONESC[:], in_=onescr[:])))
        bONESCF.setw(P.dma("y", lambda e, n: e.dma_start(out=ONESCF[:], in_=onescf[:])))
        bIDENT.setw(P.dma("y", lambda e, n: e.dma_start(out=IDENT[:], in_=ident[:])))
        bSEL.setw(P.dma("y", lambda e, n: e.dma_start(out=SELXQ[0:32, 0:2048].bitcast(BF16), in_=sel32[:])))
        bWXP.setw(P.dma("y", lambda e, n: e.dma_start(
            out=WXP[:].rearrange("p (a t) -> p a t", a=DT),
            in_=wxpT.rearrange("(a p) t -> p a t", p=128))))
        bWDT.setw(P.dma("y", lambda e, n: e.dma_start(out=WDT[:], in_=wdtT[:])))
        bBDT.setw(P.dma("y", lambda e, n: e.dma_start(out=BDT[:], in_=bdt[:])))
        tal = P.dma("y", lambda e, n: e.dma_start(out=AT[:], in_=alog[:]))
        bDP.setw(P.dma("y", lambda e, n: e.dma_start(out=DP[:], in_=dpar[:])))
        t1 = P.op("s", lambda e, n: e.activation(out=AT[:], in_=AT[:], func=AF.Exp), deps=[tal])
        t2 = P.op("v", lambda e, n: e.tensor_scalar(out=AT[:], in0=AT[:], scalar1=-1.0, scalar2=None,
                  op0=ALU.mult), deps=[t1])
        bAT.setw(t2)
        bST.setw(P.op("v", lambda e, n: e.memset(STT[:], 0.0)))

        def rdma_block(body_fn, n_preps, deps, synchronizes=None):
            lsems = []
            alldeps = list(deps)
            for _ in range(n_preps):
                sem, prior = P.q_alloc("g")
                if prior > 0:
                    alldeps.append((sem, prior))
                lsems.append(sem)
            P._wait("g", alldeps)

            def em(e, n, lsems=tuple(lsems)):
                pid = e.partition_id()
                for r in n.Switch(engines=[e], index=[pid], n=8):
                    body_fn(e, n, r, [SEM[x] for x in lsems])
            P.ops["g"].append(("r", em))
            P.count["PREP"] += n_preps
            P._wait("g", [("PREP", P.count["PREP"])])

            def tr(e, n, c=n_preps):
                e.trigger_dma(count=c)
            P.ops["g"].append(("r", tr))
            toks = []
            for sem in lsems:
                P.count[sem] += 16
                toks.append((sem, P.count[sem]))
            return toks

        XTw = Bp[:, 0:8192]
        XNs = Bp[:, 8192:16384]
        SQs = Bp[:, 8192:8704]
        XD = Bp[:, 8704:9216]
        bXT, bXN = Bu(), Bu()
        tmz = P.op("v", lambda e, n: e.memset(Bp[96:128, 8704:9216], 0.0))
        bXN.setw(tmz)
        for hp in range(2):
            csl = slice(hp * PW, (hp + 1) * PW)
            tok = P.dma("y", lambda e, n, csl=csl: e.dma_start(
                out=XTw.bitcast(F32R).rearrange("p (a t) -> p a t", a=DT),
                in_=xTr.rearrange("(a p) t -> p a t", p=128)[:, :, csl]), deps=bXT.wdeps())
            bXT.setw(tok)
            for a in range(DT):
                asl = slice(a * PW, (a + 1) * PW)
                tm = P.op("t", lambda e, n, asl=asl, a=a: e.matmul(
                    out=ps4[0:1, 0:512], lhsT=ONESC[:], rhs=XTw.bitcast(F32R)[:, asl],
                    start=(a == 0), stop=(a == DT - 1)),
                    deps=bXT.rdeps() + bONESC.rdeps() + (bps4.wdeps() if a == 0 else []))
                bXT.addr(tm)
                tsq = P.op("s", lambda e, n, asl=asl: e.activation(out=SQs.bitcast(F32R),
                           in_=XTw.bitcast(F32R)[:, asl], func=AF.Square),
                           deps=bXT.rdeps() + bXN.wdeps())
                bXN.setw(tsq); bXT.addr(tsq)
                tq = P.op("t", lambda e, n, a=a: e.matmul(
                    out=ps4[0:1, 512:1024], lhsT=ONESC[:], rhs=SQs.bitcast(F32R),
                    start=(a == 0), stop=(a == DT - 1)),
                    deps=bXN.rdeps() + bONESC.rdeps() + (bps4.wdeps() if a == 0 else []))
                bXN.addr(tq)
                if a == DT - 1:
                    bps4.setw(tq)
            tm = P.op("v", lambda e, n: e.tensor_scalar(out=row(0), in0=ps4[0:1, 0:512], scalar1=1.0 / D,
                      scalar2=None, op0=ALU.mult), deps=bps4.rdeps() + bROWS.wdeps())
            bROWS.setw(tm); bps4.addr(tm)
            tsq = P.op("v", lambda e, n: e.tensor_scalar(out=row(1), in0=ps4[0:1, 512:1024], scalar1=1.0 / D,
                       scalar2=None, op0=ALU.mult), deps=bps4.rdeps() + bROWS.wdeps())
            bROWS.setw(tsq); bps4.addr(tsq)
            tm2 = P.op("s", lambda e, n: e.activation(out=row(2), in_=row(0), func=AF.Square),
                       deps=[tm] + bROWS.wdeps())
            bROWS.setw(tm2)
            tv = P.op("v", lambda e, n: e.tensor_tensor(out=row(1), in0=row(1), in1=row(2), op=ALU.subtract),
                      deps=[tsq, tm2] + bROWS.wdeps())
            bROWS.setw(tv)
            tsr = P.op("s", lambda e, n: e.activation(out=row(2), in_=row(1), func=AF.Sqrt, bias=CON[0:1, 0:1]),
                       deps=[tv] + bCON.rdeps() + bROWS.wdeps())
            bROWS.setw(tsr)
            trs = P.op("v", lambda e, n: e.reciprocal(out=row(3), in_=row(2)), deps=[tsr] + bROWS.wdeps())
            bROWS.setw(trs)
            tmr = P.op("v", lambda e, n: e.tensor_tensor(out=row(2), in0=row(0), in1=row(3), op=ALU.mult),
                       deps=[trs, tm] + bROWS.wdeps())
            bROWS.setw(tmr)
            tb = P.op("t", lambda e, n: e.matmul(out=ps3[:, 0:512], lhsT=ONES[:], rhs=row(3), start=True,
                      stop=True), deps=[trs] + bONES.rdeps() + bps3.wdeps())
            bps3.setw(tb); bROWS.addr(tb)
            for a in range(DT):
                asl = slice(a * PW, (a + 1) * PW)
                tok = P.op("v", lambda e, n, asl=asl: e.tensor_tensor(out=XNs.bitcast(F32R)[:, asl], in0=XTw[:, asl],
                           in1=ps3[:, 0:512], op=ALU.mult), deps=bps3.rdeps() + bXT.rdeps() + bXN.wdeps())
                bXN.setw(tok); bps3.addr(tok); bXT.addr(tok)
            tb = P.op("t", lambda e, n: e.matmul(out=ps3[:, 0:512], lhsT=ONES[:], rhs=row(2), start=True,
                      stop=True), deps=[tmr] + bONES.rdeps() + bps3.wdeps())
            bps3.setw(tb); bROWS.addr(tb)
            for a in range(DT):
                asl = slice(a * PW, (a + 1) * PW)
                tok = P.op("v", lambda e, n, asl=asl: e.tensor_tensor(out=XTw.bitcast(F32R)[:, asl],
                           in0=XNs[:, asl], in1=ps3[:, 0:512], op=ALU.subtract),
                           deps=bps3.rdeps() + bXN.rdeps() + bXT.wdeps())
                bXT.setw(tok); bps3.addr(tok); bXN.addr(tok)
            for a in range(DT):
                asl = slice(a * PW, (a + 1) * PW)
                wsl = slice(a * E_XD, (a + 1) * E_XD)
                tok = P.op("t", lambda e, n, asl=asl, wsl=wsl, a=a: e.matmul(
                    out=ps1[0:96, 0:512], lhsT=WXP[:, wsl], rhs=XTw.bitcast(F32R)[:, asl],
                    start=(a == 0), stop=(a == DT - 1)),
                    deps=bXT.rdeps() + bWXP.rdeps() + (bps1.wdeps() if a == 0 else []))
                bXT.addr(tok)
                if a == DT - 1:
                    bps1.setw(tok)
            tok = P.op("s", lambda e, n: e.activation(out=XD.bitcast(F32R)[0:96, :], in_=ps1[0:96, 0:512], func=AF.Copy),
                       deps=bps1.rdeps() + bXN.wdeps())
            bXN.setw(tok); bps1.addr(tok)

            def pbody(e, n, r, ls, hp=hp):
                for rr in range(8):
                    dlt = r ^ rr
                    if dlt & 4:
                        dlt ^= 2
                    dests = [None] * 8
                    dests[dlt] = (0, dlt)
                    e.remote_dma_broadcast(
                        out_ap=A[:, r * 2048 + hp * 1024: r * 2048 + hp * 1024 + 1024],
                        in_ap=XTw[:, (2 * rr) * PW:(2 * rr + 2) * PW],
                        remote_sem=SEM["RSXN"], local_sem=ls[rr],
                        rdests=dests).then_inc(SEM["PREP"], 1)
                e.remote_dma_broadcast(
                    out_ap=Cp[:, r * 1024 + hp * 512: r * 1024 + hp * 512 + 512],
                    in_ap=XD[:],
                    remote_sem=SEM["RSXD"], local_sem=ls[8],
                    rdests=[(0, k) for k in range(8)]).then_inc(SEM["PREP"], 1)
            toks = rdma_block(pbody, 9, bXT.rdeps() + bXN.rdeps())
            for tok in toks:
                bXT.addr(tok); bXN.addr(tok)

        P.count["RSXN"] = 32
        P.count["RSXD"] = 32
        sp_toks = []
        for i in range(8):
            for b_ in range(2):
                tok = P.dma("g", lambda e, n, i=i, b_=b_: e.dma_start(
                    out=xn1h[b_ * 128:(b_ + 1) * 128, i * 1024:(i + 1) * 1024].rearrange(
                        "p (hp c) -> p hp c", hp=2),
                    in_=A[:, i * 2048:(i + 1) * 2048].rearrange(
                        "p (hp b c) -> p hp b c", hp=2, b=2)[:, :, b_, :]),
                    deps=[("RSXN", 32)])
                sp_toks.append(tok); bA.addr(tok); bXN1H.setw(tok)
            tok = P.dma("g", lambda e, n, i=i: e.dma_start(
                out=xdbh[:, i * 1024:(i + 1) * 1024],
                in_=Cp[0:96, i * 1024:(i + 1) * 1024]),
                deps=[("RSXD", 32)])
            sp_toks.append(tok); bC.addr(tok); bXDBH.setw(tok)
        if debug:
            tokd = P.dma("g", lambda e, n: e.dma_start(out=dxn[:], in_=xn1h[:]), deps=sp_toks)
            store_toks.append(tokd)
            tokd = P.dma("g", lambda e, n: e.dma_start(out=dxd[:], in_=xdbh[:].bitcast(F32)), deps=sp_toks)
            store_toks.append(tokd)
        P._wait("g", sp_toks)

        bsem, bprior = P.q_alloc("g")
        if bprior > 0:
            P._wait("g", [(bsem, bprior)])

        def bar(e, n, bsem=bsem):
            e.remote_sem_update_broadcast(SEM["BAR1"], SEM[bsem],
                                          rdests=[(0, k) for k in range(8)],
                                          sim_remote_sem_inc=2).then_inc(SEM["PREP"], 1)
        P.ops["g"].append(("r", bar))
        P.count["PREP"] += 1
        P._wait("g", [("PREP", P.count["PREP"])])

        def trb(e, n):
            e.trigger_dma(count=1)
        P.ops["g"].append(("r", trb))
        P.count[bsem] += 16
        P.count["BAR1"] = 16

        DELTA = Bp[:, 0:2048]
        UU = Bp[:, 2048:4096]
        SA = Bp[:, 4096:5120]
        SB = Bp[:, 5120:6144]
        SC = Bp[:, 6144:7168]
        SD = Bp[:, 7168:8192]
        DA = Bp[:, 4096:5120]
        DBX = Bp[:, 5120:6144]
        HH = Bp[:, 6144:7168]
        ZZ = Bp[:, 7168:8192]
        YO4 = lambda c: Bp[:, 8192 + (c % 4) * 2048:8192 + (c % 4) * 2048 + 2048]
        WABS = Bp[:, 10240:15872]
        SEL = SELXQ[0:32, 0:2048].bitcast(BF16)
        bDELTA, bUU, bSAB, bDAB, bYO, bWABS = (Bu() for _ in range(6))
        pgate = bXT.wdeps() + bXN.wdeps()

        wc2 = wcols // 2
        for m, win in enumerate((w1a, w3a, w2a)):
            for piece in range(2):
                tk = P.dma("y", lambda e, n, win=win, piece=piece: e.dma_start(
                    out=WABS.bitcast(F32R)[:], in_=win[:, piece * wc2:(piece + 1) * wc2]),
                    deps=bWABS.wdeps() + pgate)
                bWABS.setw(tk)
                col = 2 * m + piece
                ts_ = P.op("s", lambda e, n, col=col: e.activation(out=WABS.bitcast(F32R)[:], in_=WABS[:], func=AF.Abs,
                           accum_out=WS[:, col:col + 1]), deps=bWABS.rdeps() + bWS.wdeps())
                bWABS.setw(ts_); bWS.setw(ts_)

        def wsbody(e, n, r, ls):
            e.remote_dma_broadcast(
                out_ap=WQS[:, r * 6:(r + 1) * 6], in_ap=WS[:, 0:6],
                remote_sem=SEM["RSWS"], local_sem=ls[0],
                rdests=[(0, k) for k in range(8)]).then_inc(SEM["PREP"], 1)
        toks = rdma_block(wsbody, 1, bWS.rdeps())
        for tok in toks:
            bWS.addr(tok)
        P.count["RSWS"] = 16

        XNW = lambda cb: Cp[:, cb * 2048:(cb + 1) * 2048]
        XDRW = lambda cb: Cp[0:64, 4096 + cb * 1024:4096 + (cb + 1) * 1024]
        XBCF = lambda cb: Cp[0:32, 6144:7168]
        XBH = lambda cb: Cp[0:32, 7168:7680].bitcast(BF16)
        XBL = lambda cb: Cp[0:32, 7680:8192].bitcast(BF16)
        bXNW = [Bu(), Bu()]
        bXDRW = [Bu(), Bu()]
        _bx = Bu()
        bXBCW = [_bx, _bx]
        cgate = bC.wdeps()
        for c in range(8):
            cb = c % 2
            tsl = slice(c * TC, (c + 1) * TC)
            tok = P.dma("y", lambda e, n, tsl=tsl, cb=cb: e.dma_start(
                out=XNW(cb).rearrange("p (b t) -> p b t", b=2),
                in_=xn1h.rearrange("(b p) t -> p b t", p=128)[:, :, tsl]),
                deps=bXNW[cb].wdeps() + sp_toks + (cgate if c < 2 else []))
            bXNW[cb].setw(tok); bXN1H.addr(tok)
            tok = P.dma("y", lambda e, n, tsl=tsl, cb=cb: e.dma_start(
                out=XDRW(cb), in_=xdbh[0:R_LO, tsl]),
                deps=bXDRW[cb].wdeps() + sp_toks + (cgate if c < 2 else []))
            bXDRW[cb].setw(tok); bXDBH.addr(tok)
            tok = P.dma("y", lambda e, n, tsl=tsl, cb=cb: e.dma_start(
                out=XBCF(cb), in_=xdbh[R_LO:E_XD, tsl]),
                deps=bXBCW[cb].wdeps() + sp_toks + (cgate if c < 2 else []))
            bXBCW[cb].setw(tok); bXDBH.addr(tok)
            tok = P.op("v", lambda e, n, cb=cb: e.tensor_copy(out=XBH(cb), in_=XBCF(cb)),
                       deps=bXBCW[cb].rdeps() + bXBCW[cb].wdeps())
            bXBCW[cb].setw(tok)
            tok = P.op("v", lambda e, n, cb=cb: e.tensor_tensor(out=XBL(cb), in0=XBCF(cb),
                       in1=XBH(cb), op=ALU.subtract), deps=bXBCW[cb].rdeps() + bXBCW[cb].wdeps())
            bXBCW[cb].setw(tok)

            for half in range(2):
                hsl = slice(half * TC, (half + 1) * TC)
                for c2 in range(2):
                    c2o = slice(c2 * 512, (c2 + 1) * 512)
                    rsl = slice(c2 * 512, (c2 + 1) * 512)
                    tok = P.op("t", lambda e, n, c2o=c2o, rsl=rsl, half=half, cb=cb: e.matmul(
                        out=ps1[:, c2o], lhsT=WDT[:, half * 128:(half + 1) * 128],
                        rhs=XDRW(cb)[:, rsl], start=True, stop=True),
                        deps=bXDRW[cb].rdeps() + bWDT.rdeps() + bps1.wdeps() + (pgate if c == 0 else []))
                    bps1.setw(tok); bXDRW[cb].addr(tok)
                tok = P.op("s", lambda e, n, half=half: e.activation(out=SA.bitcast(F32R)[:], in_=ps1[:], func=AF.Abs,
                           bias=BDT[:, half:half + 1]), deps=bps1.rdeps() + bBDT.rdeps() + bSAB.wdeps() + (pgate if c == 0 else []))
                bSAB.setw(tok); bps1.addr(tok)
                tok = P.op("s", lambda e, n: e.activation(out=SB.bitcast(F32R)[:], in_=SA[:], func=AF.Exp, scale=-1.0),
                           deps=bSAB.rdeps() + bSAB.wdeps())
                bSAB.setw(tok)
                tok = P.op("s", lambda e, n: e.activation(out=SC.bitcast(F32R)[:], in_=SB[:], func=AF.Ln, bias=CON[0:128, 1:2]),
                           deps=bSAB.rdeps() + bSAB.wdeps() + bCON.rdeps())
                bSAB.setw(tok)
                tok = P.op("s", lambda e, n, half=half: e.activation(out=SD.bitcast(F32R)[:], in_=ps1[:], func=AF.Relu,
                           bias=BDT[:, half:half + 1]), deps=bps1.rdeps() + bBDT.rdeps() + bSAB.wdeps())
                bSAB.setw(tok); bps1.addr(tok)
                tok = P.op("v", lambda e, n, half=half: e.tensor_tensor(
                    out=DELTA.bitcast(F32R)[:, half * TC:(half + 1) * TC], in0=SC[:], in1=SD[:], op=ALU.add),
                    deps=bSAB.rdeps() + bDELTA.wdeps() + (pgate if c == 0 else []))
                bDELTA.setw(tok); bSAB.addr(tok)
                tok = P.op("v", lambda e, n, half=half, cb=cb: e.tensor_tensor(
                    out=UU.bitcast(F32R)[:, half * TC:(half + 1) * TC], in0=DELTA[:, half * TC:(half + 1) * TC],
                    in1=XNW(cb)[:, half * TC:(half + 1) * TC], op=ALU.mult),
                    deps=bDELTA.rdeps() + bXNW[cb].rdeps() + bUU.wdeps() + (pgate if c == 0 else []))
                bUU.setw(tok); bDELTA.addr(tok); bXNW[cb].addr(tok)

            for half in range(2):
                hsl = slice(half * TC, (half + 1) * TC)
                for nn in range(NST):
                    col = nn * 2 + half
                    for c2 in range(2):
                        c2o = slice(c2 * 512, (c2 + 1) * 512)
                        rsl = slice(c2 * 512, (c2 + 1) * 512)
                        for wv, first in ((XBH, True), (XBL, False)):
                            tok = P.op("t", lambda e, n, c2o=c2o, rsl=rsl, nn=nn, cb=cb, wv=wv, first=first: e.matmul(
                                out=ps2[:, c2o], lhsT=SEL[:, nn * 128:(nn + 1) * 128], rhs=wv(cb)[:, rsl],
                                start=first, stop=not first),
                                deps=bXBCW[cb].rdeps() + bSEL.rdeps() + (bps2.wdeps() if first else []))
                            if not first:
                                bps2.setw(tok)
                            bXBCW[cb].addr(tok); bSEL.addr(tok)
                            tok = P.op("t", lambda e, n, c2o=c2o, rsl=rsl, nn=nn, cb=cb, wv=wv, first=first: e.matmul(
                                out=ps3[:, c2o], lhsT=SEL[:, (NST + nn) * 128:(NST + nn + 1) * 128],
                                rhs=wv(cb)[:, rsl], start=first, stop=not first),
                                deps=bXBCW[cb].rdeps() + bSEL.rdeps() + (bps3.wdeps() if first else []))
                            if not first:
                                bps3.setw(tok)
                            bXBCW[cb].addr(tok); bSEL.addr(tok)
                    tok = P.op("s", lambda e, n, half=half, col=col: e.activation(
                        out=DA.bitcast(F32R)[:], in_=DELTA[:, half * TC:(half + 1) * TC], func=AF.Exp,
                        scale=AT[:, col:col + 1]),
                        deps=bDELTA.rdeps() + bAT.rdeps() + bDAB.wdeps())
                    bDAB.setw(tok); bDELTA.addr(tok)
                    tok = P.op("v", lambda e, n, half=half: e.tensor_tensor(
                        out=DBX.bitcast(F32R)[:], in0=UU[:, half * TC:(half + 1) * TC], in1=ps2[:], op=ALU.mult),
                        deps=bUU.rdeps() + bps2.rdeps() + bDAB.wdeps())
                    bDAB.setw(tok); bUU.addr(tok); bps2.addr(tok)
                    tok = P.op("v", lambda e, n, col=col: e.tensor_tensor_scan(
                        out=HH.bitcast(F32R)[:], data0=DA[:], data1=DBX[:], initial=STT[:, col:col + 1],
                        op0=ALU.mult, op1=ALU.add),
                        deps=bDAB.rdeps() + bST.rdeps() + bDAB.wdeps())
                    bDAB.setw(tok); bST.addr(tok)
                    tok = P.op("v", lambda e, n, col=col: e.tensor_copy(out=STT[:, col:col + 1],
                               in_=HH[:, TC - 1:TC]), deps=bDAB.rdeps() + bST.wdeps())
                    bST.setw(tok); bDAB.addr(tok)
                    tok = P.op("v", lambda e, n: e.tensor_tensor(out=ZZ.bitcast(F32R), in0=HH[:],
                               in1=ps3[:], op=ALU.mult),
                               deps=bDAB.rdeps() + bps3.rdeps() + bDAB.wdeps())
                    bDAB.setw(tok); bps3.addr(tok)
                    for c2 in range(2):
                        c2o = slice(c2 * 512, (c2 + 1) * 512)
                        tok = P.op("t", lambda e, n, c2o=c2o, nn=nn: e.matmul(
                            out=ps4[:, c2o], lhsT=IDENT[:], rhs=ZZ.bitcast(F32R)[:, c2o],
                            start=(nn == 0), stop=(nn == NST - 1)),
                            deps=bDAB.rdeps() + bIDENT.rdeps() + (bps4.wdeps() if nn == 0 else []))
                        bDAB.addr(tok)
                        if nn == NST - 1:
                            bps4.setw(tok)
                tok = P.op("v", lambda e, n, half=half, cb=cb, c=c: e.scalar_tensor_tensor(
                    out=YO4(c).bitcast(F32R)[:, half * TC:(half + 1) * TC], in0=XNW(cb)[:, half * TC:(half + 1) * TC],
                    scalar=DP[:, half:half + 1], in1=ps4[:], op0=ALU.mult, op1=ALU.add),
                    deps=bXNW[cb].rdeps() + bDP.rdeps() + bps4.rdeps() + bYO.wdeps() +
                         (pgate if c == 0 else []) + (bWABS.wdeps() if c < 4 else []))
                bYO.setw(tok); bps4.addr(tok); bXNW[cb].addr(tok)

            if c % 4 == 3:
                def ybody(e, n, r, ls, c0=c - 3):
                    for j, cc in enumerate(range(c0, c0 + 4)):
                        dlt = r ^ cc
                        if dlt & 4:
                            dlt ^= 2
                        dests = [None] * 8
                        dests[dlt] = (0, dlt)
                        e.remote_dma_broadcast(
                            out_ap=A[:, r * 2048:(r + 1) * 2048], in_ap=YO4(cc)[:],
                            remote_sem=SEM["RSY"], local_sem=ls[j],
                            rdests=dests).then_inc(SEM["PREP"], 1)
                toks = rdma_block(ybody, 4, bYO.rdeps() + [("BAR1", 16)])
                for tok in toks:
                    bYO.addr(tok)
        P.count["RSY"] = 16

        PC = Bp[:, 0:4096]
        PD = Bp[:, 4096:8192]
        XT2 = Bp[:, 8192:12288]
        W2B = A[:, 0:5632]
        MGF = None
        MGF = Cp[:, 5120:6144].bitcast(F16)
        QH = A[:, 8192:13824].bitcast(BF16)
        w1s = lambda buf: A[:, buf * 2048:(buf + 1) * 2048]
        w3s = lambda buf: A[:, (2 + buf) * 2048:(3 + buf) * 2048]
        def cs(i):
            return Cp[:, i * 256:(i + 1) * 256]
        def bc(i):
            return Cp[:, (13 + i) * 256:(14 + i) * 256]
        SQF = Bp[:, 13824:14080]
        bSQF = Bu()
        QF = Cp[:, 24 * 256:32 * 256].bitcast(F16)
        XQ = SELXQ[:, 0:2048].bitcast(BF16)
        WQT = SELXQ[:, 2048:4096].bitcast(BF16)
        bPC, bPD, bXT2, bMGF, bCS, bBCP, bQF = (Bu() for _ in range(7))
        bW2B = bA
        fgate_b = bWABS.wdeps() + bDELTA.wdeps() + bUU.wdeps() + bSAB.wdeps() + bDAB.wdeps() + bYO.wdeps()
        fgate_c = [t for b in (bXNW + bXDRW + bXBCW) for t in b.wdeps()]
        fgate_a = [("RSY", 16)]
        fgate_x = bSEL.wdeps()

        tts = []
        for m in range(3):
            tmm = P.op("t", lambda e, n, m=m: e.matmul(
                out=ps3[0:1, 0:16], lhsT=ONESCF[:],
                rhs=WQS[:].rearrange("p (r c) -> p r c", c=6)[:, :, 2 * m:2 * m + 2],
                start=True, stop=True),
                deps=[("RSWS", 16)] + bONESCF.rdeps() + bps3.wdeps())
            bps3.setw(tmm)
            trd = P.op("v", lambda e, n, m=m: e.reduce_sum(
                out=WROW[0:1, m:m + 1], in_=ps3[0:1, 0:16], axis=AX.X),
                deps=bps3.rdeps() + bWROW.wdeps())
            bps3.addr(trd); bWROW.setw(trd)
            tts.append(trd)
        NW = float(D) * H
        t_swm = P.op("v", lambda e, n: e.tensor_scalar(out=WROW[0:1, 0:3], in0=WROW[0:1, 0:3],
                     scalar1=1.0 / NW, scalar2=1e-5, op0=ALU.mult, op1=ALU.max),
                     deps=tts + bWROW.wdeps())
        bWROW.setw(t_swm)
        t_inv = P.op("v", lambda e, n: e.reciprocal(out=WROW[0:1, 3:6], in_=WROW[0:1, 0:3]),
                     deps=[t_swm] + bWROW.wdeps())
        bWROW.setw(t_inv)
        t_ncw = P.op("v", lambda e, n: e.tensor_scalar(out=WROW[0:1, 6:9], in0=WROW[0:1, 0:3],
                     scalar1=-1.45, scalar2=None, op0=ALU.mult), deps=[t_swm] + bWROW.wdeps())
        bWROW.setw(t_ncw)
        t_pcw = P.op("v", lambda e, n: e.tensor_scalar(out=WROW[0:1, 9:12], in0=WROW[0:1, 0:3],
                     scalar1=1.45, scalar2=None, op0=ALU.mult), deps=[t_swm] + bWROW.wdeps())
        bWROW.setw(t_pcw)
        tok = P.op("t", lambda e, n: e.matmul(out=ps2[:, 0:12], lhsT=ONES[:], rhs=WROW[0:1, 0:12],
                   start=True, stop=True),
                   deps=[t_swm, t_inv, t_ncw, t_pcw] + bONES.rdeps() + bps2.wdeps())
        bps2.setw(tok); bWROW.addr(tok)
        tok = P.op("s", lambda e, n: e.activation(out=SWB[:, 0:12], in_=ps2[:, 0:12], func=AF.Copy),
                   deps=bps2.rdeps() + bSWB.wdeps())
        bSWB.setw(tok); bps2.addr(tok)

        def swm_c(m):
            return SWB[:, m:m + 1]

        def swinv_c(m):
            return SWB[:, 3 + m:4 + m]

        def ncw_c(m):
            return SWB[:, 6 + m:7 + m]

        def pcw_c(m):
            return SWB[:, 9 + m:10 + m]

        ps_a = ps1[:, 0:256]
        ps_b = ps1[:, 512:768]
        ps_c = ps2[:, 0:256]
        ps_d = ps2[:, 512:768]
        ps_r = ps3[0:1, 0:256]
        ps_r2 = ps3[0:1, 512:768]

        def fbcast(row_ap, slot, deps):
            tok = P.op("t", lambda e, n, row_ap=row_ap: e.matmul(
                out=ps_d, lhsT=ONES[:], rhs=row_ap, start=True, stop=True),
                deps=deps + bONES.rdeps() + bps2.wdeps())
            bps2.setw(tok); bROWS.addr(tok)
            tok2 = P.op("s", lambda e, n, slot=slot: e.activation(out=bc(slot), in_=ps_d, func=AF.Copy),
                        deps=bps2.rdeps() + bBCP.wdeps() + fgate_c + fgate_b)
            bBCP.setw(tok2); bps2.addr(tok2)
            return tok2

        def fstats(TEN, bTEN, r0, extra_deps=()):
            tokl = None
            for a in range(16):
                asl = slice(a * FW, (a + 1) * FW)
                tokl = P.op("t", lambda e, n, asl=asl, a=a: e.matmul(
                    out=ps_r, lhsT=ONESC[:], rhs=TEN.bitcast(F32R)[:, asl],
                    start=(a == 0), stop=(a == 15)),
                    deps=bTEN.rdeps() + bONESC.rdeps() + list(extra_deps) + (bps3.wdeps() if a == 0 else []))
                bTEN.addr(tokl)
            bps3.setw(tokl)
            for a in range(16):
                asl = slice(a * FW, (a + 1) * FW)
                tsq = P.op("s", lambda e, n, asl=asl: e.activation(out=SQF.bitcast(F32R),
                           in_=TEN.bitcast(F32R)[:, asl], func=AF.Square),
                           deps=bTEN.rdeps() + bSQF.wdeps())
                bSQF.setw(tsq); bTEN.addr(tsq)
                tokq = P.op("t", lambda e, n, a=a: e.matmul(
                    out=ps_r2, lhsT=ONESC[:], rhs=SQF.bitcast(F32R),
                    start=(a == 0), stop=(a == 15)),
                    deps=bSQF.rdeps() + bONESC.rdeps() + (bps3.wdeps() if a == 0 else []))
                bSQF.addr(tokq)
            bps3.setw(tokq)
            tm = P.op("v", lambda e, n, r0=r0: e.tensor_scalar(out=frow(r0), in0=ps_r, scalar1=1.0 / D,
                      scalar2=None, op0=ALU.mult), deps=bps3.rdeps() + bROWS.wdeps())
            bROWS.setw(tm); bps3.addr(tm)
            tsq = P.op("v", lambda e, n, r0=r0: e.tensor_scalar(out=frow(r0 + 3), in0=ps_r2, scalar1=1.0 / D,
                       scalar2=None, op0=ALU.mult), deps=bps3.rdeps() + bROWS.wdeps())
            bROWS.setw(tsq); bps3.addr(tsq)
            tm2 = P.op("s", lambda e, n, r0=r0: e.activation(out=frow(r0 + 4), in_=frow(r0), func=AF.Square),
                       deps=[tm] + bROWS.wdeps())
            bROWS.setw(tm2)
            tv = P.op("v", lambda e, n, r0=r0: e.tensor_tensor(out=frow(r0 + 3), in0=frow(r0 + 3),
                      in1=frow(r0 + 4), op=ALU.subtract), deps=[tsq, tm2] + bROWS.wdeps())
            bROWS.setw(tv)
            tsr = P.op("s", lambda e, n, r0=r0: e.activation(out=frow(r0 + 4), in_=frow(r0 + 3), func=AF.Sqrt,
                       bias=CON[0:1, 0:1]), deps=[tv] + bCON.rdeps() + bROWS.wdeps())
            bROWS.setw(tsr)
            trs = P.op("v", lambda e, n, r0=r0: e.reciprocal(out=frow(r0 + 1), in_=frow(r0 + 4)),
                       deps=[tsr] + bROWS.wdeps())
            bROWS.setw(trs)
            tmr = P.op("v", lambda e, n, r0=r0: e.tensor_tensor(out=frow(r0 + 2), in0=frow(r0),
                       in1=frow(r0 + 1), op=ALU.mult), deps=[trs, tm] + bROWS.wdeps())
            bROWS.setw(tmr)
            return trs, tmr

        x2toks = []
        for f in range(NFP):
            csl = slice(f * FW, (f + 1) * FW)
            yw = lambda a, f=f: A[:, a * 1024 + f * 256: a * 1024 + f * 256 + 256]
            tokl = None
            for a in range(16):
                tokl = P.op("t", lambda e, n, a=a, yw=yw: e.matmul(
                    out=ps_r, lhsT=ONESCF[:], rhs=yw(a),
                    start=(a == 0), stop=(a == 15)),
                    deps=fgate_a + bONESCF.rdeps() + (bps3.wdeps() if a == 0 else []))
                bA.addr(tokl)
            bps3.setw(tokl)
            for a in range(16):
                tsq = P.op("s", lambda e, n, a=a, yw=yw: e.activation(out=SQF.bitcast(F32R),
                           in_=yw(a), func=AF.Square),
                           deps=fgate_a + bSQF.wdeps() + fgate_c + fgate_b)
                bSQF.setw(tsq); bA.addr(tsq)
                tokq = P.op("t", lambda e, n, a=a: e.matmul(
                    out=ps_r2, lhsT=ONESC[:], rhs=SQF.bitcast(F32R),
                    start=(a == 0), stop=(a == 15)),
                    deps=bSQF.rdeps() + bONESC.rdeps() + (bps3.wdeps() if a == 0 else []))
                bSQF.addr(tokq)
            bps3.setw(tokq)
            tm = P.op("v", lambda e, n: e.tensor_scalar(out=frow(0), in0=ps_r, scalar1=1.0 / D,
                      scalar2=None, op0=ALU.mult), deps=bps3.rdeps() + bROWS.wdeps())
            bROWS.setw(tm); bps3.addr(tm)
            tsq = P.op("v", lambda e, n: e.tensor_scalar(out=frow(3), in0=ps_r2, scalar1=1.0 / D,
                       scalar2=None, op0=ALU.mult), deps=bps3.rdeps() + bROWS.wdeps())
            bROWS.setw(tsq); bps3.addr(tsq)
            tm2 = P.op("s", lambda e, n: e.activation(out=frow(4), in_=frow(0), func=AF.Square),
                       deps=[tm] + bROWS.wdeps())
            bROWS.setw(tm2)
            tv = P.op("v", lambda e, n: e.tensor_tensor(out=frow(3), in0=frow(3), in1=frow(4),
                      op=ALU.subtract), deps=[tsq, tm2] + bROWS.wdeps())
            bROWS.setw(tv)
            tsr = P.op("s", lambda e, n: e.activation(out=frow(4), in_=frow(3), func=AF.Sqrt,
                       bias=CON[0:1, 0:1]), deps=[tv] + bCON.rdeps() + bROWS.wdeps())
            bROWS.setw(tsr)
            trs = P.op("v", lambda e, n: e.reciprocal(out=frow(1), in_=frow(4)), deps=[tsr] + bROWS.wdeps())
            bROWS.setw(trs)
            tmr = P.op("v", lambda e, n: e.tensor_tensor(out=frow(2), in0=frow(0), in1=frow(1),
                       op=ALU.mult), deps=[trs, tm] + bROWS.wdeps())
            bROWS.setw(tmr)
            b_rstd = fbcast(frow(1), 0, [trs])
            b_mr = fbcast(frow(2), 1, [tmr])
            tok = P.dma("y", lambda e, n, csl=csl: e.dma_start(
                out=XT2.bitcast(F32R).rearrange("p (a t) -> p a t", a=16),
                in_=xTr.rearrange("(a p) t -> p a t", p=128)[:, :, csl]),
                deps=bXT2.wdeps() + fgate_b)
            bXT2.setw(tok)
            for a in range(16):
                asl = slice(a * FW, (a + 1) * FW)
                t1_ = P.op("v", lambda e, n, a=a, asl=asl, yw=yw: e.tensor_tensor(out=PD.bitcast(F32R)[:, asl],
                           in0=yw(a), in1=bc(0), op=ALU.mult),
                           deps=fgate_a + [b_rstd] + bPD.wdeps() + fgate_b)
                bPD.setw(t1_); bA.addr(t1_); bBCP.addr(t1_)
                t2_ = P.op("v", lambda e, n, asl=asl: e.tensor_tensor(out=PD.bitcast(F32R)[:, asl], in0=PD[:, asl],
                           in1=bc(1), op=ALU.subtract), deps=[t1_, b_mr] + bPD.wdeps())
                bPD.setw(t2_); bBCP.addr(t2_)
                t3_ = P.op("v", lambda e, n, asl=asl: e.tensor_tensor(out=PC.bitcast(F32R)[:, asl],
                           in0=PD[:, asl], in1=XT2[:, asl], op=ALU.add),
                           deps=[t2_] + bXT2.rdeps() + bPC.wdeps() + fgate_b)
                bPC.setw(t3_); bXT2.addr(t3_); bPD.addr(t3_)
            tok = P.dma("g", lambda e, n, csl=csl: e.dma_start(
                out=x2h.rearrange("(a p) t -> p a t", p=128)[:, :, csl],
                in_=PC.bitcast(F32R).rearrange("p (a t) -> p a t", a=16)), deps=bPC.rdeps())
            bPC.addr(tok); bX2H.setw(tok); x2toks.append(tok)

        if debug:
            tokd = P.dma("g", lambda e, n: e.dma_start(out=dyr[:], in_=A[:]), deps=fgate_a)
            bA.addr(tokd); store_toks.append(tokd)
            tokd = P.dma("g", lambda e, n: e.dma_start(out=dx2[:], in_=x2h[:].bitcast(F32)), deps=x2toks)
            bX2H.addr(tokd); store_toks.append(tokd)
        hf_wtoks, hf_rtoks = [], []
        for fp in range(NFP):
            csl = slice(fp * FW, (fp + 1) * FW)
            tok = P.dma("y", lambda e, n, csl=csl: e.dma_start(
                out=PC.bitcast(F32R).rearrange("p (a t) -> p a t", a=16),
                in_=x2h.rearrange("(a p) t -> p a t", p=128)[:, :, csl]),
                deps=bPC.wdeps() + [x2toks[fp]])
            bPC.setw(tok); bX2H.addr(tok)
            trs2, tmr2 = fstats(PC, bPC, 0)
            b_rstd2 = fbcast(frow(1), 2, [trs2])
            b_mr2 = fbcast(frow(2), 3, [tmr2])
            for a in range(16):
                asl = slice(a * FW, (a + 1) * FW)
                t1_ = P.op("v", lambda e, n, asl=asl: e.tensor_tensor(out=PD.bitcast(F32R)[:, asl],
                           in0=PC[:, asl], in1=bc(2), op=ALU.mult),
                           deps=bPC.rdeps() + [b_rstd2] + bPD.wdeps())
                bPD.setw(t1_); bPC.addr(t1_); bBCP.addr(t1_)
                t2_ = P.op("v", lambda e, n, asl=asl: e.tensor_tensor(out=PD.bitcast(F32R)[:, asl], in0=PD[:, asl],
                           in1=bc(3), op=ALU.subtract), deps=[t1_, b_mr2] + bPD.wdeps())
                bPD.setw(t2_); bBCP.addr(t2_)
            tok = P.op("v", lambda e, n: e.tensor_copy(out=cs(0), in_=PD[:, 0:FW]),
                       deps=bPD.rdeps() + bCS.wdeps() + fgate_c)
            bCS.setw(tok); bPD.addr(tok)
            tok = P.op("v", lambda e, n: e.tensor_copy(out=cs(1), in_=PD[:, 0:FW]),
                       deps=bPD.rdeps() + bCS.wdeps())
            bCS.setw(tok); bPD.addr(tok)
            for a in range(1, 16):
                tok = P.op("v", lambda e, n, a=a: e.tensor_tensor(out=cs(0), in0=PD[:, a * FW:(a + 1) * FW],
                           in1=cs(0), op=ALU.max), deps=bPD.rdeps() + bCS.rdeps() + bCS.wdeps())
                bCS.setw(tok); bPD.addr(tok)
                tok = P.op("v", lambda e, n, a=a: e.tensor_tensor(out=cs(1), in0=PD[:, a * FW:(a + 1) * FW],
                           in1=cs(1), op=ALU.min), deps=bPD.rdeps() + bCS.rdeps() + bCS.wdeps())
                bCS.setw(tok); bPD.addr(tok)
            tok = P.op("v", lambda e, n: e.tensor_scalar(out=cs(1), in0=cs(1), scalar1=-1.0, scalar2=None,
                       op0=ALU.mult), deps=bCS.rdeps() + bCS.wdeps())
            bCS.setw(tok)
            tok = P.op("v", lambda e, n: e.tensor_tensor(out=cs(0), in0=cs(1), in1=cs(0), op=ALU.max),
                       deps=bCS.rdeps() + bCS.wdeps())
            bCS.setw(tok)
            tok = P.op("g", lambda e, n: e.partition_all_reduce(out_ap=cs(2), in_ap=cs(0),
                       channels=128, reduce_op=ReduceOp.max), deps=bCS.rdeps() + bCS.wdeps())
            bCS.setw(tok)
            t1_ = P.op("v", lambda e, n: e.tensor_scalar(out=cs(2), in0=cs(2), scalar1=1e-5, scalar2=None,
                       op0=ALU.max), deps=bCS.rdeps() + bCS.wdeps())
            bCS.setw(t1_)
            t2_ = P.op("v", lambda e, n: e.reciprocal(out=cs(3), in_=cs(2)), deps=[t1_] + bCS.wdeps())
            bCS.setw(t2_)
            t3_ = P.op("v", lambda e, n: e.tensor_scalar(out=cs(3), in0=cs(3), scalar1=127.0, scalar2=None,
                       op0=ALU.mult), deps=[t2_] + bCS.wdeps())
            bCS.setw(t3_)
            t4_ = P.op("v", lambda e, n: e.tensor_scalar(out=cs(4), in0=cs(2), scalar1=1.0 / 127.0,
                       scalar2=None, op0=ALU.mult), deps=[t1_] + bCS.wdeps())
            bCS.setw(t4_)
            for a in range(16):
                asl = slice(a * FW, (a + 1) * FW)
                t5_ = P.op("v", lambda e, n, asl=asl: e.tensor_tensor(out=PD.bitcast(F32R)[:, asl], in0=PD[:, asl],
                           in1=cs(3), op=ALU.mult), deps=bPD.rdeps() + [t3_] + bPD.wdeps())
                bPD.setw(t5_); bCS.addr(t5_)
                t6_ = P.op("v", lambda e, n, asl=asl: e.tensor_scalar(out=QF[:, asl], in0=PD[:, asl],
                           scalar1=12584448.0, scalar2=-12582912.0, op0=ALU.add, op1=ALU.add),
                           deps=[t5_] + bQF.wdeps() + fgate_c)
                bQF.setw(t6_); bPD.addr(t6_)
                t7_ = P.op("v", lambda e, n, asl=asl: e.tensor_scalar(out=XQ[:, asl], in0=QF[:, asl],
                           scalar1=-1536.0, scalar2=None, op0=ALU.add),
                           deps=[t6_] + bXQ.wdeps() + (fgate_x if fp == 0 and a == 0 else []))
                bXQ.setw(t7_); bQF.addr(t7_)
            hf_rtoks_old = hf_rtoks
            hf_wtoks, hf_rtoks_new = [], []
            for ht in range(HT):
                buf = ht % 2
                tw1 = P.dma("y", lambda e, n, ht=ht, buf=buf: e.dma_start(
                    out=w1s(buf).rearrange("p (a c) -> p a c", a=16),
                    in_=w1T.rearrange("(a p) h -> p a h", p=128)[:, :, ht * 128:(ht + 1) * 128]),
                    deps=bA.wdeps())
                bA.setw(tw1)
                tw3 = P.dma("y", lambda e, n, ht=ht, buf=buf: e.dma_start(
                    out=w3s(buf).rearrange("p (a c) -> p a c", a=16),
                    in_=w3T.rearrange("(a p) h -> p a h", p=128)[:, :, ht * 128:(ht + 1) * 128]),
                    deps=bA.wdeps())
                bA.setw(tw3)

                def wq(src, m, qsl):
                    tq0 = P.op("v", lambda e, n, src=src, m=m: e.tensor_scalar(
                        out=src, in0=src, scalar1=swinv_c(m), scalar2=None, op0=ALU.mult),
                        deps=bA.rdeps() + bSWB.rdeps() + bA.wdeps())
                    bA.setw(tq0)
                    tq1 = P.op("v", lambda e, n, src=src: e.tensor_scalar(
                        out=src, in0=src, scalar1=-1.45, scalar2=1.45, op0=ALU.max, op1=ALU.min),
                        deps=bA.rdeps() + bA.wdeps())
                    bA.setw(tq1)
                    tq2 = P.op("v", lambda e, n, src=src: e.tensor_scalar(
                        out=MGF[:], in0=src, scalar1=12584448.0, scalar2=-12582912.0,
                        op0=ALU.add, op1=ALU.add), deps=bA.rdeps() + bMGF.wdeps())
                    bMGF.setw(tq2); bA.addr(tq2)
                    tq3 = P.op("v", lambda e, n, qsl=qsl: e.tensor_scalar(
                        out=WQT[:, qsl], in0=MGF[:], scalar1=-1536.0, scalar2=None, op0=ALU.add),
                        deps=bMGF.rdeps() + bXQ.wdeps())
                    bXQ.setw(tq3); bMGF.addr(tq3)
                    return tq3
                tq1 = wq(w1s(buf), 0, slice(0, 2048))
                tq3 = wq(w3s(buf), 1, slice(2048, 4096))
                tokm = None
                for a in range(16):
                    tokm = P.op("t", lambda e, n, a=a: e.matmul(
                        out=ps_a, lhsT=WQT[:, a * 128:(a + 1) * 128], rhs=XQ[:, a * FW:(a + 1) * FW],
                        start=(a == 0), stop=(a == 15)),
                        deps=[tq1] + bXQ.rdeps() + (bps1.wdeps() if a == 0 else []))
                    bXQ.addr(tokm)
                bps1.setw(tokm)
                for a in range(16):
                    tokm = P.op("t", lambda e, n, a=a: e.matmul(
                        out=ps_b, lhsT=WQT[:, 2048 + a * 128:2048 + (a + 1) * 128],
                        rhs=XQ[:, a * FW:(a + 1) * FW],
                        start=(a == 0), stop=(a == 15)),
                        deps=[tq3] + bXQ.rdeps() + (bps1.wdeps() if a == 0 else []))
                    bXQ.addr(tokm)
                bps1.setw(tokm)
                tt1 = P.op("v", lambda e, n: e.tensor_tensor(out=cs(7), in0=ps_a, in1=cs(4), op=ALU.mult),
                           deps=bps1.rdeps() + [t4_] + bCS.wdeps())
                bCS.setw(tt1); bps1.addr(tt1)
                ts1 = P.op("s", lambda e, n: e.activation(out=cs(7), in_=cs(7), func=AF.Silu, scale=swm_c(0)),
                           deps=[tt1] + bSWB.rdeps() + bCS.wdeps())
                bCS.setw(ts1)
                tt3 = P.op("v", lambda e, n: e.tensor_tensor(out=cs(8), in0=ps_b, in1=cs(4), op=ALU.mult),
                           deps=bps1.rdeps() + [t4_] + bCS.wdeps())
                bCS.setw(tt3); bps1.addr(tt3)
                thf = P.op("v", lambda e, n: e.scalar_tensor_tensor(out=cs(9), in0=cs(7), scalar=swm_c(1),
                           in1=cs(8), op0=ALU.mult, op1=ALU.mult),
                           deps=[ts1, tt3] + bSWB.rdeps() + bCS.wdeps())
                bCS.setw(thf)
                if ht == 0:
                    thm = P.op("v", lambda e, n: e.tensor_copy(out=cs(5), in_=cs(9)), deps=[thf] + bCS.wdeps())
                    bCS.setw(thm)
                    thm = P.op("v", lambda e, n: e.tensor_copy(out=cs(6), in_=cs(9)), deps=[thf] + bCS.wdeps())
                    bCS.setw(thm)
                else:
                    thm = P.op("v", lambda e, n: e.tensor_tensor(out=cs(5), in0=cs(9), in1=cs(5), op=ALU.max),
                               deps=[thf] + bCS.rdeps() + bCS.wdeps())
                    bCS.setw(thm)
                    thm = P.op("v", lambda e, n: e.tensor_tensor(out=cs(6), in0=cs(9), in1=cs(6), op=ALU.min),
                               deps=[thf] + bCS.rdeps() + bCS.wdeps())
                    bCS.setw(thm)
                tsp = P.dma("g", lambda e, n, ht=ht: e.dma_start(out=hfh[ht * 128:(ht + 1) * 128, :], in_=cs(9)),
                            deps=[thf] + (hf_rtoks_old if ht == 0 else []))
                hf_wtoks.append(tsp); bCS.addr(tsp)
            tok = P.op("v", lambda e, n: e.tensor_scalar(out=cs(6), in0=cs(6), scalar1=-1.0, scalar2=None,
                       op0=ALU.mult), deps=bCS.rdeps() + bCS.wdeps())
            bCS.setw(tok)
            tok = P.op("v", lambda e, n: e.tensor_tensor(out=cs(5), in0=cs(6), in1=cs(5), op=ALU.max),
                       deps=bCS.rdeps() + bCS.wdeps())
            bCS.setw(tok)
            tok = P.op("g", lambda e, n: e.partition_all_reduce(out_ap=cs(10), in_ap=cs(5),
                       channels=128, reduce_op=ReduceOp.max), deps=bCS.rdeps() + bCS.wdeps())
            bCS.setw(tok)
            t1h = P.op("v", lambda e, n: e.tensor_scalar(out=cs(10), in0=cs(10), scalar1=1e-5, scalar2=None,
                       op0=ALU.max), deps=bCS.rdeps() + bCS.wdeps())
            bCS.setw(t1h)
            t2h = P.op("v", lambda e, n: e.reciprocal(out=cs(11), in_=cs(10)), deps=[t1h] + bCS.wdeps())
            bCS.setw(t2h)
            t3h = P.op("v", lambda e, n: e.tensor_scalar(out=cs(11), in0=cs(11), scalar1=127.0, scalar2=None,
                       op0=ALU.mult), deps=[t2h] + bCS.wdeps())
            bCS.setw(t3h)
            t4h = P.op("v", lambda e, n: e.tensor_scalar(out=cs(12), in0=cs(10), scalar1=1.0 / 127.0,
                       scalar2=None, op0=ALU.mult), deps=[t1h] + bCS.wdeps())
            bCS.setw(t4h)
            for ht in range(HT):
                trd = P.dma("y", lambda e, n, ht=ht: e.dma_start(out=cs(9), in_=hfh[ht * 128:(ht + 1) * 128, :]),
                            deps=(hf_wtoks if ht == 0 else []) + bCS.wdeps())
                bCS.setw(trd); hf_rtoks_new.append(trd)
                tmq = P.op("v", lambda e, n: e.tensor_tensor(out=cs(9), in0=cs(9), in1=cs(11), op=ALU.mult),
                           deps=[trd, t3h] + bCS.rdeps() + bCS.wdeps())
                bCS.setw(tmq)
                tf_ = P.op("v", lambda e, n: e.tensor_scalar(out=QF[:, 0:FW], in0=cs(9),
                           scalar1=12584448.0, scalar2=-12582912.0, op0=ALU.add, op1=ALU.add),
                           deps=[tmq] + bQF.wdeps())
                bQF.setw(tf_); bCS.addr(tf_)
                tb_ = P.op("v", lambda e, n, ht=ht: e.tensor_scalar(out=QH[:, ht * FW:(ht + 1) * FW],
                           in0=QF[:, 0:FW], scalar1=-1536.0, scalar2=None, op0=ALU.add),
                           deps=[tf_] + bA.wdeps())
                bA.setw(tb_); bQF.addr(tb_)
            hf_rtoks = hf_rtoks_new
            for dt_ in range(16):
                tw2 = P.dma("y", lambda e, n, dt_=dt_: e.dma_start(
                    out=W2B.rearrange("p (a c) -> p a c", a=44),
                    in_=w2T.rearrange("(a p) d -> p a d", p=128)[:, :, dt_ * 128:(dt_ + 1) * 128]),
                    deps=bW2B.wdeps() + (bXT2.rdeps() if fp == 0 and dt_ == 0 else []))
                bW2B.setw(tw2)
                tq0 = P.op("v", lambda e, n: e.tensor_scalar(out=W2B[:], in0=W2B[:], scalar1=swinv_c(2),
                           scalar2=None, op0=ALU.mult), deps=bW2B.rdeps() + bSWB.rdeps() + bW2B.wdeps())
                bW2B.setw(tq0)
                tq1_ = P.op("v", lambda e, n: e.tensor_scalar(out=W2B[:], in0=W2B[:], scalar1=-1.45,
                            scalar2=1.45, op0=ALU.max, op1=ALU.min), deps=bW2B.rdeps() + bW2B.wdeps())
                bW2B.setw(tq1_)
                prev = None
                for seg0, seg1, msl in ((0, 2048, 2048), (2048, 4096, 2048), (4096, 5632, 1536)):
                    tq2_ = P.op("v", lambda e, n, seg0=seg0, seg1=seg1, msl=msl: e.tensor_scalar(
                        out=MGF[:, 0:msl], in0=W2B[:, seg0:seg1], scalar1=12584448.0, scalar2=-12582912.0,
                        op0=ALU.add, op1=ALU.add), deps=bW2B.rdeps() + bMGF.wdeps())
                    bMGF.setw(tq2_); bW2B.addr(tq2_)
                    tq3_ = P.op("v", lambda e, n, seg0=seg0, seg1=seg1, msl=msl: e.tensor_scalar(
                        out=W2Q[:, seg0:seg1], in0=MGF[:, 0:msl], scalar1=-1536.0, scalar2=None,
                        op0=ALU.add), deps=bMGF.rdeps() + bW2Q.wdeps())
                    bW2Q.setw(tq3_); bMGF.addr(tq3_)
                    prev = tq3_
                tokm = None
                for ht in range(HT):
                    tokm = P.op("t", lambda e, n, ht=ht: e.matmul(
                        out=ps_c, lhsT=W2Q[:, ht * 128:(ht + 1) * 128], rhs=QH[:, ht * FW:(ht + 1) * FW],
                        start=(ht == 0), stop=(ht == HT - 1)),
                        deps=[prev] + bA.rdeps() + bW2Q.rdeps() + (bps2.wdeps() if ht == 0 else []))
                    bA.addr(tokm); bW2Q.addr(tokm)
                bps2.setw(tokm)
                te1 = P.op("s", lambda e, n, dt_=dt_: e.activation(out=PD.bitcast(F32R)[:, dt_ * FW:(dt_ + 1) * FW],
                           in_=ps_c, func=AF.Copy, scale=swm_c(2)),
                           deps=bps2.rdeps() + bSWB.rdeps() + bPD.wdeps())
                bPD.setw(te1); bps2.addr(te1)
                te2 = P.op("v", lambda e, n, dt_=dt_: e.tensor_tensor(out=PD.bitcast(F32R)[:, dt_ * FW:(dt_ + 1) * FW],
                           in0=PD[:, dt_ * FW:(dt_ + 1) * FW], in1=cs(12), op=ALU.mult),
                           deps=[te1, t4h] + bPD.wdeps())
                bPD.setw(te2); bCS.addr(te2)
            trs3, tmr3 = fstats(PD, bPD, 5)
            b_rstd3 = fbcast(frow(6), 4, [trs3])
            b_mr3 = fbcast(frow(7), 5, [tmr3])
            for a in range(16):
                asl = slice(a * FW, (a + 1) * FW)
                t1o = P.op("v", lambda e, n, asl=asl: e.tensor_tensor(out=cs(19), in0=PD[:, asl],
                           in1=bc(4), op=ALU.mult), deps=bPD.rdeps() + [b_rstd3] + bCS.wdeps())
                bCS.setw(t1o); bPD.addr(t1o); bBCP.addr(t1o)
                t2o = P.op("v", lambda e, n: e.tensor_tensor(out=cs(19), in0=cs(19), in1=bc(5),
                           op=ALU.subtract), deps=[t1o] + bCS.wdeps() + [b_mr3])
                bCS.setw(t2o); bBCP.addr(t2o)
                t3o = P.op("v", lambda e, n, asl=asl: e.tensor_tensor(out=cs(19), in0=cs(19),
                           in1=PC[:, asl], op=ALU.add), deps=[t2o] + bPC.rdeps() + bCS.wdeps())
                bCS.setw(t3o); bPC.addr(t3o)
                tko = P.dma("g", lambda e, n, a=a, fp=fp: e.dma_start(
                    out=out_o[a * 128:(a + 1) * 128, fp * FW:(fp + 1) * FW],
                    in_=cs(19)), deps=[t3o])
                bCS.addr(tko); store_toks.append(tko)

        P.barrier("g", store_toks)
        block = E(nc.Block())
        P.build_blocks(block)
    return finalize(nc)


def kernel(L0_attn_x, L0_attn_W_xproj, L0_attn_W_dt, L0_attn_b_dt, L0_attn_A_log,
           L0_attn_D_param, L0_ffn_w1, L0_ffn_w2, L0_ffn_w3):
    from concourse.bass_utils import run_bass_kernel_spmd
    import ml_dtypes
    x = np.ascontiguousarray(np.asarray(L0_attn_x, np.float32)[0])
    W_xproj = np.asarray(L0_attn_W_xproj, np.float32)
    W_dt = np.asarray(L0_attn_W_dt, np.float32)
    b_dt = np.asarray(L0_attn_b_dt, np.float32)
    A_log = np.asarray(L0_attn_A_log, np.float32)
    D_param = np.asarray(L0_attn_D_param, np.float32)
    w1 = np.asarray(L0_ffn_w1, np.float32)
    w2 = np.asarray(L0_ffn_w2, np.float32)
    w3 = np.asarray(L0_ffn_w3, np.float32)

    xT = np.ascontiguousarray(x.T)
    wxpT = np.ascontiguousarray(W_xproj.T)
    WdtT = np.ascontiguousarray(W_dt.T)
    w1T = np.ascontiguousarray(w1.T)
    w3T = np.ascontiguousarray(w3.T)
    w2T = np.ascontiguousarray(w2.T)
    wcols = w1.size // 8 // 128
    ones1 = np.ones((1, 128), np.float32)
    onesc = np.ones((128, 1), np.float32)
    ident = np.eye(128, dtype=np.float32)
    sel32 = np.zeros((32, 32 * 128), ml_dtypes.bfloat16)
    for j in range(32):
        sel32[j, j * 128:(j + 1) * 128] = 1.0
    consts = np.zeros((128, 8), np.float32)
    consts[:, 0] = 1e-5; consts[:, 1] = 1.0; consts[:, 2] = 1536.0; consts[:, 3] = -1536.0

    nc = build_fused(wcols)
    in_maps = []
    for i in range(8):
        tsl = slice(LC * i, LC * (i + 1))
        dsl = slice(256 * i, 256 * (i + 1))
        al = A_log[dsl].reshape(2, 128, 16)
        al = np.ascontiguousarray(al.transpose(1, 2, 0).reshape(128, 32))
        xs = np.ascontiguousarray(xT[:, tsl])
        in_maps.append({
            "xTr": xs, "xTf": xs, "wxpT": wxpT,
            "wdtT": np.ascontiguousarray(WdtT[:, dsl]),
            "bdt": np.ascontiguousarray(b_dt[dsl].reshape(2, 128).T),
            "alog": al,
            "dpar": np.ascontiguousarray(D_param[dsl].reshape(2, 128).T),
            "ones1": ones1, "onescr": onesc, "onescf": onesc,
            "ident": ident, "sel32": sel32, "consts": consts,
            "w1a": np.ascontiguousarray(w1.reshape(8, 128, -1)[i]),
            "w3a": np.ascontiguousarray(w3.reshape(8, 128, -1)[i]),
            "w2a": np.ascontiguousarray(w2.reshape(8, 128, -1)[i]),
            "w1T": w1T, "w3T": w3T, "w2T": w2T,
        })
    r = run_bass_kernel_spmd(nc, in_maps, core_ids=list(range(8)))
    outT = np.concatenate([r_["outT"] for r_ in r.results], axis=1)
    return np.ascontiguousarray(outT.T)[None].astype(np.float32)


# version 08
# speedup vs baseline: 4.0335x; 4.0335x over previous
import numpy as np
import concourse.bass as bass
import concourse.mybir as mybir

F32 = mybir.dt.float32
F32R = mybir.dt.float32r
BF16 = mybir.dt.bfloat16
F16 = mybir.dt.float16
AF = mybir.ActivationFunctionType
ALU = mybir.AluOpType
AX = mybir.AxisListType

ENGINES = ("v", "s", "t", "g", "y")
NQ = 12

class Prog:

    def __init__(self, nc):
        self.nc = nc
        self.ops = {e: [] for e in ENGINES}
        self.count = {}
        self.sem_names = ["SM_" + e for e in ENGINES]
        self.qpool = {}
        for q in ("y", "s", "g"):
            self.qpool[q] = ["Q%s%d" % (q, i) for i in range(NQ)]
            self.sem_names += self.qpool[q]
        for s_ in self.sem_names:
            self.count[s_] = 0
        self.qidx = {q: 0 for q in self.qpool}
        self.eng_sem = {e: "SM_" + e for e in ENGINES}
        self.wm = {e: {} for e in ENGINES}
        self.sem_ctx = None

    def _wait(self, e, deps):
        own = self.eng_sem[e]
        for d in deps:
            if d is None:
                continue
            sem, val = d
            if val <= 0:
                continue
            if e == "t" and sem == own:
                continue
            cur = self.wm[e].get(sem, 0)
            if val > cur:
                self.wm[e][sem] = val
                self.ops[e].append(("w", sem, val))

    def op(self, e, emit, deps=()):
        self._wait(e, deps)
        sem = self.eng_sem[e]
        self.count[sem] += 1
        self.ops[e].append(("i", emit, sem))
        return (sem, self.count[sem])

    def q_alloc(self, e):
        pool = self.qpool[e]
        sem = pool[self.qidx[e] % NQ]
        self.qidx[e] += 1
        return sem, self.count[sem]

    def dma(self, e, emit, deps=()):
        sem, prior = self.q_alloc(e)
        alldeps = list(deps)
        if prior > 0:
            alldeps.append((sem, prior))
        self._wait(e, alldeps)
        self.count[sem] += 16
        self.ops[e].append(("i", emit, sem))
        return (sem, self.count[sem])

    def raw(self, e, emit, deps=()):
        self._wait(e, deps)
        self.ops[e].append(("r", emit))
        return None

    def barrier(self, e, deps):
        self._wait(e, deps)

    def build_blocks(self, block):
        nc = self.nc
        handles = self.sem_ctx
        engmap = {"v": nc.vector, "s": nc.scalar, "t": nc.tensor, "g": nc.gpsimd, "y": nc.sync}
        def mk(e):
            eng = engmap[e]
            def body(_):
                for item in self.ops[e]:
                    if item[0] == "w":
                        eng.wait_ge(handles[item[1]], item[2])
                    elif item[0] == "i":
                        inst = item[1](eng, nc)
                        inc = 16 if item[2] not in self.eng_sem.values() else 1
                        inst.then_inc(handles[item[2]], inc)
                    else:
                        item[1](eng, nc)
            return body
        for e in ENGINES:
            if self.ops[e]:
                getattr(block, {"v": "vector", "s": "scalar", "t": "tensor", "g": "gpsimd", "y": "sync"}[e])(mk(e))


class TBuf:
    __slots__ = ("wtok", "rtoks")
    def __init__(self):
        self.wtok = None
        self.rtoks = []
    def rdeps(self):
        return [self.wtok] if self.wtok else []
    def wdeps(self):
        d = list(self.rtoks)
        if self.wtok:
            d.append(self.wtok)
        return d
    def setw(self, tok):
        self.wtok = tok
        self.rtoks = []
    def addr(self, tok):
        if tok:
            self.rtoks.append(tok)


def finalize(nc):
    import bass_rust
    import concourse.mybir as _mb
    from concourse.library_config import all_libraries, standard
    mask = {}
    for lib in all_libraries:
        for it in lib.instructions:
            mask[it] = mask.get(it, 0) | (1 << lib.index)
    bass_rust.insert_library_loads(nc, mask, len(all_libraries), standard.index)
    _mb.codegen_inst_isa_subclasses(nc)
    return nc

"""V2: token-sharded end-to-end, zero stagger-exposed cross-core waits.

Per core (tokens [LC*i, LC*(i+1))): P (LN1+x_proj), M (scan w/ carry chain
core i -> i+1), F (BitLinear FFN, local abs-pass for weight scales).
"""
from contextlib import ExitStack
from concourse.bass_isa import ReduceOp

D = 2048
NA = 16
NST = 16
R_LO = 64
E_XD = 96
H = 5632
HT = 44
MAGIC1 = 12584448.0
MAGIC2 = -12582912.0


def build2(LC=1024, debug=False):
    FH = min(512, LC)
    NHF = LC // FH
    nc = bass.Bass("TRN2", target_bir_lowering=False, debug=False)
    E = nc.ctx.enter_context

    xTf = nc.dram_tensor("xTf", [D, LC], F32, kind="ExternalInput").ap()
    wxp = nc.dram_tensor("wxp", [D, E_XD], F32, kind="ExternalInput").ap()
    wdt = nc.dram_tensor("wdt", [R_LO, D], F32, kind="ExternalInput").ap()
    bdt = nc.dram_tensor("bdt", [128, 16], F32, kind="ExternalInput").ap()
    alog = nc.dram_tensor("alog", [128, 256], F32, kind="ExternalInput").ap()
    dpar = nc.dram_tensor("dpar", [128, 16], F32, kind="ExternalInput").ap()
    w1T = nc.dram_tensor("w1T", [D, H], F32, kind="ExternalInput").ap()
    w3T = nc.dram_tensor("w3T", [D, H], F32, kind="ExternalInput").ap()
    w2T = nc.dram_tensor("w2T", [H, D], F32, kind="ExternalInput").ap()
    ident = nc.dram_tensor("ident", [128, 128], F32R, kind="ExternalInput").ap()
    sel32 = nc.dram_tensor("sel32", [32, 32 * 128], BF16, kind="ExternalInput").ap()
    ones1 = nc.dram_tensor("ones1", [1, 128], F32, kind="ExternalInput").ap()
    onescf = nc.dram_tensor("onescf", [128, 1], F32, kind="ExternalInput").ap()
    consts = nc.dram_tensor("consts", [128, 8], F32, kind="ExternalInput").ap()
    outT = nc.dram_tensor("outT", [D, LC], F32, kind="ExternalOutput").ap()
    hfh = nc.dram_tensor("hfh", [128, HT * LC], F32, kind="Internal").ap()
    x1h = nc.dram_tensor("x1h", [128, NA * LC], F32, kind="Internal").ap()
    if debug:
        dbgY = nc.dram_tensor("dbgY", [128, NA * LC], F32, kind="ExternalOutput").ap()
        dbgSW = nc.dram_tensor("dbgSW", [128, 16], F32, kind="ExternalOutput").ap()
        dbgHF = nc.dram_tensor("dbgHF", [128, HT * LC], F32, kind="ExternalOutput").ap()
        dbgX1 = nc.dram_tensor("dbgX1", [128, NA * LC], F32, kind="ExternalOutput").ap()
        dbgHIN = nc.dram_tensor("dbgHIN", [128, 256], F32, kind="ExternalOutput").ap()
        dbgSTT = nc.dram_tensor("dbgSTT", [128, 256], F32, kind="ExternalOutput").ap()
        dbgM = nc.dram_tensor("dbgM", [128, 6 * LC], F32, kind="ExternalOutput").ap()

    with ExitStack() as ctx:
        nc.ctx = ctx
        A = E(nc.sbuf_tensor("A", [128, NA * LC], F32))
        B = E(nc.sbuf_tensor("B", [128, NA * LC], F32))
        C = E(nc.sbuf_tensor("C", [128, 10 * LC], F32))
        ZZT = E(nc.sbuf_tensor("ZZT", [128, LC], F32))
        MGW = E(nc.sbuf_tensor("MGW", [128, 1024], F32))
        STTB = E(nc.sbuf_tensor("STTB", [128, 256], F32))
        HINB = E(nc.sbuf_tensor("HINB", [128, 256], F32))
        DUMB = E(nc.sbuf_tensor("DUMB", [128, 256], F32))
        ZB = E(nc.sbuf_tensor("ZB", [128, 128], F32))
        SWB = E(nc.sbuf_tensor("SWB", [128, 16], F32))
        ABSACC = E(nc.sbuf_tensor("ABSACC", [128, 296], F32))
        AT = E(nc.sbuf_tensor("ATT", [128, 256], F32))
        WXPS = E(nc.sbuf_tensor("WXPS", [128, 2 * E_XD], F32))
        WDTS = E(nc.sbuf_tensor("WDTS", [64, 256], F32))
        BDT = E(nc.sbuf_tensor("BDT", [128, 16], F32))
        DP = E(nc.sbuf_tensor("DP", [128, 16], F32))
        CON = E(nc.sbuf_tensor("CON", [128, 8], F32))
        ONES = E(nc.sbuf_tensor("ONES", [1, 128], F32))
        ONESCF = E(nc.sbuf_tensor("ONESCF", [128, 1], F32))
        IDENT = E(nc.sbuf_tensor("IDENT", [128, 128], F32R))
        SELX = E(nc.sbuf_tensor("SELX", [32, 2048], F32))
        ROWS = E(nc.sbuf_tensor("ROWS", [1, 3 * LC], F32))
        ps1 = E(nc.psum_tensor("ps1", [128, LC], F32))
        ps2 = E(nc.psum_tensor("ps2", [128, LC], F32))
        ps3 = E(nc.psum_tensor("ps3", [128, LC], F32))
        ps4 = E(nc.psum_tensor("ps4", [128, LC], F32))
        PS = [ps1, ps2, ps3, ps4]

        SEL = SELX[:, 0:2048].bitcast(BF16)
        Ab = A.bitcast(BF16)
        Bb = B.bitcast(BF16)

        P = Prog(nc)
        extra = ["RSC%d" % k for k in range(16)] + ["CL%d" % k for k in range(17)] + ["CLZ%d" % k for k in range(4)] + ["RSD", "PREP"]
        P.sem_names += extra
        for s_ in extra:
            P.count[s_] = 0
        P.sem_ctx = {name: E(nc.semaphore(name)) for name in P.sem_names}
        SEM = P.sem_ctx

        Bu = TBuf

        def asl_(a):
            return slice(a * LC, (a + 1) * LC)

        def cs(i):
            return C[:, i * LC:(i + 1) * LC]

        def rs(i):
            return ROWS[0:1, i * LC:(i + 1) * LC]

        bAs = [Bu() for _ in range(NA)]
        bBs = [Bu() for _ in range(NA)]
        bCs = [Bu() for _ in range(10)]
        bPS = [Bu() for _ in range(4)]
        bZZ = [Bu()]; bZZ = bZZ + bZZ
        bROWS = Bu()
        bSWB, bABS, bAT, bWXPS, bWDTS, bBDT, bDP, bCON = (Bu() for _ in range(8))
        bONES, bONESCF, bIDENT, bSEL, bSTT, bZB = (Bu() for _ in range(6))
        bMGW, bXQ, bHQ, bHFH, bX1H = (Bu() for _ in range(5))

        bCON.setw(P.dma("y", lambda e, n: e.dma_start(out=CON[:], in_=consts[:])))
        bONES.setw(P.dma("y", lambda e, n: e.dma_start(out=ONES[:], in_=ones1[:])))
        bONESCF.setw(P.dma("y", lambda e, n: e.dma_start(out=ONESCF[:], in_=onescf[:])))
        bIDENT.setw(P.dma("y", lambda e, n: e.dma_start(out=IDENT[:], in_=ident[:])))
        bSEL.setw(P.dma("y", lambda e, n: e.dma_start(out=SEL[:], in_=sel32[:])))
        bBDT.setw(P.dma("y", lambda e, n: e.dma_start(out=BDT[:], in_=bdt[:])))
        bDP.setw(P.dma("y", lambda e, n: e.dma_start(out=DP[:], in_=dpar[:])))
        tal = P.dma("y", lambda e, n: e.dma_start(out=AT[:], in_=alog[:]))
        t1 = P.op("s", lambda e, n: e.activation(out=AT[:], in_=AT[:], func=AF.Exp), deps=[tal])
        t2 = P.op("v", lambda e, n: e.tensor_scalar(out=AT[:], in0=AT[:], scalar1=-1.0, scalar2=None,
                  op0=ALU.mult), deps=[t1])
        bAT.setw(t2)
        tzb = P.op("v", lambda e, n: e.memset(ZB[:], 0.0))
        bZB.setw(tzb)
        tswz = P.op("v", lambda e, n: e.memset(SWB[:, 6:7], 1536.0))
        bSWB.setw(tswz)

        def emit_carry_switch(k4):
            lsems = ["CL%d" % (k4 * 4 + kk) for kk in range(4)]
            def carry_em(e, n, ls=tuple(lsems), k4=k4):
                pid = e.partition_id()
                for r in n.Switch(engines=[e], index=[pid], n=8):
                    dst = (r + 1) % 8
                    dlt = r ^ dst
                    if dlt & 4:
                        dlt ^= 2
                    for kk in range(4):
                        k = k4 * 4 + kk
                        dests = [None] * 8
                        dests[dlt] = (0, dlt)
                        if r < 7:
                            e.remote_dma_broadcast(
                                out_ap=HINB[:, 16 * k:16 * (k + 1)],
                                in_ap=STTB[:, 16 * k:16 * (k + 1)],
                                remote_sem=SEM["RSC%d" % k], local_sem=SEM[ls[kk]],
                                rdests=dests).then_inc(SEM["PREP"], 1)
                        else:
                            e.remote_dma_broadcast(
                                out_ap=DUMB[:, 16 * k:16 * (k + 1)],
                                in_ap=STTB[:, 16 * k:16 * (k + 1)],
                                remote_sem=SEM["RSD"], local_sem=SEM[ls[kk]],
                                rdests=dests).then_inc(SEM["PREP"], 1)
            P.ops["g"].append(("r", carry_em))
            P.count["PREP"] += 4
            for sem in lsems:
                P.count[sem] += 16
            P._wait("g", [("PREP", P.count["PREP"])])

        abs_jobs = []
        for m, wsrc in enumerate((w1T, w3T, w2T)):
            if m < 2:
                flat = wsrc.rearrange("(a p) h -> p a h", p=128)
                nblk, ncol = NA, H
            else:
                flat = wsrc.rearrange("(b p) d -> p b d", p=128)
                nblk, ncol = HT, D
            ji = 0
            for blk in range(nblk):
                c0 = 0
                while c0 < ncol:
                    w = min(1024, ncol - c0)
                    abs_jobs.append((m, flat, blk, c0, w, m * 96 + ji))
                    ji += 1
                    c0 += w
        abs_state = {"i": 0, "buf": 0}
        bWST = [Bu(), Bu()]

        def emit_abs(k, tile_cols, base):
            for _ in range(k):
                i = abs_state["i"]
                if i >= len(abs_jobs):
                    return
                m, flat, blk, c0, w, col = abs_jobs[i]
                abs_state["i"] += 1
                nsub = max(1, w // tile_cols)
                for s_i in range(nsub):
                    sw = min(tile_cols, w)
                    buf = abs_state["buf"]
                    abs_state["buf"] ^= 1
                    wst = C[:, base + buf * tile_cols: base + buf * tile_cols + sw]
                    q = "s" if (i % 2 == 0) else "y"
                    tok = P.dma(q, lambda e, n, flat=flat, blk=blk, c0=c0 + s_i * tile_cols, w=sw, wst=wst:
                                e.dma_start(out=wst, in_=flat[:, blk, c0:c0 + w]),
                                deps=bWST[buf].wdeps())
                    bWST[buf].setw(tok)
                    acc_ap = ABSACC[:, col:col + 1] if s_i == 0 else ABSACC[:, 288:289]
                    tok = P.op("s", lambda e, n, wst=wst, acc_ap=acc_ap:
                               e.activation(out=wst, in_=wst, func=AF.Abs, accum_out=acc_ap),
                               deps=bWST[buf].rdeps() + bWST[buf].wdeps() + bABS.wdeps())
                    bWST[buf].setw(tok)
                    bABS.setw(tok)
                    if s_i > 0:
                        tok = P.op("v", lambda e, n, col=col: e.tensor_tensor(
                            out=ABSACC[:, col:col + 1], in0=ABSACC[:, col:col + 1],
                            in1=ABSACC[:, 288:289], op=ALU.add),
                            deps=bABS.rdeps() + bABS.wdeps())
                        bABS.setw(tok)

        for a in range(NA):
            tok = P.dma("y", lambda e, n, a=a: e.dma_start(
                out=B[:, asl_(a)], in_=xTf[a * 128:(a + 1) * 128, :]),
                deps=bBs[a].wdeps())
            bBs[a].setw(tok)
            sq = cs(2 + (a % 2))
            for h in range(NHF):
                hsl = slice(h * FH, (h + 1) * FH)
                tm = P.op("t", lambda e, n, a=a, hsl=hsl: e.matmul(
                    out=ps4[0:1, hsl], lhsT=ONESCF[:], rhs=B[:, asl_(a)][:, hsl],
                    start=(a == 0), stop=(a == NA - 1)),
                    deps=bBs[a].rdeps() + bONESCF.rdeps() + (bPS[3].wdeps() if a == 0 else []))
                bBs[a].addr(tm)
                if a == NA - 1:
                    bPS[3].setw(tm)
            tsq = P.op("s", lambda e, n, a=a, sq=sq: e.activation(out=sq, in_=B[:, asl_(a)],
                       func=AF.Square), deps=bBs[a].rdeps() + bCs[2 + (a % 2)].wdeps())
            bCs[2 + (a % 2)].setw(tsq)
            bBs[a].addr(tsq)
            for h in range(NHF):
                hsl = slice(h * FH, (h + 1) * FH)
                tq = P.op("t", lambda e, n, a=a, hsl=hsl, sq=sq: e.matmul(
                    out=ps3[0:1, hsl], lhsT=ONESCF[:], rhs=sq[:, hsl],
                    start=(a == 0), stop=(a == NA - 1)),
                    deps=bCs[2 + (a % 2)].rdeps() + bONESCF.rdeps() + (bPS[2].wdeps() if a == 0 else []))
                bCs[2 + (a % 2)].addr(tq)
                if a == NA - 1:
                    bPS[2].setw(tq)
            emit_abs(4, LC, 8 * LC)

        def ln_rows(psA, psB2, bpsA, bpsB, scale):
            tm = P.op("v", lambda e, n: e.tensor_scalar(out=rs(0), in0=psA[0:1, :], scalar1=scale,
                      scalar2=None, op0=ALU.mult), deps=bpsA.rdeps() + bROWS.wdeps())
            bROWS.setw(tm); bpsA.addr(tm)
            tsq = P.op("v", lambda e, n: e.tensor_scalar(out=rs(2), in0=psB2[0:1, :], scalar1=scale,
                       scalar2=None, op0=ALU.mult), deps=bpsB.rdeps() + bROWS.wdeps())
            bROWS.setw(tsq); bpsB.addr(tsq)
            tm2 = P.op("s", lambda e, n: e.activation(out=rs(1), in_=rs(0), func=AF.Square),
                       deps=[tm] + bROWS.wdeps())
            bROWS.setw(tm2)
            tv = P.op("v", lambda e, n: e.tensor_tensor(out=rs(2), in0=rs(2), in1=rs(1), op=ALU.subtract),
                      deps=[tsq, tm2] + bROWS.wdeps())
            bROWS.setw(tv)
            tsr = P.op("s", lambda e, n: e.activation(out=rs(2), in_=rs(2), func=AF.Sqrt, bias=CON[0:1, 0:1]),
                       deps=[tv] + bCON.rdeps() + bROWS.wdeps())
            bROWS.setw(tsr)
            trs = P.op("v", lambda e, n: e.reciprocal(out=rs(1), in_=rs(2)), deps=[tsr] + bROWS.wdeps())
            bROWS.setw(trs)
            tmr = P.op("v", lambda e, n: e.tensor_tensor(out=rs(2), in0=rs(0), in1=rs(1), op=ALU.mult),
                       deps=[trs, tm] + bROWS.wdeps())
            bROWS.setw(tmr)
            return trs, tmr

        def bcast(row_ap, psOUT, bpsOUT, deps):
            toks = []
            for h in range(NHF):
                hsl = slice(h * FH, (h + 1) * FH)
                tok = P.op("t", lambda e, n, hsl=hsl, row_ap=row_ap, psOUT=psOUT: e.matmul(
                    out=psOUT[:, hsl], lhsT=ONES[:], rhs=row_ap[0:1, hsl], start=True, stop=True),
                    deps=deps + bONES.rdeps() + bpsOUT.wdeps())
                toks.append(tok)
            bpsOUT.setw(toks[-1])
            bROWS.addr(toks[-1])
            return toks[-1]

        trs, tmr = ln_rows(ps4, ps3, bPS[3], bPS[2], 1.0 / D)
        tb1 = bcast(rs(1), ps1, bPS[0], [trs])
        tb2 = bcast(rs(2), ps2, bPS[1], [tmr])
        twxp = P.dma("y", lambda e, n: e.dma_start(
            out=WXPS[:].rearrange("p (a c) -> p a c", a=2),
            in_=wxp.rearrange("(a p) c -> p a c", p=128)[:, 0:2, :]), deps=bWXPS.wdeps())
        bWXPS.setw(twxp)
        for a in range(NA):
            t1_ = P.op("v", lambda e, n, a=a: e.tensor_tensor(out=A[:, asl_(a)], in0=B[:, asl_(a)],
                       in1=ps1[:], op=ALU.mult),
                       deps=bBs[a].rdeps() + [tb1] + bAs[a].wdeps())
            bAs[a].setw(t1_); bBs[a].addr(t1_); bPS[0].addr(t1_)
            t2_ = P.op("v", lambda e, n, a=a: e.tensor_tensor(out=A[:, asl_(a)], in0=A[:, asl_(a)],
                       in1=ps2[:], op=ALU.subtract), deps=[t1_, tb2] + bAs[a].wdeps())
            bAs[a].setw(t2_); bPS[1].addr(t2_)
            for h in range(NHF):
                hsl = slice(h * FH, (h + 1) * FH)
                tok = P.op("t", lambda e, n, a=a, hsl=hsl: e.matmul(
                    out=ps4[0:E_XD, hsl], lhsT=WXPS[:, (a % 2) * E_XD:(a % 2 + 1) * E_XD],
                    rhs=A[:, asl_(a)][:, hsl],
                    start=(a == 0), stop=(a == NA - 1)),
                    deps=bAs[a].rdeps() + bWXPS.rdeps() + (bPS[3].wdeps() if a == 0 else []))
                bAs[a].addr(tok); bWXPS.addr(tok)
                if a == NA - 1:
                    bPS[3].setw(tok)
            if a % 2 == 1 and a < NA - 1:
                tok = P.dma("y", lambda e, n, a=a: e.dma_start(
                    out=WXPS[:].rearrange("p (a c) -> p a c", a=2),
                    in_=wxp.rearrange("(a p) c -> p a c", p=128)[:, a + 1:a + 3, :]),
                    deps=bWXPS.wdeps())
                bWXPS.setw(tok)
        tok = P.op("s", lambda e, n: e.activation(out=cs(0)[0:E_XD, :], in_=ps4[0:E_XD, :], func=AF.Copy),
                   deps=bPS[3].rdeps() + bCs[0].wdeps())
        bCs[0].setw(tok); bPS[3].addr(tok)
        tok = P.dma("g", lambda e, n: e.dma_start(out=cs(3)[0:32, :], in_=cs(0)[R_LO:E_XD, :]),
                    deps=bCs[0].rdeps() + bCs[3].wdeps())
        bCs[3].setw(tok); bCs[0].addr(tok)
        XBC = cs(3)[0:32, 0:LC]
        XBH = C[0:32, LC + 0:LC + LC // 2].bitcast(BF16)
        XBL = C[0:32, LC + LC // 2:2 * LC].bitcast(BF16)
        tok = P.op("v", lambda e, n: e.tensor_copy(out=XBH, in_=XBC),
                   deps=bCs[3].rdeps() + bCs[1].wdeps())
        bCs[1].setw(tok); bCs[3].addr(tok)
        tok = P.op("v", lambda e, n: e.tensor_tensor(out=XBL, in0=XBC, in1=XBH, op=ALU.subtract),
                   deps=bCs[3].rdeps() + bCs[1].rdeps() + bCs[1].wdeps())
        bCs[1].setw(tok); bCs[3].addr(tok)
        if debug:
            tokd = P.dma("g", lambda e, n: e.dma_start(out=dbgM[0:E_XD, 0:LC], in_=cs(0)[0:E_XD, :]),
                         deps=bCs[0].rdeps())
            bCs[0].addr(tokd)

        def emit_zero_switch(half):
            P._wait("g", bZB.rdeps())

            def zem(e, n, half=half):
                pid = e.partition_id()
                for r in n.Switch(engines=[e], index=[pid], n=8):
                    dests = [None] * 8
                    dests[0] = (0, 0)
                    for k in range(half * 8, half * 8 + 8):
                        e.remote_dma_broadcast(
                            out_ap=(HINB if r == 0 else DUMB)[:, 16 * k:16 * (k + 1)],
                            in_ap=ZB[:, 0:16],
                            remote_sem=SEM[("RSC%d" % k) if r == 0 else "RSD"],
                            local_sem=SEM["CLZ%d" % (2 * half + (k % 2))],
                            rdests=dests).then_inc(SEM["PREP"], 1)
            P.ops["g"].append(("r", zem))
            P.count["PREP"] += 8
            P._wait("g", [("PREP", P.count["PREP"])])
            P.ops["g"].append(("r", lambda e, n: e.trigger_dma(count=8)))

        emit_zero_switch(0)
        emit_zero_switch(1)
        for b_ in (bCs[8], bCs[9]):
            b_.rtoks += bWST[0].wdeps() + bWST[1].wdeps()
        for dt in range(NA):
            if dt % 4 == 0:
                emit_carry_switch(dt // 4)
            dbuf = dt % 2
            tok = P.dma("y", lambda e, n, dt=dt, dbuf=dbuf: e.dma_start(
                out=WDTS[:, dbuf * 128:(dbuf + 1) * 128], in_=wdt[:, dt * 128:(dt + 1) * 128]),
                deps=bWDTS.wdeps())
            bWDTS.setw(tok)
            for h in range(NHF):
                hsl = slice(h * FH, (h + 1) * FH)
                tok = P.op("t", lambda e, n, dbuf=dbuf, hsl=hsl: e.matmul(
                    out=ps4[:, hsl], lhsT=WDTS[:, dbuf * 128:(dbuf + 1) * 128],
                    rhs=cs(0)[0:R_LO, hsl], start=True, stop=True),
                    deps=bWDTS.rdeps() + bCs[0].rdeps() + bPS[3].wdeps())
                bPS[3].setw(tok); bWDTS.addr(tok); bCs[0].addr(tok)
            tok = P.op("s", lambda e, n, dt=dt: e.activation(out=cs(4), in_=ps4[:], func=AF.Abs,
                       bias=BDT[:, dt:dt + 1]), deps=bPS[3].rdeps() + bBDT.rdeps() + bCs[4].wdeps())
            bCs[4].setw(tok); bPS[3].addr(tok)
            tok = P.op("s", lambda e, n: e.activation(out=cs(5), in_=cs(4), func=AF.Exp, scale=-1.0),
                       deps=bCs[4].rdeps() + bCs[5].wdeps())
            bCs[5].setw(tok); bCs[4].addr(tok)
            tok = P.op("s", lambda e, n: e.activation(out=cs(4), in_=cs(5), func=AF.Ln, bias=CON[0:128, 1:2]),
                       deps=bCs[5].rdeps() + bCs[4].wdeps() + bCON.rdeps())
            bCs[4].setw(tok); bCs[5].addr(tok)
            tok = P.op("s", lambda e, n, dt=dt: e.activation(out=cs(5), in_=ps4[:], func=AF.Relu,
                       bias=BDT[:, dt:dt + 1]), deps=bPS[3].rdeps() + bBDT.rdeps() + bCs[5].wdeps())
            bCs[5].setw(tok); bPS[3].addr(tok)
            tok = P.op("v", lambda e, n: e.tensor_tensor(out=cs(2), in0=cs(4), in1=cs(5), op=ALU.add),
                       deps=bCs[4].rdeps() + bCs[5].rdeps() + bCs[2].wdeps())
            bCs[2].setw(tok); bCs[4].addr(tok); bCs[5].addr(tok)
            tok = P.op("v", lambda e, n, dt=dt: e.tensor_tensor(out=cs(3), in0=cs(2), in1=A[:, asl_(dt)],
                       op=ALU.mult), deps=bCs[2].rdeps() + bAs[dt].rdeps() + bCs[3].wdeps())
            bCs[3].setw(tok); bCs[2].addr(tok); bAs[dt].addr(tok)
            if debug and dt == 0:
                tokd = P.dma("g", lambda e, n: e.dma_start(out=dbgM[:, LC:2 * LC], in_=cs(2)),
                             deps=bCs[2].rdeps())
                bCs[2].addr(tokd)
                tokd = P.dma("g", lambda e, n: e.dma_start(out=dbgM[:, 2 * LC:3 * LC], in_=cs(3)),
                             deps=bCs[3].rdeps())
                bCs[3].addr(tokd)
            emit_abs(13, LC // 2, 9 * LC)
            for nn in range(NST):
                par = nn % 2
                psBC = PS[par]
                hh = 6 + (nn % 2)
                da = 4 + (nn % 2)
                for h in range(NHF):
                    hsl = slice(h * FH, (h + 1) * FH)
                    for wv, first in ((XBH, True), (XBL, False)):
                        tok = P.op("t", lambda e, n, nn=nn, hsl=hsl, wv=wv, first=first, psBC=psBC: e.matmul(
                            out=psBC[:, hsl], lhsT=SEL[:, nn * 128:(nn + 1) * 128], rhs=wv[:, hsl],
                            start=first, stop=not first),
                            deps=bCs[1].rdeps() + bSEL.rdeps() + (bPS[par].wdeps() if first and h == 0 else []))
                        if not first:
                            bPS[par].setw(tok)
                        bCs[1].addr(tok); bSEL.addr(tok)
                tok = P.op("s", lambda e, n, dt=dt, nn=nn, da=da: e.activation(
                    out=cs(da), in_=cs(2), func=AF.Exp, scale=AT[:, dt * 16 + nn:dt * 16 + nn + 1]),
                    deps=bCs[2].rdeps() + bAT.rdeps() + bCs[da].wdeps())
                bCs[da].setw(tok); bCs[2].addr(tok)
                tok = P.op("v", lambda e, n, psBC=psBC: e.tensor_tensor(out=cs(8), in0=cs(3), in1=psBC[:],
                           op=ALU.mult), deps=bCs[3].rdeps() + bPS[par].rdeps() + bCs[8].wdeps())
                bCs[8].setw(tok); bCs[3].addr(tok); bPS[par].addr(tok)
                for h in range(NHF):
                    hsl = slice(h * FH, (h + 1) * FH)
                    for wv, first in ((XBH, True), (XBL, False)):
                        tok = P.op("t", lambda e, n, nn=nn, hsl=hsl, wv=wv, first=first, psBC=psBC: e.matmul(
                            out=psBC[:, hsl], lhsT=SEL[:, (NST + nn) * 128:(NST + nn + 1) * 128],
                            rhs=wv[:, hsl], start=first, stop=not first),
                            deps=bCs[1].rdeps() + bSEL.rdeps() + (bPS[par].wdeps() if first and h == 0 else []))
                        if not first:
                            bPS[par].setw(tok)
                        bCs[1].addr(tok); bSEL.addr(tok)
                if debug and dt == 0 and nn == 0:
                    tokd = P.dma("g", lambda e, n, da=da: e.dma_start(out=dbgM[:, 3 * LC:4 * LC], in_=cs(da)),
                                 deps=bCs[da].rdeps())
                    bCs[da].addr(tokd)
                    tokd = P.dma("g", lambda e, n: e.dma_start(out=dbgM[:, 4 * LC:5 * LC], in_=cs(8)),
                                 deps=bCs[8].rdeps())
                    bCs[8].addr(tokd)
                scdeps = bCs[da].rdeps() + bCs[8].rdeps() + bCs[hh].wdeps()
                if nn == 0:
                    scdeps = scdeps + [("RSC%d" % dt, 2)]
                tok = P.op("v", lambda e, n, dt=dt, nn=nn, da=da, hh=hh: e.tensor_tensor_scan(
                    out=cs(hh), data0=cs(da), data1=cs(8),
                    initial=HINB[:, dt * 16 + nn:dt * 16 + nn + 1],
                    op0=ALU.mult, op1=ALU.add), deps=scdeps)
                bCs[hh].setw(tok); bCs[da].addr(tok); bCs[8].addr(tok)
                if debug and dt == 0 and nn == 0:
                    tokd = P.dma("g", lambda e, n, hh=hh: e.dma_start(out=dbgM[:, 5 * LC:6 * LC], in_=cs(hh)),
                                 deps=bCs[hh].rdeps())
                    bCs[hh].addr(tokd)
                tok = P.op("v", lambda e, n, dt=dt, nn=nn, hh=hh: e.tensor_copy(
                    out=STTB[:, dt * 16 + nn:dt * 16 + nn + 1], in_=cs(hh)[:, LC - 1:LC]),
                    deps=bCs[hh].rdeps() + bSTT.wdeps())
                bSTT.setw(tok); bCs[hh].addr(tok)
                zz = ZZT[:, 0:LC]
                tok = P.op("v", lambda e, n, hh=hh, psBC=psBC, zz=zz: e.tensor_tensor(
                    out=zz.bitcast(F32R), in0=cs(hh), in1=psBC[:], op=ALU.mult),
                    deps=bCs[hh].rdeps() + bPS[par].rdeps() + bZZ[par].wdeps())
                bZZ[par].setw(tok); bCs[hh].addr(tok); bPS[par].addr(tok)
                for h in range(NHF):
                    hsl = slice(h * FH, (h + 1) * FH)
                    tok = P.op("t", lambda e, n, hsl=hsl, nn=nn, zz=zz: e.matmul(
                        out=ps3[:, hsl], lhsT=IDENT[:], rhs=zz.bitcast(F32R)[:, hsl],
                        start=(nn == 0), stop=(nn == NST - 1)),
                        deps=bZZ[par].rdeps() + bIDENT.rdeps() + (bPS[2].wdeps() if nn == 0 and h == 0 else []))
                    bZZ[par].addr(tok)
                    if nn == NST - 1:
                        bPS[2].setw(tok)
            tok = P.op("v", lambda e, n, dt=dt: e.scalar_tensor_tensor(
                out=A[:, asl_(dt)], in0=A[:, asl_(dt)], scalar=DP[:, dt:dt + 1], in1=ps3[:],
                op0=ALU.mult, op1=ALU.add),
                deps=bAs[dt].rdeps() + bAs[dt].wdeps() + bDP.rdeps() + bPS[2].rdeps())
            bAs[dt].setw(tok); bPS[2].addr(tok)
            trig_deps = bSTT.rdeps() + bSTT.wdeps()
            P._wait("g", trig_deps)
            P.ops["g"].append(("r", lambda e, n: e.trigger_dma(count=1)))

        if debug:
            tokd = P.dma("g", lambda e, n: e.dma_start(out=dbgY[:], in_=A[:]),
                         deps=[t for b in bAs for t in b.rdeps()])
            for b in bAs:
                b.addr(tokd)
            tokd = P.dma("g", lambda e, n: e.dma_start(out=dbgHIN[:], in_=HINB[:]),
                         deps=bSTT.rdeps() + bSTT.wdeps())
            tokd = P.dma("g", lambda e, n: e.dma_start(out=dbgSTT[:], in_=STTB[:]),
                         deps=bSTT.rdeps() + bSTT.wdeps())

        bWST[0].rtoks += bCs[8].wdeps()
        bWST[1].rtoks += bCs[8].wdeps()
        emit_abs(len(abs_jobs), LC, 8 * LC)
        for b_ in (bCs[8], bCs[9]):
            b_.rtoks += bWST[0].wdeps() + bWST[1].wdeps()
        tred = []
        for m in range(3):
            nj = 88 if m == 2 else 96
            tok = P.op("s", lambda e, n, m=m, nj=nj: e.activation(out=C[:, 4 * LC:4 * LC + nj], in_=ABSACC[:, m * 96:m * 96 + nj],
                       func=AF.Abs, accum_out=ABSACC[:, 290 + m:291 + m]),
                       deps=bABS.rdeps() + bABS.wdeps() + bCs[4].wdeps())
            bABS.setw(tok); bCs[4].setw(tok)
            tok = P.op("t", lambda e, n, m=m: e.matmul(out=ps4[0:1, m:m + 1], lhsT=ABSACC[:, 290 + m:291 + m],
                       rhs=ONESCF[:], start=True, stop=True),
                       deps=bABS.rdeps() + bONESCF.rdeps() + (bPS[3].wdeps() if m == 0 else []))
            bABS.addr(tok)
            tred.append(tok)
        bPS[3].setw(tred[-1])
        NW = float(D) * H
        tok = P.op("v", lambda e, n: e.tensor_scalar(out=ROWS[0:1, 0:3], in0=ps4[0:1, 0:3],
                   scalar1=1.0 / NW, scalar2=1e-5, op0=ALU.mult, op1=ALU.max),
                   deps=tred + bROWS.wdeps())
        bROWS.setw(tok); bPS[3].addr(tok)
        tok = P.op("v", lambda e, n: e.reciprocal(out=ROWS[0:1, 3:6], in_=ROWS[0:1, 0:3]),
                   deps=bROWS.rdeps() + bROWS.wdeps())
        bROWS.setw(tok)
        tok = P.op("t", lambda e, n: e.matmul(out=ps4[:, 0:6], lhsT=ONES[:], rhs=ROWS[0:1, 0:6],
                   start=True, stop=True), deps=bROWS.rdeps() + bONES.rdeps() + bPS[3].wdeps())
        bPS[3].setw(tok); bROWS.addr(tok)
        tok = P.op("s", lambda e, n: e.activation(out=SWB[:, 0:6], in_=ps4[:, 0:6], func=AF.Copy),
                   deps=bPS[3].rdeps() + bSWB.wdeps() + bSWB.rdeps())
        bSWB.setw(tok); bPS[3].addr(tok)
        if debug:
            tokd = P.dma("g", lambda e, n: e.dma_start(out=dbgSW[:, 0:7], in_=SWB[:, 0:7]), deps=bSWB.rdeps() + [tok])
            bSWB.addr(tokd)

        def swmean(m):
            return SWB[:, m:m + 1]

        def swinv(m):
            return SWB[:, 3 + m:4 + m]

        B1536 = SWB[:, 6:7]

        trs, tmr = None, None
        for a in range(NA):
            sq = cs(4)
            for h in range(NHF):
                hsl = slice(h * FH, (h + 1) * FH)
                tm = P.op("t", lambda e, n, a=a, hsl=hsl: e.matmul(
                    out=ps4[0:1, hsl], lhsT=ONESCF[:], rhs=A[:, asl_(a)][:, hsl],
                    start=(a == 0), stop=(a == NA - 1)),
                    deps=bAs[a].rdeps() + bONESCF.rdeps() + (bPS[3].wdeps() if a == 0 else []))
                bAs[a].addr(tm)
                if a == NA - 1:
                    bPS[3].setw(tm)
            tsq = P.op("s", lambda e, n, a=a, sq=sq: e.activation(out=sq, in_=A[:, asl_(a)],
                       func=AF.Square), deps=bAs[a].rdeps() + bCs[4].wdeps())
            bCs[4].setw(tsq); bAs[a].addr(tsq)
            for h in range(NHF):
                hsl = slice(h * FH, (h + 1) * FH)
                tq = P.op("t", lambda e, n, hsl=hsl, a=a, sq=sq: e.matmul(
                    out=ps3[0:1, hsl], lhsT=ONESCF[:], rhs=sq[:, hsl],
                    start=(a == 0), stop=(a == NA - 1)),
                    deps=bCs[4].rdeps() + bONESCF.rdeps() + (bPS[2].wdeps() if a == 0 else []))
                bCs[4].addr(tq)
                if a == NA - 1:
                    bPS[2].setw(tq)
        trs, tmr = ln_rows(ps4, ps3, bPS[3], bPS[2], 1.0 / D)
        tb1 = bcast(rs(1), ps1, bPS[0], [trs])
        tb2 = bcast(rs(2), ps2, bPS[1], [tmr])
        x1toks = []
        for a in range(NA):
            t1_ = P.op("v", lambda e, n, a=a: e.tensor_tensor(out=cs(5), in0=A[:, asl_(a)], in1=ps1[:],
                       op=ALU.mult), deps=bAs[a].rdeps() + [tb1] + bCs[5].wdeps())
            bCs[5].setw(t1_); bAs[a].addr(t1_); bPS[0].addr(t1_)
            t2_ = P.op("v", lambda e, n: e.tensor_tensor(out=cs(5), in0=cs(5), in1=ps2[:],
                       op=ALU.subtract), deps=bCs[5].rdeps() + [tb2] + bCs[5].wdeps())
            bCs[5].setw(t2_); bPS[1].addr(t2_)
            t3_ = P.op("v", lambda e, n, a=a: e.tensor_tensor(out=B[:, asl_(a)], in0=cs(5), in1=B[:, asl_(a)],
                       op=ALU.add), deps=bCs[5].rdeps() + bBs[a].rdeps() + bBs[a].wdeps())
            bBs[a].setw(t3_); bCs[5].addr(t3_)
            tok = P.dma("g", lambda e, n, a=a: e.dma_start(out=x1h[:, asl_(a)], in_=B[:, asl_(a)]),
                        deps=bBs[a].rdeps() + bX1H.wdeps())
            bBs[a].addr(tok); bX1H.setw(tok); x1toks.append(tok)
            sq = cs(4)
            for h in range(NHF):
                hsl = slice(h * FH, (h + 1) * FH)
                tm = P.op("t", lambda e, n, a=a, hsl=hsl: e.matmul(
                    out=ps4[0:1, hsl], lhsT=ONESCF[:], rhs=B[:, asl_(a)][:, hsl],
                    start=(a == 0), stop=(a == NA - 1)),
                    deps=bBs[a].rdeps() + bONESCF.rdeps() + (bPS[3].wdeps() if a == 0 else []))
                bBs[a].addr(tm)
                if a == NA - 1:
                    bPS[3].setw(tm)
            tsq = P.op("s", lambda e, n, a=a, sq=sq: e.activation(out=sq, in_=B[:, asl_(a)],
                       func=AF.Square), deps=bBs[a].rdeps() + bCs[4].wdeps())
            bCs[4].setw(tsq); bBs[a].addr(tsq)
            for h in range(NHF):
                hsl = slice(h * FH, (h + 1) * FH)
                tq = P.op("t", lambda e, n, hsl=hsl, a=a, sq=sq: e.matmul(
                    out=ps3[0:1, hsl], lhsT=ONESCF[:], rhs=sq[:, hsl],
                    start=(a == 0), stop=(a == NA - 1)),
                    deps=bCs[4].rdeps() + bONESCF.rdeps() + (bPS[2].wdeps() if a == 0 else []))
                bCs[4].addr(tq)
                if a == NA - 1:
                    bPS[2].setw(tq)
        if debug:
            tokd = P.dma("g", lambda e, n: e.dma_start(out=dbgX1[:], in_=x1h[:]), deps=x1toks)
            bX1H.addr(tokd)
        trs, tmr = ln_rows(ps4, ps3, bPS[3], bPS[2], 1.0 / D)
        tb1 = bcast(rs(1), ps1, bPS[0], [trs])
        tb2 = bcast(rs(2), ps2, bPS[1], [tmr])
        for a in range(NA):
            t1_ = P.op("v", lambda e, n, a=a: e.tensor_tensor(out=A[:, asl_(a)], in0=B[:, asl_(a)], in1=ps1[:],
                       op=ALU.mult), deps=bBs[a].rdeps() + [tb1] + bAs[a].wdeps())
            bAs[a].setw(t1_); bBs[a].addr(t1_); bPS[0].addr(t1_)
            t2_ = P.op("v", lambda e, n, a=a: e.tensor_tensor(out=A[:, asl_(a)], in0=A[:, asl_(a)], in1=ps2[:],
                       op=ALU.subtract), deps=bAs[a].rdeps() + [tb2] + bAs[a].wdeps())
            bAs[a].setw(t2_); bPS[1].addr(t2_)
            if a == 0:
                tok = P.op("v", lambda e, n: e.tensor_copy(out=cs(6), in_=A[:, asl_(0)]),
                           deps=bAs[0].rdeps() + bCs[6].wdeps())
                bCs[6].setw(tok); bAs[0].addr(tok)
                tok = P.op("v", lambda e, n: e.tensor_copy(out=cs(7), in_=A[:, asl_(0)]),
                           deps=bAs[0].rdeps() + bCs[7].wdeps())
                bCs[7].setw(tok); bAs[0].addr(tok)
            else:
                tok = P.op("v", lambda e, n, a=a: e.tensor_tensor(out=cs(6), in0=A[:, asl_(a)], in1=cs(6),
                           op=ALU.max), deps=bAs[a].rdeps() + bCs[6].rdeps() + bCs[6].wdeps())
                bCs[6].setw(tok); bAs[a].addr(tok)
                tok = P.op("v", lambda e, n, a=a: e.tensor_tensor(out=cs(7), in0=A[:, asl_(a)], in1=cs(7),
                           op=ALU.min), deps=bAs[a].rdeps() + bCs[7].rdeps() + bCs[7].wdeps())
                bCs[7].setw(tok); bAs[a].addr(tok)
        tok = P.op("v", lambda e, n: e.tensor_scalar(out=cs(7), in0=cs(7), scalar1=-1.0, scalar2=None,
                   op0=ALU.mult), deps=bCs[7].rdeps() + bCs[7].wdeps())
        bCs[7].setw(tok)
        tok = P.op("v", lambda e, n: e.tensor_tensor(out=cs(6), in0=cs(7), in1=cs(6), op=ALU.max),
                   deps=bCs[6].rdeps() + bCs[7].rdeps() + bCs[6].wdeps())
        bCs[6].setw(tok); bCs[7].addr(tok)
        tok = P.op("g", lambda e, n: e.partition_all_reduce(out_ap=cs(9), in_ap=cs(6),
                   channels=128, reduce_op=ReduceOp.max), deps=bCs[6].rdeps() + bCs[9].wdeps())
        bCs[9].setw(tok); bCs[6].addr(tok)
        tok = P.op("v", lambda e, n: e.tensor_scalar(out=cs(9), in0=cs(9), scalar1=1e-5, scalar2=None,
                   op0=ALU.max), deps=bCs[9].rdeps() + bCs[9].wdeps())
        bCs[9].setw(tok)
        tok = P.op("v", lambda e, n: e.tensor_scalar(out=cs(8), in0=cs(9), scalar1=1.0 / 127.0, scalar2=None,
                   op0=ALU.mult), deps=bCs[9].rdeps() + bCs[8].wdeps())
        bCs[8].setw(tok); bCs[9].addr(tok)
        tok = P.op("v", lambda e, n: e.reciprocal(out=cs(9), in_=cs(9)), deps=bCs[9].rdeps() + bCs[9].wdeps())
        bCs[9].setw(tok)
        tok = P.op("v", lambda e, n: e.tensor_scalar(out=cs(9), in0=cs(9), scalar1=127.0, scalar2=None,
                   op0=ALU.mult), deps=bCs[9].rdeps() + bCs[9].wdeps())
        bCs[9].setw(tok)
        QF = C[:, 4 * LC:4 * LC + LC // 2].bitcast(F16)
        for a in range(NA):
            t5_ = P.op("v", lambda e, n, a=a: e.tensor_tensor(out=cs(5), in0=A[:, asl_(a)], in1=cs(9),
                       op=ALU.mult), deps=bAs[a].rdeps() + bCs[9].rdeps() + bCs[5].wdeps())
            bCs[5].setw(t5_); bAs[a].addr(t5_); bCs[9].addr(t5_)
            t6_ = P.op("v", lambda e, n: e.tensor_scalar(out=QF[:], in0=cs(5), scalar1=12584448.0,
                       scalar2=-12582912.0, op0=ALU.add, op1=ALU.add), deps=bCs[5].rdeps() + bCs[4].wdeps())
            bCs[4].setw(t6_); bCs[5].addr(t6_)
            t7_ = P.op("v", lambda e, n, a=a: e.tensor_scalar(out=Ab[:, asl_(a)], in0=QF[:],
                       scalar1=-1536.0, scalar2=None, op0=ALU.add),
                       deps=bCs[4].rdeps() + bXQ.wdeps() + bAs[a // 2].wdeps())
            bXQ.setw(t7_); bCs[4].addr(t7_); bAs[a // 2].addr(t7_)

        def bsl(i):
            return B[:, i * LC:(i + 1) * LC]

        bF = [Bu() for _ in range(16)]
        fgateB = [t for b in bBs for t in b.wdeps()]
        hf_toks = []
        for ht in range(HT):
            par = ht % 2
            w1strip = C[:, (0 if par == 0 else 2) * LC:(0 if par == 0 else 2) * LC + 2048]
            w3strip = C[:, (4 if par == 0 else 6) * LC:(4 if par == 0 else 6) * LC + 2048]
            cw1 = bCs[0] if par == 0 else bCs[2]
            cw3 = bCs[4] if par == 0 else bCs[6]
            tok = P.dma("y", lambda e, n, ht=ht, w1strip=w1strip: e.dma_start(
                out=w1strip.rearrange("p (a c) -> p a c", a=16),
                in_=w1T.rearrange("(a p) h -> p a h", p=128)[:, :, ht * 128:(ht + 1) * 128]),
                deps=cw1.wdeps() + (fgateB if ht < 2 else []))
            cw1.setw(tok)
            tok = P.dma("s", lambda e, n, ht=ht, w3strip=w3strip: e.dma_start(
                out=w3strip.rearrange("p (a c) -> p a c", a=16),
                in_=w3T.rearrange("(a p) h -> p a h", p=128)[:, :, ht * 128:(ht + 1) * 128]),
                deps=cw3.wdeps() + (fgateB if ht < 2 else []))
            cw3.setw(tok)
            MG1 = B[:, par * LC:par * LC + 1024].bitcast(F16)
            MG3 = B[:, (2 + par) * LC:(2 + par) * LC + 1024].bitcast(F16)
            W1Q = B[:, (4 + par) * LC:(4 + par) * LC + 1024].bitcast(BF16)
            W3Q = B[:, (6 + par) * LC:(6 + par) * LC + 1024].bitcast(BF16)

            def wqchain(strip, cw, MG, WQ, bmg, bwq, m):
                t0 = P.op("v", lambda e, n, strip=strip, m=m: e.tensor_scalar(
                    out=strip, in0=strip, scalar1=swinv(m), scalar2=None, op0=ALU.mult),
                    deps=cw.rdeps() + bSWB.rdeps() + cw.wdeps() + (fgateB if ht < 2 else []))
                cw.setw(t0)
                t1 = P.op("v", lambda e, n, strip=strip: e.tensor_scalar(
                    out=strip, in0=strip, scalar1=-1.45, scalar2=1.45, op0=ALU.max, op1=ALU.min),
                    deps=cw.rdeps() + cw.wdeps())
                cw.setw(t1)
                t2 = P.op("v", lambda e, n, strip=strip, MG=MG: e.tensor_scalar(
                    out=MG[:], in0=strip, scalar1=12584448.0, scalar2=-12582912.0,
                    op0=ALU.add, op1=ALU.add), deps=cw.rdeps() + bmg.wdeps())
                bmg.setw(t2); cw.addr(t2)
                t3 = P.op("v", lambda e, n, MG=MG, WQ=WQ: e.tensor_scalar(out=WQ[:], in0=MG[:],
                          scalar1=-1536.0, scalar2=None, op0=ALU.add),
                          deps=bmg.rdeps() + bwq.wdeps())
                bwq.setw(t3); bmg.addr(t3)
                return t3
            tq1 = wqchain(w1strip, cw1, MG1, W1Q, bF[par], bF[4 + par], 0)
            tq3 = wqchain(w3strip, cw3, MG3, W3Q, bF[2 + par], bF[6 + par], 1)
            psb1 = bPS[0] if par == 0 else bPS[2]
            psb3 = bPS[1] if par == 0 else bPS[3]
            ps_1 = ps1 if par == 0 else ps3
            ps_3 = ps2 if par == 0 else ps4
            for a in range(NA):
                for h in range(NHF):
                    hsl = slice(h * FH, (h + 1) * FH)
                    xsl = slice(a * LC + h * FH, a * LC + (h + 1) * FH)
                    tok = P.op("t", lambda e, n, a=a, hsl=hsl, xsl=xsl, W1Q=W1Q, ps_1=ps_1: e.matmul(
                        out=ps_1[:, hsl], lhsT=W1Q[:, a * 128:(a + 1) * 128], rhs=Ab[:, xsl],
                        start=(a == 0), stop=(a == NA - 1)),
                        deps=[tq1] + bXQ.rdeps() + (psb1.wdeps() if a == 0 and h == 0 else []))
                    bXQ.addr(tok); bF[4 + par].addr(tok)
                    if a == NA - 1:
                        psb1.setw(tok)
            for a in range(NA):
                for h in range(NHF):
                    hsl = slice(h * FH, (h + 1) * FH)
                    xsl = slice(a * LC + h * FH, a * LC + (h + 1) * FH)
                    tok = P.op("t", lambda e, n, a=a, hsl=hsl, xsl=xsl, W3Q=W3Q, ps_3=ps_3: e.matmul(
                        out=ps_3[:, hsl], lhsT=W3Q[:, a * 128:(a + 1) * 128], rhs=Ab[:, xsl],
                        start=(a == 0), stop=(a == NA - 1)),
                        deps=[tq3] + bXQ.rdeps() + (psb3.wdeps() if a == 0 and h == 0 else []))
                    bXQ.addr(tok); bF[6 + par].addr(tok)
                    if a == NA - 1:
                        psb3.setw(tok)
            tt1 = P.op("v", lambda e, n, ps_1=ps_1: e.tensor_tensor(out=bsl(10), in0=ps_1[:], in1=cs(8),
                       op=ALU.mult), deps=psb1.rdeps() + bCs[8].rdeps() + bF[10].wdeps())
            bF[10].setw(tt1); psb1.addr(tt1); bCs[8].addr(tt1)
            ts1 = P.op("s", lambda e, n: e.activation(out=bsl(10), in_=bsl(10), func=AF.Silu,
                       scale=swmean(0)), deps=bF[10].rdeps() + bSWB.rdeps() + bF[10].wdeps())
            bF[10].setw(ts1)
            tt3 = P.op("v", lambda e, n, ps_3=ps_3: e.tensor_tensor(out=bsl(11), in0=ps_3[:], in1=cs(8),
                       op=ALU.mult), deps=psb3.rdeps() + bCs[8].rdeps() + bF[11].wdeps())
            bF[11].setw(tt3); psb3.addr(tt3); bCs[8].addr(tt3)
            thf = P.op("v", lambda e, n: e.scalar_tensor_tensor(out=bsl(12), in0=bsl(10), scalar=swmean(1),
                       in1=bsl(11), op0=ALU.mult, op1=ALU.mult),
                       deps=bF[10].rdeps() + bF[11].rdeps() + bSWB.rdeps() + bF[12].wdeps())
            bF[12].setw(thf); bF[10].addr(thf); bF[11].addr(thf)
            if ht == 0:
                tok = P.op("v", lambda e, n: e.tensor_copy(out=bsl(8), in_=bsl(12)),
                           deps=bF[12].rdeps() + bF[8].wdeps())
                bF[8].setw(tok); bF[12].addr(tok)
                tok = P.op("v", lambda e, n: e.tensor_copy(out=bsl(9), in_=bsl(12)),
                           deps=bF[12].rdeps() + bF[9].wdeps())
                bF[9].setw(tok); bF[12].addr(tok)
            else:
                tok = P.op("v", lambda e, n: e.tensor_tensor(out=bsl(8), in0=bsl(12), in1=bsl(8),
                           op=ALU.max), deps=bF[12].rdeps() + bF[8].rdeps() + bF[8].wdeps())
                bF[8].setw(tok); bF[12].addr(tok)
                tok = P.op("v", lambda e, n: e.tensor_tensor(out=bsl(9), in0=bsl(12), in1=bsl(9),
                           op=ALU.min), deps=bF[12].rdeps() + bF[9].rdeps() + bF[9].wdeps())
                bF[9].setw(tok); bF[12].addr(tok)
            tok = P.dma("g", lambda e, n, ht=ht: e.dma_start(out=hfh[:, ht * LC:(ht + 1) * LC],
                        in_=bsl(12)), deps=bF[12].rdeps() + bHFH.wdeps())
            bF[12].addr(tok); bHFH.setw(tok); hf_toks.append(tok)
        if debug:
            tokd = P.dma("g", lambda e, n: e.dma_start(out=dbgHF[:], in_=hfh[:]), deps=hf_toks)
            bHFH.addr(tokd)

        tok = P.op("v", lambda e, n: e.tensor_scalar(out=bsl(9), in0=bsl(9), scalar1=-1.0, scalar2=None,
                   op0=ALU.mult), deps=bF[9].rdeps() + bF[9].wdeps())
        bF[9].setw(tok)
        tok = P.op("v", lambda e, n: e.tensor_tensor(out=bsl(8), in0=bsl(9), in1=bsl(8), op=ALU.max),
                   deps=bF[8].rdeps() + bF[9].rdeps() + bF[8].wdeps())
        bF[8].setw(tok); bF[9].addr(tok)
        tok = P.op("g", lambda e, n: e.partition_all_reduce(out_ap=cs(9), in_ap=bsl(8),
                   channels=128, reduce_op=ReduceOp.max), deps=bF[8].rdeps() + bCs[9].wdeps() + bCs[9].rdeps())
        bCs[9].setw(tok); bF[8].addr(tok)
        tok = P.op("v", lambda e, n: e.tensor_scalar(out=cs(9), in0=cs(9), scalar1=1e-5, scalar2=None,
                   op0=ALU.max), deps=bCs[9].rdeps() + bCs[9].wdeps())
        bCs[9].setw(tok)
        tok = P.op("v", lambda e, n: e.tensor_scalar(out=cs(8), in0=cs(9), scalar1=1.0 / 127.0, scalar2=None,
                   op0=ALU.mult), deps=bCs[9].rdeps() + bCs[8].wdeps() + bCs[8].rdeps())
        bCs[8].setw(tok); bCs[9].addr(tok)
        tok = P.op("v", lambda e, n: e.reciprocal(out=cs(9), in_=cs(9)), deps=bCs[9].rdeps() + bCs[9].wdeps())
        bCs[9].setw(tok)
        tok = P.op("v", lambda e, n: e.tensor_scalar(out=cs(9), in0=cs(9), scalar1=127.0, scalar2=None,
                   op0=ALU.mult), deps=bCs[9].rdeps() + bCs[9].wdeps())
        bCs[9].setw(tok)

        def hq_ap(j):
            if j < 32:
                return Bb[:, j * LC:(j + 1) * LC]
            return Ab[:, (16 + (j - 32)) * LC:(17 + (j - 32)) * LC]

        bHFR = [Bu(), Bu()]
        QF2 = C[:, 4 * LC:4 * LC + LC // 2].bitcast(F16)
        allF1B = [t for b in bF for t in b.wdeps()] + [t for b in bAs for t in b.wdeps()]
        for j in range(HT):
            par = j % 2
            hfr = A[:, (14 + par) * LC:(15 + par) * LC]
            tok = P.dma("y", lambda e, n, j=j, hfr=hfr: e.dma_start(
                out=hfr, in_=hfh[:, j * LC:(j + 1) * LC]),
                deps=bHFR[par].wdeps() + [hf_toks[j]] +
                     ([t for b in bAs for t in b.wdeps()] if j < 2 else []))
            bHFR[par].setw(tok); bHFH.addr(tok)
            t5_ = P.op("v", lambda e, n, hfr=hfr: e.tensor_tensor(out=cs(5), in0=hfr, in1=cs(9),
                       op=ALU.mult), deps=bHFR[par].rdeps() + bCs[9].rdeps() + bCs[5].wdeps())
            bCs[5].setw(t5_); bHFR[par].addr(t5_); bCs[9].addr(t5_)
            t6_ = P.op("v", lambda e, n: e.tensor_scalar(out=QF2[:], in0=cs(5), scalar1=12584448.0,
                       scalar2=-12582912.0, op0=ALU.add, op1=ALU.add), deps=bCs[5].rdeps() + bCs[4].wdeps())
            bCs[4].setw(t6_); bCs[5].addr(t6_)
            t7_ = P.op("v", lambda e, n, j=j: e.tensor_scalar(out=hq_ap(j), in0=QF2[:],
                       scalar1=-1536.0, scalar2=None, op0=ALU.add),
                       deps=bCs[4].rdeps() + bHQ.wdeps() + (allF1B if j == 0 else []) + bXQ.wdeps())
            bHQ.setw(t7_); bCs[4].addr(t7_)

        def b2_ap(dout):
            if dout < 8:
                return A[:, asl_(dout)]
            return cs(dout - 8)

        bB2 = [Bu() for _ in range(NA)]
        bW2R = [Bu(), Bu()]
        bMG2 = [Bu(), Bu()]
        bW2Q = [Bu(), Bu()]
        for grp in range(4):
            for j in range(HT):
                par = j % 2
                w2raw = C[:, 9 * LC + par * 512:9 * LC + (par + 1) * 512]
                MG2 = MGW[:, par * 256:(par + 1) * 256].bitcast(F16)
                W2Q = MGW[:, 512 + par * 256:512 + (par + 1) * 256].bitcast(BF16)
                tok = P.dma("s", lambda e, n, j=j, grp=grp, w2raw=w2raw: e.dma_start(
                    out=w2raw, in_=w2T[j * 128:(j + 1) * 128, grp * 512:(grp + 1) * 512]),
                    deps=bW2R[par].wdeps() + (bCs[9].wdeps() if grp == 0 and j < 2 else []))
                bW2R[par].setw(tok)
                t0 = P.op("v", lambda e, n, w2raw=w2raw: e.tensor_scalar(
                    out=w2raw, in0=w2raw, scalar1=swinv(2), scalar2=None, op0=ALU.mult),
                    deps=bW2R[par].rdeps() + bSWB.rdeps() + bW2R[par].wdeps())
                bW2R[par].setw(t0)
                t1 = P.op("v", lambda e, n, w2raw=w2raw: e.tensor_scalar(
                    out=w2raw, in0=w2raw, scalar1=-1.45, scalar2=1.45, op0=ALU.max, op1=ALU.min),
                    deps=bW2R[par].rdeps() + bW2R[par].wdeps())
                bW2R[par].setw(t1)
                t1b = P.op("v", lambda e, n, w2raw=w2raw, MG2=MG2: e.tensor_scalar(
                    out=MG2[:], in0=w2raw, scalar1=12584448.0, scalar2=-12582912.0,
                    op0=ALU.add, op1=ALU.add), deps=bW2R[par].rdeps() + bMG2[par].wdeps())
                bMG2[par].setw(t1b); bW2R[par].addr(t1b)
                t2 = P.op("v", lambda e, n, MG2=MG2, W2Q=W2Q: e.tensor_scalar(out=W2Q[:], in0=MG2[:],
                          scalar1=-1536.0, scalar2=None, op0=ALU.add),
                          deps=bMG2[par].rdeps() + bW2Q[par].wdeps())
                bW2Q[par].setw(t2); bMG2[par].addr(t2)
                for dg in range(4):
                    for h in range(NHF):
                        hsl = slice(h * FH, (h + 1) * FH)
                        psd = bPS[dg]
                        pst = [ps1, ps2, ps3, ps4][dg]
                        tok = P.op("t", lambda e, n, dg=dg, hsl=hsl, j=j, W2Q=W2Q, pst=pst: e.matmul(
                            out=pst[:, hsl], lhsT=W2Q[:, dg * 128:(dg + 1) * 128],
                            rhs=hq_ap(j)[:, hsl], start=(j == 0), stop=(j == HT - 1)),
                            deps=bW2Q[par].rdeps() + bHQ.rdeps() +
                                 (psd.wdeps() if j == 0 and h == 0 else []))
                        bW2Q[par].addr(tok); bHQ.addr(tok)
                        if j == HT - 1:
                            psd.setw(tok)
            for dg in range(4):
                dout = grp * 4 + dg
                tok = P.op("v", lambda e, n, dg=dg, dout=dout: e.scalar_tensor_tensor(
                    out=b2_ap(dout), in0=[ps1, ps2, ps3, ps4][dg][:], scalar=swmean(2), in1=cs(8),
                    op0=ALU.mult, op1=ALU.mult),
                    deps=bPS[dg].rdeps() + bSWB.rdeps() + bCs[8].rdeps() + bB2[dout].wdeps() +
                         (bXQ.wdeps() if dout < 8 else []) + (bCs[dout - 8].wdeps() if dout >= 8 else []))
                bB2[dout].setw(tok); bPS[dg].addr(tok); bCs[8].addr(tok)
                if dout >= 8:
                    bCs[dout - 8].setw(tok)

        bCs[9].rtoks += bW2R[0].wdeps() + bW2R[1].wdeps()
        for a in range(NA):
            sq = cs(9)
            for h in range(NHF):
                hsl = slice(h * FH, (h + 1) * FH)
                tm = P.op("t", lambda e, n, a=a, hsl=hsl: e.matmul(
                    out=ps1[0:1, hsl], lhsT=ONESCF[:], rhs=b2_ap(a)[:, hsl],
                    start=(a == 0), stop=(a == NA - 1)),
                    deps=bB2[a].rdeps() + bONESCF.rdeps() + (bPS[0].wdeps() if a == 0 else []))
                bB2[a].addr(tm)
                if a == NA - 1:
                    bPS[0].setw(tm)
            tsq = P.op("s", lambda e, n, a=a, sq=sq: e.activation(out=sq, in_=b2_ap(a), func=AF.Square),
                       deps=bB2[a].rdeps() + bCs[9].wdeps() + bCs[9].rdeps())
            bCs[9].setw(tsq); bB2[a].addr(tsq)
            for h in range(NHF):
                hsl = slice(h * FH, (h + 1) * FH)
                tq = P.op("t", lambda e, n, hsl=hsl, a=a, sq=sq: e.matmul(
                    out=ps2[0:1, hsl], lhsT=ONESCF[:], rhs=sq[:, hsl],
                    start=(a == 0), stop=(a == NA - 1)),
                    deps=bCs[9].rdeps() + bONESCF.rdeps() + (bPS[1].wdeps() if a == 0 else []))
                bCs[9].addr(tq)
                if a == NA - 1:
                    bPS[1].setw(tq)
        trs, tmr = ln_rows(ps1, ps2, bPS[0], bPS[1], 1.0 / D)
        tb1 = bcast(rs(1), ps3, bPS[2], [trs])
        tb2 = bcast(rs(2), ps4, bPS[3], [tmr])
        out_toks = []
        bOUT = [Bu(), Bu()]
        for a in range(NA):
            par = a % 2
            x1r = A[:, (14 + par) * LC:(15 + par) * LC]
            oscr = cs(9) if par == 0 else MGW[:, 0:LC]
            bo = bOUT[par]
            bx = bHFR[par]
            tok = P.dma("y", lambda e, n, a=a, x1r=x1r: e.dma_start(out=x1r, in_=x1h[:, asl_(a)]),
                        deps=bx.wdeps() + [x1toks[a]])
            bx.setw(tok); bX1H.addr(tok)
            t1_ = P.op("v", lambda e, n, a=a, oscr=oscr: e.tensor_tensor(out=oscr, in0=b2_ap(a), in1=ps3[:],
                       op=ALU.mult), deps=bB2[a].rdeps() + [tb1] + bo.wdeps() +
                       (bCs[9].rdeps() + bCs[9].wdeps() if par == 0 else bMGW.rdeps() + bMGW.wdeps()))
            bo.setw(t1_); bB2[a].addr(t1_); bPS[2].addr(t1_)
            if par == 0:
                bCs[9].setw(t1_)
            else:
                bMGW.setw(t1_)
            t2_ = P.op("v", lambda e, n, oscr=oscr: e.tensor_tensor(out=oscr, in0=oscr, in1=ps4[:],
                       op=ALU.subtract), deps=bo.rdeps() + [tb2] + bo.wdeps())
            bo.setw(t2_); bPS[3].addr(t2_)
            t3_ = P.op("v", lambda e, n, oscr=oscr, x1r=x1r: e.tensor_tensor(out=oscr, in0=oscr, in1=x1r,
                       op=ALU.add), deps=bo.rdeps() + bx.rdeps() + bo.wdeps())
            bo.setw(t3_); bx.addr(t3_)
            tok = P.dma("g", lambda e, n, a=a, oscr=oscr: e.dma_start(
                out=outT[a * 128:(a + 1) * 128, :], in_=oscr), deps=bo.rdeps())
            bo.addr(tok); out_toks.append(tok)
        P.barrier("g", out_toks)
        P.barrier("v", out_toks)

        with nc.Block() as block:
            P.build_blocks(block)
    finalize(nc)
    return nc


LCK = 1024

def _build_inmaps(L0_attn_x, L0_attn_W_xproj, L0_attn_W_dt, L0_attn_b_dt,
                  L0_attn_A_log, L0_attn_D_param, L0_ffn_w1, L0_ffn_w2, L0_ffn_w3):
    import ml_dtypes
    x = np.asarray(L0_attn_x[0], np.float32)
    xT = np.ascontiguousarray(x.T)
    sel = np.zeros((32, 32 * 128), np.float32)
    for k in range(32):
        sel[k, k * 128:(k + 1) * 128] = 1.0
    common = dict(
        wxp=np.ascontiguousarray(np.asarray(L0_attn_W_xproj, np.float32).T),
        wdt=np.ascontiguousarray(np.asarray(L0_attn_W_dt, np.float32).T),
        bdt=np.ascontiguousarray(np.asarray(L0_attn_b_dt, np.float32).reshape(16, 128).T),
        alog=np.ascontiguousarray(np.asarray(L0_attn_A_log, np.float32).reshape(16, 128, 16)
                                  .transpose(1, 0, 2).reshape(128, 256)),
        dpar=np.ascontiguousarray(np.asarray(L0_attn_D_param, np.float32).reshape(16, 128).T),
        w1T=np.ascontiguousarray(np.asarray(L0_ffn_w1, np.float32).T),
        w3T=np.ascontiguousarray(np.asarray(L0_ffn_w3, np.float32).T),
        w2T=np.ascontiguousarray(np.asarray(L0_ffn_w2, np.float32).T),
        ident=np.eye(128, dtype=np.float32),
        sel32=sel.astype(ml_dtypes.bfloat16),
        ones1=np.ones((1, 128), np.float32),
        onescf=np.ones((128, 1), np.float32),
        consts=np.tile(np.array([1e-5, 1.0, 0, 0, 0, 0, 0, 0], np.float32), (128, 1)),
    )
    in_maps = []
    for i in range(8):
        m = dict(common)
        m["xTf"] = np.ascontiguousarray(xT[:, i * LCK:(i + 1) * LCK])
        in_maps.append(m)
    return in_maps


def kernel(L0_attn_x, L0_attn_W_xproj, L0_attn_W_dt, L0_attn_b_dt, L0_attn_A_log,
           L0_attn_D_param, L0_ffn_w1, L0_ffn_w2, L0_ffn_w3):
    from concourse.bass_utils import run_bass_kernel_spmd
    nc = build2(LC=LCK, debug=False)
    in_maps = _build_inmaps(L0_attn_x, L0_attn_W_xproj, L0_attn_W_dt, L0_attn_b_dt,
                            L0_attn_A_log, L0_attn_D_param, L0_ffn_w1, L0_ffn_w2, L0_ffn_w3)
    r = run_bass_kernel_spmd(nc, in_maps, core_ids=list(range(8)))
    out = np.concatenate([np.asarray(r.results[i]["outT"]).T for i in range(8)], 0)
    return out[None, :, :].astype(np.float32)


# version 10 (final)
# speedup vs baseline: 4.5427x; 1.1262x over previous
import numpy as np
import concourse.bass as bass
import concourse.mybir as mybir

F32 = mybir.dt.float32
F32R = mybir.dt.float32r
BF16 = mybir.dt.bfloat16
F16 = mybir.dt.float16
AF = mybir.ActivationFunctionType
ALU = mybir.AluOpType
AX = mybir.AxisListType

ENGINES = ("v", "s", "t", "g", "y")
NQ = 12

class Prog:

    def __init__(self, nc):
        self.nc = nc
        self.ops = {e: [] for e in ENGINES}
        self.count = {}
        self.sem_names = ["SM_" + e for e in ENGINES]
        self.qpool = {}
        for q in ("y", "s", "g"):
            self.qpool[q] = ["Q%s%d" % (q, i) for i in range(NQ)]
            self.sem_names += self.qpool[q]
        for s_ in self.sem_names:
            self.count[s_] = 0
        self.qidx = {q: 0 for q in self.qpool}
        self.eng_sem = {e: "SM_" + e for e in ENGINES}
        self.wm = {e: {} for e in ENGINES}
        self.sem_ctx = None

    def _wait(self, e, deps):
        own = self.eng_sem[e]
        for d in deps:
            if d is None:
                continue
            sem, val = d
            if val <= 0:
                continue
            if e == "t" and sem == own:
                continue
            cur = self.wm[e].get(sem, 0)
            if val > cur:
                self.wm[e][sem] = val
                self.ops[e].append(("w", sem, val))

    def op(self, e, emit, deps=()):
        self._wait(e, deps)
        sem = self.eng_sem[e]
        self.count[sem] += 1
        self.ops[e].append(("i", emit, sem))
        return (sem, self.count[sem])

    def q_alloc(self, e):
        pool = self.qpool[e]
        sem = pool[self.qidx[e] % NQ]
        self.qidx[e] += 1
        return sem, self.count[sem]

    def dma(self, e, emit, deps=()):
        sem, prior = self.q_alloc(e)
        alldeps = list(deps)
        if prior > 0:
            alldeps.append((sem, prior))
        self._wait(e, alldeps)
        self.count[sem] += 16
        self.ops[e].append(("i", emit, sem))
        return (sem, self.count[sem])

    def raw(self, e, emit, deps=()):
        self._wait(e, deps)
        self.ops[e].append(("r", emit))
        return None

    def barrier(self, e, deps):
        self._wait(e, deps)

    def build_blocks(self, block):
        nc = self.nc
        handles = self.sem_ctx
        engmap = {"v": nc.vector, "s": nc.scalar, "t": nc.tensor, "g": nc.gpsimd, "y": nc.sync}
        def mk(e):
            eng = engmap[e]
            def body(_):
                for item in self.ops[e]:
                    if item[0] == "w":
                        eng.wait_ge(handles[item[1]], item[2])
                    elif item[0] == "i":
                        inst = item[1](eng, nc)
                        inc = 16 if item[2] not in self.eng_sem.values() else 1
                        inst.then_inc(handles[item[2]], inc)
                    else:
                        item[1](eng, nc)
            return body
        for e in ENGINES:
            if self.ops[e]:
                getattr(block, {"v": "vector", "s": "scalar", "t": "tensor", "g": "gpsimd", "y": "sync"}[e])(mk(e))


class TBuf:
    __slots__ = ("wtok", "rtoks")
    def __init__(self):
        self.wtok = None
        self.rtoks = []
    def rdeps(self):
        return [self.wtok] if self.wtok else []
    def wdeps(self):
        d = list(self.rtoks)
        if self.wtok:
            d.append(self.wtok)
        return d
    def setw(self, tok):
        self.wtok = tok
        self.rtoks = []
    def addr(self, tok):
        if tok:
            self.rtoks.append(tok)


def finalize(nc):
    import bass_rust
    import concourse.mybir as _mb
    from concourse.library_config import all_libraries, standard
    mask = {}
    for lib in all_libraries:
        for it in lib.instructions:
            mask[it] = mask.get(it, 0) | (1 << lib.index)
    bass_rust.insert_library_loads(nc, mask, len(all_libraries), standard.index)
    _mb.codegen_inst_isa_subclasses(nc)
    return nc

"""V2: token-sharded end-to-end, zero stagger-exposed cross-core waits.

Per core (tokens [LC*i, LC*(i+1))): P (LN1+x_proj), M (scan w/ carry chain
core i -> i+1), F (BitLinear FFN, local abs-pass for weight scales).
"""
from contextlib import ExitStack
from concourse.bass_isa import ReduceOp

D = 2048
NA = 16
NST = 16
R_LO = 64
E_XD = 96
H = 5632
HT = 44
MAGIC1 = 12584448.0
MAGIC2 = -12582912.0


def build2(LC=1024, debug=False):
    FH = min(512, LC)
    NHF = LC // FH
    nc = bass.Bass("TRN2", target_bir_lowering=False, debug=False)
    E = nc.ctx.enter_context

    xTf = nc.dram_tensor("xTf", [D, LC], F32, kind="ExternalInput").ap()
    wxp = nc.dram_tensor("wxp", [D, E_XD], F32, kind="ExternalInput").ap()
    wdt = nc.dram_tensor("wdt", [R_LO, D], F32, kind="ExternalInput").ap()
    bdt = nc.dram_tensor("bdt", [128, 16], F32, kind="ExternalInput").ap()
    alog = nc.dram_tensor("alog", [128, 256], F32, kind="ExternalInput").ap()
    dpar = nc.dram_tensor("dpar", [128, 16], F32, kind="ExternalInput").ap()
    w1T = nc.dram_tensor("w1T", [D, H], F32, kind="ExternalInput").ap()
    w3T = nc.dram_tensor("w3T", [D, H], F32, kind="ExternalInput").ap()
    w2T = nc.dram_tensor("w2T", [H, D], F32, kind="ExternalInput").ap()
    ident = nc.dram_tensor("ident", [128, 128], F32R, kind="ExternalInput").ap()
    sel32 = nc.dram_tensor("sel32", [32, 32 * 128], BF16, kind="ExternalInput").ap()
    ones1 = nc.dram_tensor("ones1", [1, 128], F32, kind="ExternalInput").ap()
    onescf = nc.dram_tensor("onescf", [128, 1], F32, kind="ExternalInput").ap()
    consts = nc.dram_tensor("consts", [128, 8], F32, kind="ExternalInput").ap()
    outT = nc.dram_tensor("outT", [D, LC], F32, kind="ExternalOutput").ap()
    hfh = nc.dram_tensor("hfh", [128, HT * LC], F32, kind="Internal").ap()
    x1h = nc.dram_tensor("x1h", [128, NA * LC], F32, kind="Internal").ap()
    if debug:
        dbgY = nc.dram_tensor("dbgY", [128, NA * LC], F32, kind="ExternalOutput").ap()
        dbgSW = nc.dram_tensor("dbgSW", [128, 16], F32, kind="ExternalOutput").ap()
        dbgHF = nc.dram_tensor("dbgHF", [128, HT * LC], F32, kind="ExternalOutput").ap()
        dbgX1 = nc.dram_tensor("dbgX1", [128, NA * LC], F32, kind="ExternalOutput").ap()
        dbgHIN = nc.dram_tensor("dbgHIN", [128, 256], F32, kind="ExternalOutput").ap()
        dbgSTT = nc.dram_tensor("dbgSTT", [128, 256], F32, kind="ExternalOutput").ap()
        dbgM = nc.dram_tensor("dbgM", [128, 6 * LC], F32, kind="ExternalOutput").ap()

    with ExitStack() as ctx:
        nc.ctx = ctx
        A = E(nc.sbuf_tensor("A", [128, NA * LC], F32))
        B = E(nc.sbuf_tensor("B", [128, NA * LC], F32))
        C = E(nc.sbuf_tensor("C", [128, 10 * LC], F32))
        ZZT = E(nc.sbuf_tensor("ZZT", [128, 2 * LC], F32))
        MGW = E(nc.sbuf_tensor("MGW", [128, 1024], F32))
        STTB = E(nc.sbuf_tensor("STTB", [128, 256], F32))
        HINB = E(nc.sbuf_tensor("HINB", [128, 256], F32))
        DUMB = E(nc.sbuf_tensor("DUMB", [128, 256], F32))
        ZB = E(nc.sbuf_tensor("ZB", [128, 128], F32))
        SWB = E(nc.sbuf_tensor("SWB", [128, 16], F32))
        ABSACC = E(nc.sbuf_tensor("ABSACC", [128, 296], F32))
        AT = E(nc.sbuf_tensor("ATT", [128, 256], F32))
        WXPS = E(nc.sbuf_tensor("WXPS", [128, 2 * E_XD], F32))
        WDTS = E(nc.sbuf_tensor("WDTS", [64, 256], F32))
        BDT = E(nc.sbuf_tensor("BDT", [128, 16], F32))
        DP = E(nc.sbuf_tensor("DP", [128, 16], F32))
        CON = E(nc.sbuf_tensor("CON", [128, 8], F32))
        ONES = E(nc.sbuf_tensor("ONES", [1, 128], F32))
        ONESCF = E(nc.sbuf_tensor("ONESCF", [128, 1], F32))
        IDENT = E(nc.sbuf_tensor("IDENT", [128, 128], F32R))
        SELX = E(nc.sbuf_tensor("SELX", [32, 2048], F32))
        ROWS = E(nc.sbuf_tensor("ROWS", [1, 2 * LC], F32))
        ps1 = E(nc.psum_tensor("ps1", [128, LC], F32))
        ps2 = E(nc.psum_tensor("ps2", [128, LC], F32))
        ps3 = E(nc.psum_tensor("ps3", [128, LC], F32))
        ps4 = E(nc.psum_tensor("ps4", [128, LC], F32))
        PS = [ps1, ps2, ps3, ps4]

        SEL = SELX[:, 0:2048].bitcast(BF16)
        Ab = A.bitcast(BF16)
        Bb = B.bitcast(BF16)

        P = Prog(nc)
        extra = ["RSC%d" % k for k in range(16)] + ["CL%d" % k for k in range(17)] + ["CLZ%d" % k for k in range(4)] + ["RSD", "PREP"]
        P.sem_names += extra
        for s_ in extra:
            P.count[s_] = 0
        P.sem_ctx = {name: E(nc.semaphore(name)) for name in P.sem_names}
        SEM = P.sem_ctx

        Bu = TBuf

        def asl_(a):
            return slice(a * LC, (a + 1) * LC)

        def cs(i):
            return C[:, i * LC:(i + 1) * LC]

        def rs(i):
            return ROWS[0:1, i * LC:(i + 1) * LC]

        bAs = [Bu() for _ in range(NA)]
        bBs = [Bu() for _ in range(NA)]
        bCs = [Bu() for _ in range(10)]
        bPS = [Bu() for _ in range(4)]
        bZZ = [Bu(), Bu()]
        bROWS = Bu()
        bSWB, bABS, bAT, bWXPS, bWDTS, bBDT, bDP, bCON = (Bu() for _ in range(8))
        bONES, bONESCF, bIDENT, bSEL, bSTT, bZB = (Bu() for _ in range(6))
        bMGW, bXQ, bHQ, bHFH, bX1H = (Bu() for _ in range(5))

        bCON.setw(P.dma("y", lambda e, n: e.dma_start(out=CON[:], in_=consts[:])))
        bONES.setw(P.dma("y", lambda e, n: e.dma_start(out=ONES[:], in_=ones1[:])))
        bONESCF.setw(P.dma("y", lambda e, n: e.dma_start(out=ONESCF[:], in_=onescf[:])))
        bIDENT.setw(P.dma("y", lambda e, n: e.dma_start(out=IDENT[:], in_=ident[:])))
        bSEL.setw(P.dma("y", lambda e, n: e.dma_start(out=SEL[:], in_=sel32[:])))
        bBDT.setw(P.dma("y", lambda e, n: e.dma_start(out=BDT[:], in_=bdt[:])))
        bDP.setw(P.dma("y", lambda e, n: e.dma_start(out=DP[:], in_=dpar[:])))
        tal = P.dma("y", lambda e, n: e.dma_start(out=AT[:], in_=alog[:]))
        t1 = P.op("s", lambda e, n: e.activation(out=AT[:], in_=AT[:], func=AF.Exp), deps=[tal])
        t2 = P.op("v", lambda e, n: e.tensor_scalar(out=AT[:], in0=AT[:], scalar1=-1.0, scalar2=None,
                  op0=ALU.mult), deps=[t1])
        bAT.setw(t2)
        tzb = P.op("v", lambda e, n: e.memset(ZB[:], 0.0))
        bZB.setw(tzb)
        tswz = P.op("v", lambda e, n: e.memset(SWB[:, 6:7], 1536.0))
        bSWB.setw(tswz)

        def emit_carry_switch(k4):
            lsems = ["CL%d" % (k4 * 4 + kk) for kk in range(4)]
            def carry_em(e, n, ls=tuple(lsems), k4=k4):
                pid = e.partition_id()
                for r in n.Switch(engines=[e], index=[pid], n=8):
                    dst = (r + 1) % 8
                    dlt = r ^ dst
                    if dlt & 4:
                        dlt ^= 2
                    for kk in range(4):
                        k = k4 * 4 + kk
                        dests = [None] * 8
                        dests[dlt] = (0, dlt)
                        if r < 7:
                            e.remote_dma_broadcast(
                                out_ap=HINB[:, 16 * k:16 * (k + 1)],
                                in_ap=STTB[:, 16 * k:16 * (k + 1)],
                                remote_sem=SEM["RSC%d" % k], local_sem=SEM[ls[kk]],
                                rdests=dests).then_inc(SEM["PREP"], 1)
                        else:
                            e.remote_dma_broadcast(
                                out_ap=DUMB[:, 16 * k:16 * (k + 1)],
                                in_ap=STTB[:, 16 * k:16 * (k + 1)],
                                remote_sem=SEM["RSD"], local_sem=SEM[ls[kk]],
                                rdests=dests).then_inc(SEM["PREP"], 1)
            P.ops["g"].append(("r", carry_em))
            P.count["PREP"] += 4
            for sem in lsems:
                P.count[sem] += 16
            P._wait("g", [("PREP", P.count["PREP"])])

        abs_jobs = []
        for m, wsrc in enumerate((w1T, w3T, w2T)):
            if m < 2:
                flat = wsrc.rearrange("(a p) h -> p a h", p=128)
                nblk, ncol = NA, H
            else:
                flat = wsrc.rearrange("(b p) d -> p b d", p=128)
                nblk, ncol = HT, D
            ji = 0
            for blk in range(nblk):
                c0 = 0
                while c0 < ncol:
                    w = min(1024, ncol - c0)
                    abs_jobs.append((m, flat, blk, c0, w, m * 96 + ji))
                    ji += 1
                    c0 += w
        abs_state = {"i": 0, "buf": 0}
        bWST = [Bu(), Bu()]

        def emit_abs(k, tile_cols, base):
            for _ in range(k):
                i = abs_state["i"]
                if i >= len(abs_jobs):
                    return
                m, flat, blk, c0, w, col = abs_jobs[i]
                abs_state["i"] += 1
                nsub = max(1, w // tile_cols)
                for s_i in range(nsub):
                    sw = min(tile_cols, w)
                    buf = abs_state["buf"]
                    abs_state["buf"] ^= 1
                    wst = C[:, base + buf * tile_cols: base + buf * tile_cols + sw]
                    q = "s" if (i % 2 == 0) else "y"
                    tok = P.dma(q, lambda e, n, flat=flat, blk=blk, c0=c0 + s_i * tile_cols, w=sw, wst=wst:
                                e.dma_start(out=wst, in_=flat[:, blk, c0:c0 + w]),
                                deps=bWST[buf].wdeps())
                    bWST[buf].setw(tok)
                    acc_ap = ABSACC[:, col:col + 1] if s_i == 0 else ABSACC[:, 288:289]
                    tok = P.op("s", lambda e, n, wst=wst, acc_ap=acc_ap:
                               e.activation(out=wst, in_=wst, func=AF.Abs, accum_out=acc_ap),
                               deps=bWST[buf].rdeps() + bWST[buf].wdeps() + bABS.wdeps())
                    bWST[buf].setw(tok)
                    bABS.setw(tok)
                    if s_i > 0:
                        tok = P.op("v", lambda e, n, col=col: e.tensor_tensor(
                            out=ABSACC[:, col:col + 1], in0=ABSACC[:, col:col + 1],
                            in1=ABSACC[:, 288:289], op=ALU.add),
                            deps=bABS.rdeps() + bABS.wdeps())
                        bABS.setw(tok)

        for a in range(NA):
            tok = P.dma("y", lambda e, n, a=a: e.dma_start(
                out=B[:, asl_(a)], in_=xTf[a * 128:(a + 1) * 128, :]),
                deps=bBs[a].wdeps())
            bBs[a].setw(tok)
            sq = cs(2 + (a % 2))
            for h in range(NHF):
                hsl = slice(h * FH, (h + 1) * FH)
                tm = P.op("t", lambda e, n, a=a, hsl=hsl: e.matmul(
                    out=ps4[0:1, hsl], lhsT=ONESCF[:], rhs=B[:, asl_(a)][:, hsl],
                    start=(a == 0), stop=(a == NA - 1)),
                    deps=bBs[a].rdeps() + bONESCF.rdeps() + (bPS[3].wdeps() if a == 0 else []))
                bBs[a].addr(tm)
                if a == NA - 1:
                    bPS[3].setw(tm)
            tsq = P.op("s", lambda e, n, a=a, sq=sq: e.activation(out=sq, in_=B[:, asl_(a)],
                       func=AF.Square), deps=bBs[a].rdeps() + bCs[2 + (a % 2)].wdeps())
            bCs[2 + (a % 2)].setw(tsq)
            bBs[a].addr(tsq)
            for h in range(NHF):
                hsl = slice(h * FH, (h + 1) * FH)
                tq = P.op("t", lambda e, n, a=a, hsl=hsl, sq=sq: e.matmul(
                    out=ps3[0:1, hsl], lhsT=ONESCF[:], rhs=sq[:, hsl],
                    start=(a == 0), stop=(a == NA - 1)),
                    deps=bCs[2 + (a % 2)].rdeps() + bONESCF.rdeps() + (bPS[2].wdeps() if a == 0 else []))
                bCs[2 + (a % 2)].addr(tq)
                if a == NA - 1:
                    bPS[2].setw(tq)
            emit_abs(4, LC, 8 * LC)

        def ln_rows(psA, psB2, bpsA, bpsB, scale):
            tm = P.op("v", lambda e, n: e.tensor_scalar(out=rs(0), in0=psA[0:1, :], scalar1=scale,
                      scalar2=None, op0=ALU.mult), deps=bpsA.rdeps() + bROWS.wdeps())
            bROWS.setw(tm); bpsA.addr(tm)
            tsq = P.op("v", lambda e, n: e.tensor_scalar(out=rs(1), in0=psB2[0:1, :], scalar1=scale,
                       scalar2=None, op0=ALU.mult), deps=bpsB.rdeps() + bROWS.wdeps())
            bROWS.setw(tsq); bpsB.addr(tsq)
            tm2 = P.op("s", lambda e, n, psA=psA: e.activation(out=psA[0:1, :], in_=rs(0), func=AF.Square),
                       deps=[tm] + bROWS.wdeps() + bpsA.wdeps())
            bpsA.setw(tm2)
            tv = P.op("v", lambda e, n, psA=psA: e.tensor_tensor(out=rs(1), in0=rs(1), in1=psA[0:1, :],
                      op=ALU.subtract), deps=[tsq, tm2] + bROWS.wdeps())
            bROWS.setw(tv); bpsA.addr(tv)
            tsr = P.op("s", lambda e, n: e.activation(out=rs(1), in_=rs(1), func=AF.Sqrt, bias=CON[0:1, 0:1]),
                       deps=[tv] + bCON.rdeps() + bROWS.wdeps())
            bROWS.setw(tsr)
            trs = P.op("v", lambda e, n: e.reciprocal(out=rs(1), in_=rs(1)), deps=[tsr] + bROWS.wdeps())
            bROWS.setw(trs)
            tmr = P.op("v", lambda e, n: e.tensor_tensor(out=rs(0), in0=rs(0), in1=rs(1), op=ALU.mult),
                       deps=[trs, tm] + bROWS.wdeps())
            bROWS.setw(tmr)
            return trs, tmr

        def bcast(row_ap, psOUT, bpsOUT, deps):
            toks = []
            for h in range(NHF):
                hsl = slice(h * FH, (h + 1) * FH)
                tok = P.op("t", lambda e, n, hsl=hsl, row_ap=row_ap, psOUT=psOUT: e.matmul(
                    out=psOUT[:, hsl], lhsT=ONES[:], rhs=row_ap[0:1, hsl], start=True, stop=True),
                    deps=deps + bONES.rdeps() + bpsOUT.wdeps())
                toks.append(tok)
            bpsOUT.setw(toks[-1])
            bROWS.addr(toks[-1])
            return toks[-1]

        trs, tmr = ln_rows(ps4, ps3, bPS[3], bPS[2], 1.0 / D)
        tb1 = bcast(rs(1), ps1, bPS[0], [trs])
        tb2 = bcast(rs(0), ps2, bPS[1], [tmr])
        twxp = P.dma("y", lambda e, n: e.dma_start(
            out=WXPS[:].rearrange("p (a c) -> p a c", a=2),
            in_=wxp.rearrange("(a p) c -> p a c", p=128)[:, 0:2, :]), deps=bWXPS.wdeps())
        bWXPS.setw(twxp)
        for a in range(NA):
            t1_ = P.op("v", lambda e, n, a=a: e.tensor_tensor(out=A[:, asl_(a)], in0=B[:, asl_(a)],
                       in1=ps1[:], op=ALU.mult),
                       deps=bBs[a].rdeps() + [tb1] + bAs[a].wdeps())
            bAs[a].setw(t1_); bBs[a].addr(t1_); bPS[0].addr(t1_)
            t2_ = P.op("v", lambda e, n, a=a: e.tensor_tensor(out=A[:, asl_(a)], in0=A[:, asl_(a)],
                       in1=ps2[:], op=ALU.subtract), deps=[t1_, tb2] + bAs[a].wdeps())
            bAs[a].setw(t2_); bPS[1].addr(t2_)
            for h in range(NHF):
                hsl = slice(h * FH, (h + 1) * FH)
                tok = P.op("t", lambda e, n, a=a, hsl=hsl: e.matmul(
                    out=ps4[0:E_XD, hsl], lhsT=WXPS[:, (a % 2) * E_XD:(a % 2 + 1) * E_XD],
                    rhs=A[:, asl_(a)][:, hsl],
                    start=(a == 0), stop=(a == NA - 1)),
                    deps=bAs[a].rdeps() + bWXPS.rdeps() + (bPS[3].wdeps() if a == 0 else []))
                bAs[a].addr(tok); bWXPS.addr(tok)
                if a == NA - 1:
                    bPS[3].setw(tok)
            if a % 2 == 1 and a < NA - 1:
                tok = P.dma("y", lambda e, n, a=a: e.dma_start(
                    out=WXPS[:].rearrange("p (a c) -> p a c", a=2),
                    in_=wxp.rearrange("(a p) c -> p a c", p=128)[:, a + 1:a + 3, :]),
                    deps=bWXPS.wdeps())
                bWXPS.setw(tok)
        tok = P.op("s", lambda e, n: e.activation(out=cs(0)[0:E_XD, :], in_=ps4[0:E_XD, :], func=AF.Copy),
                   deps=bPS[3].rdeps() + bCs[0].wdeps())
        bCs[0].setw(tok); bPS[3].addr(tok)
        tok = P.dma("g", lambda e, n: e.dma_start(out=cs(3)[0:32, :], in_=cs(0)[R_LO:E_XD, :]),
                    deps=bCs[0].rdeps() + bCs[3].wdeps())
        bCs[3].setw(tok); bCs[0].addr(tok)
        XBC = cs(3)[0:32, 0:LC]
        XBH = C[0:32, LC + 0:LC + LC // 2].bitcast(BF16)
        XBL = C[0:32, LC + LC // 2:2 * LC].bitcast(BF16)
        tok = P.op("v", lambda e, n: e.tensor_copy(out=XBH, in_=XBC),
                   deps=bCs[3].rdeps() + bCs[1].wdeps())
        bCs[1].setw(tok); bCs[3].addr(tok)
        tok = P.op("v", lambda e, n: e.tensor_tensor(out=XBL, in0=XBC, in1=XBH, op=ALU.subtract),
                   deps=bCs[3].rdeps() + bCs[1].rdeps() + bCs[1].wdeps())
        bCs[1].setw(tok); bCs[3].addr(tok)
        if debug:
            tokd = P.dma("g", lambda e, n: e.dma_start(out=dbgM[0:E_XD, 0:LC], in_=cs(0)[0:E_XD, :]),
                         deps=bCs[0].rdeps())
            bCs[0].addr(tokd)

        def emit_zero_switch(half):
            P._wait("g", bZB.rdeps())

            def zem(e, n, half=half):
                pid = e.partition_id()
                for r in n.Switch(engines=[e], index=[pid], n=8):
                    dests = [None] * 8
                    dests[0] = (0, 0)
                    for k in range(half * 8, half * 8 + 8):
                        e.remote_dma_broadcast(
                            out_ap=(HINB if r == 0 else DUMB)[:, 16 * k:16 * (k + 1)],
                            in_ap=ZB[:, 0:16],
                            remote_sem=SEM[("RSC%d" % k) if r == 0 else "RSD"],
                            local_sem=SEM["CLZ%d" % (2 * half + (k % 2))],
                            rdests=dests).then_inc(SEM["PREP"], 1)
            P.ops["g"].append(("r", zem))
            P.count["PREP"] += 8
            P._wait("g", [("PREP", P.count["PREP"])])
            P.ops["g"].append(("r", lambda e, n: e.trigger_dma(count=8)))

        emit_zero_switch(0)
        emit_zero_switch(1)
        for b_ in (bCs[8], bCs[9]):
            b_.rtoks += bWST[0].wdeps() + bWST[1].wdeps()
        pend_y = []
        for dt in range(NA):
            if dt % 4 == 0:
                emit_carry_switch(dt // 4)
            dbuf = dt % 2
            tok = P.dma("y", lambda e, n, dt=dt, dbuf=dbuf: e.dma_start(
                out=WDTS[:, dbuf * 128:(dbuf + 1) * 128], in_=wdt[:, dt * 128:(dt + 1) * 128]),
                deps=bWDTS.wdeps())
            bWDTS.setw(tok)
            for h in range(NHF):
                hsl = slice(h * FH, (h + 1) * FH)
                tok = P.op("t", lambda e, n, dbuf=dbuf, hsl=hsl: e.matmul(
                    out=ps4[:, hsl], lhsT=WDTS[:, dbuf * 128:(dbuf + 1) * 128],
                    rhs=cs(0)[0:R_LO, hsl], start=True, stop=True),
                    deps=bWDTS.rdeps() + bCs[0].rdeps() + bPS[3].wdeps())
                bPS[3].setw(tok); bWDTS.addr(tok); bCs[0].addr(tok)
            tok = P.op("s", lambda e, n, dt=dt: e.activation(out=cs(4), in_=ps4[:], func=AF.Abs,
                       bias=BDT[:, dt:dt + 1]), deps=bPS[3].rdeps() + bBDT.rdeps() + bCs[4].wdeps())
            bCs[4].setw(tok); bPS[3].addr(tok)
            tok = P.op("s", lambda e, n: e.activation(out=cs(5), in_=cs(4), func=AF.Exp, scale=-1.0),
                       deps=bCs[4].rdeps() + bCs[5].wdeps())
            bCs[5].setw(tok); bCs[4].addr(tok)
            tok = P.op("s", lambda e, n: e.activation(out=cs(4), in_=cs(5), func=AF.Ln, bias=CON[0:128, 1:2]),
                       deps=bCs[5].rdeps() + bCs[4].wdeps() + bCON.rdeps())
            bCs[4].setw(tok); bCs[5].addr(tok)
            tok = P.op("s", lambda e, n, dt=dt: e.activation(out=cs(5), in_=ps4[:], func=AF.Relu,
                       bias=BDT[:, dt:dt + 1]), deps=bPS[3].rdeps() + bBDT.rdeps() + bCs[5].wdeps())
            bCs[5].setw(tok); bPS[3].addr(tok)
            tok = P.op("v", lambda e, n: e.tensor_tensor(out=cs(2), in0=cs(4), in1=cs(5), op=ALU.add),
                       deps=bCs[4].rdeps() + bCs[5].rdeps() + bCs[2].wdeps())
            bCs[2].setw(tok); bCs[4].addr(tok); bCs[5].addr(tok)
            tok = P.op("v", lambda e, n, dt=dt: e.tensor_tensor(out=cs(3), in0=cs(2), in1=A[:, asl_(dt)],
                       op=ALU.mult), deps=bCs[2].rdeps() + bAs[dt].rdeps() + bCs[3].wdeps())
            bCs[3].setw(tok); bCs[2].addr(tok); bAs[dt].addr(tok)
            if debug and dt == 0:
                tokd = P.dma("g", lambda e, n: e.dma_start(out=dbgM[:, LC:2 * LC], in_=cs(2)),
                             deps=bCs[2].rdeps())
                bCs[2].addr(tokd)
                tokd = P.dma("g", lambda e, n: e.dma_start(out=dbgM[:, 2 * LC:3 * LC], in_=cs(3)),
                             deps=bCs[3].rdeps())
                bCs[3].addr(tokd)
            emit_abs(13, LC // 2, 9 * LC)
            for nn in range(NST):
                par = nn % 2
                psBC = PS[par]
                hh = 6 + (nn % 2)
                da = 4 + (nn % 2)
                for h in range(NHF):
                    hsl = slice(h * FH, (h + 1) * FH)
                    for wv, first in ((XBH, True), (XBL, False)):
                        tok = P.op("t", lambda e, n, nn=nn, hsl=hsl, wv=wv, first=first, psBC=psBC: e.matmul(
                            out=psBC[:, hsl], lhsT=SEL[:, nn * 128:(nn + 1) * 128], rhs=wv[:, hsl],
                            start=first, stop=not first),
                            deps=bCs[1].rdeps() + bSEL.rdeps() + (bPS[par].wdeps() if first and h == 0 else []))
                        if not first:
                            bPS[par].setw(tok)
                        bCs[1].addr(tok); bSEL.addr(tok)
                tok = P.op("s", lambda e, n, dt=dt, nn=nn, da=da: e.activation(
                    out=cs(da), in_=cs(2), func=AF.Exp, scale=AT[:, dt * 16 + nn:dt * 16 + nn + 1]),
                    deps=bCs[2].rdeps() + bAT.rdeps() + bCs[da].wdeps())
                bCs[da].setw(tok); bCs[2].addr(tok)
                for h in range(NHF):
                    hsl = slice(h * FH, (h + 1) * FH)
                    for wv, first in ((XBH, True), (XBL, False)):
                        tok = P.op("t", lambda e, n, nn=nn, hsl=hsl, wv=wv, first=first: e.matmul(
                            out=ps4[:, hsl], lhsT=SEL[:, (NST + nn) * 128:(NST + nn + 1) * 128],
                            rhs=wv[:, hsl], start=first, stop=not first),
                            deps=bCs[1].rdeps() + bSEL.rdeps() + (bPS[3].wdeps() if first and h == 0 else []))
                        if not first:
                            bPS[3].setw(tok)
                        bCs[1].addr(tok); bSEL.addr(tok)
                tok = P.op("v", lambda e, n, psBC=psBC: e.tensor_tensor(out=cs(8), in0=cs(3), in1=psBC[:],
                           op=ALU.mult), deps=bCs[3].rdeps() + bPS[par].rdeps() + bCs[8].wdeps())
                bCs[8].setw(tok); bCs[3].addr(tok); bPS[par].addr(tok)
                if debug and dt == 0 and nn == 0:
                    tokd = P.dma("g", lambda e, n, da=da: e.dma_start(out=dbgM[:, 3 * LC:4 * LC], in_=cs(da)),
                                 deps=bCs[da].rdeps())
                    bCs[da].addr(tokd)
                    tokd = P.dma("g", lambda e, n: e.dma_start(out=dbgM[:, 4 * LC:5 * LC], in_=cs(8)),
                                 deps=bCs[8].rdeps())
                    bCs[8].addr(tokd)
                scdeps = bCs[da].rdeps() + bCs[8].rdeps() + bCs[hh].wdeps()
                if nn == 0:
                    scdeps = scdeps + [("RSC%d" % dt, 2)]
                tok = P.op("v", lambda e, n, dt=dt, nn=nn, da=da, hh=hh: e.tensor_tensor_scan(
                    out=cs(hh), data0=cs(da), data1=cs(8),
                    initial=HINB[:, dt * 16 + nn:dt * 16 + nn + 1],
                    op0=ALU.mult, op1=ALU.add), deps=scdeps)
                bCs[hh].setw(tok); bCs[da].addr(tok); bCs[8].addr(tok)
                if debug and dt == 0 and nn == 0:
                    tokd = P.dma("g", lambda e, n, hh=hh: e.dma_start(out=dbgM[:, 5 * LC:6 * LC], in_=cs(hh)),
                                 deps=bCs[hh].rdeps())
                    bCs[hh].addr(tokd)
                tok = P.op("v", lambda e, n, dt=dt, nn=nn, hh=hh: e.tensor_copy(
                    out=STTB[:, dt * 16 + nn:dt * 16 + nn + 1], in_=cs(hh)[:, LC - 1:LC]),
                    deps=bCs[hh].rdeps() + bSTT.wdeps())
                bSTT.setw(tok); bCs[hh].addr(tok)
                zz = ZZT[:, par * LC:(par + 1) * LC]
                tok = P.op("v", lambda e, n, hh=hh, zz=zz: e.tensor_tensor(
                    out=zz.bitcast(F32R), in0=cs(hh), in1=ps4[:], op=ALU.mult),
                    deps=bCs[hh].rdeps() + bPS[3].rdeps() + bZZ[par].wdeps())
                bZZ[par].setw(tok); bCs[hh].addr(tok); bPS[3].addr(tok)
                def y_emit(nn=nn, par=par, zz=zz):
                    for h in range(NHF):
                        hsl = slice(h * FH, (h + 1) * FH)
                        tok = P.op("t", lambda e, n, hsl=hsl, nn=nn, zz=zz: e.matmul(
                            out=ps3[:, hsl], lhsT=IDENT[:], rhs=zz.bitcast(F32R)[:, hsl],
                            start=(nn == 0), stop=(nn == NST - 1)),
                            deps=bZZ[par].rdeps() + bIDENT.rdeps() + (bPS[2].wdeps() if nn == 0 and h == 0 else []))
                        bZZ[par].addr(tok)
                        if nn == NST - 1:
                            bPS[2].setw(tok)
                pend_y.append(y_emit)
                if len(pend_y) > 1:
                    pend_y.pop(0)()
            while pend_y:
                pend_y.pop(0)()
            tok = P.op("v", lambda e, n, dt=dt: e.scalar_tensor_tensor(
                out=A[:, asl_(dt)], in0=A[:, asl_(dt)], scalar=DP[:, dt:dt + 1], in1=ps3[:],
                op0=ALU.mult, op1=ALU.add),
                deps=bAs[dt].rdeps() + bAs[dt].wdeps() + bDP.rdeps() + bPS[2].rdeps())
            bAs[dt].setw(tok); bPS[2].addr(tok)
            trig_deps = bSTT.rdeps() + bSTT.wdeps()
            P._wait("g", trig_deps)
            P.ops["g"].append(("r", lambda e, n: e.trigger_dma(count=1)))

        if debug:
            tokd = P.dma("g", lambda e, n: e.dma_start(out=dbgY[:], in_=A[:]),
                         deps=[t for b in bAs for t in b.rdeps()])
            for b in bAs:
                b.addr(tokd)
            tokd = P.dma("g", lambda e, n: e.dma_start(out=dbgHIN[:], in_=HINB[:]),
                         deps=bSTT.rdeps() + bSTT.wdeps())
            tokd = P.dma("g", lambda e, n: e.dma_start(out=dbgSTT[:], in_=STTB[:]),
                         deps=bSTT.rdeps() + bSTT.wdeps())

        bWST[0].rtoks += bCs[8].wdeps()
        bWST[1].rtoks += bCs[8].wdeps()
        emit_abs(len(abs_jobs), LC, 8 * LC)
        for b_ in (bCs[8], bCs[9]):
            b_.rtoks += bWST[0].wdeps() + bWST[1].wdeps()
        tred = []
        for m in range(3):
            nj = 88 if m == 2 else 96
            tok = P.op("s", lambda e, n, m=m, nj=nj: e.activation(out=C[:, 4 * LC:4 * LC + nj], in_=ABSACC[:, m * 96:m * 96 + nj],
                       func=AF.Abs, accum_out=ABSACC[:, 290 + m:291 + m]),
                       deps=bABS.rdeps() + bABS.wdeps() + bCs[4].wdeps())
            bABS.setw(tok); bCs[4].setw(tok)
            tok = P.op("t", lambda e, n, m=m: e.matmul(out=ps4[0:1, m:m + 1], lhsT=ABSACC[:, 290 + m:291 + m],
                       rhs=ONESCF[:], start=True, stop=True),
                       deps=bABS.rdeps() + bONESCF.rdeps() + (bPS[3].wdeps() if m == 0 else []))
            bABS.addr(tok)
            tred.append(tok)
        bPS[3].setw(tred[-1])
        NW = float(D) * H
        tok = P.op("v", lambda e, n: e.tensor_scalar(out=ROWS[0:1, 0:3], in0=ps4[0:1, 0:3],
                   scalar1=1.0 / NW, scalar2=1e-5, op0=ALU.mult, op1=ALU.max),
                   deps=tred + bROWS.wdeps())
        bROWS.setw(tok); bPS[3].addr(tok)
        tok = P.op("v", lambda e, n: e.reciprocal(out=ROWS[0:1, 3:6], in_=ROWS[0:1, 0:3]),
                   deps=bROWS.rdeps() + bROWS.wdeps())
        bROWS.setw(tok)
        tok = P.op("t", lambda e, n: e.matmul(out=ps4[:, 0:6], lhsT=ONES[:], rhs=ROWS[0:1, 0:6],
                   start=True, stop=True), deps=bROWS.rdeps() + bONES.rdeps() + bPS[3].wdeps())
        bPS[3].setw(tok); bROWS.addr(tok)
        tok = P.op("s", lambda e, n: e.activation(out=SWB[:, 0:6], in_=ps4[:, 0:6], func=AF.Copy),
                   deps=bPS[3].rdeps() + bSWB.wdeps() + bSWB.rdeps())
        bSWB.setw(tok); bPS[3].addr(tok)
        if debug:
            tokd = P.dma("g", lambda e, n: e.dma_start(out=dbgSW[:, 0:7], in_=SWB[:, 0:7]), deps=bSWB.rdeps() + [tok])
            bSWB.addr(tokd)

        def swmean(m):
            return SWB[:, m:m + 1]

        def swinv(m):
            return SWB[:, 3 + m:4 + m]

        B1536 = SWB[:, 6:7]

        trs, tmr = None, None
        for a in range(NA):
            sq = cs(4)
            for h in range(NHF):
                hsl = slice(h * FH, (h + 1) * FH)
                tm = P.op("t", lambda e, n, a=a, hsl=hsl: e.matmul(
                    out=ps4[0:1, hsl], lhsT=ONESCF[:], rhs=A[:, asl_(a)][:, hsl],
                    start=(a == 0), stop=(a == NA - 1)),
                    deps=bAs[a].rdeps() + bONESCF.rdeps() + (bPS[3].wdeps() if a == 0 else []))
                bAs[a].addr(tm)
                if a == NA - 1:
                    bPS[3].setw(tm)
            tsq = P.op("s", lambda e, n, a=a, sq=sq: e.activation(out=sq, in_=A[:, asl_(a)],
                       func=AF.Square), deps=bAs[a].rdeps() + bCs[4].wdeps())
            bCs[4].setw(tsq); bAs[a].addr(tsq)
            for h in range(NHF):
                hsl = slice(h * FH, (h + 1) * FH)
                tq = P.op("t", lambda e, n, hsl=hsl, a=a, sq=sq: e.matmul(
                    out=ps3[0:1, hsl], lhsT=ONESCF[:], rhs=sq[:, hsl],
                    start=(a == 0), stop=(a == NA - 1)),
                    deps=bCs[4].rdeps() + bONESCF.rdeps() + (bPS[2].wdeps() if a == 0 else []))
                bCs[4].addr(tq)
                if a == NA - 1:
                    bPS[2].setw(tq)
        trs, tmr = ln_rows(ps4, ps3, bPS[3], bPS[2], 1.0 / D)
        tb1 = bcast(rs(1), ps1, bPS[0], [trs])
        tb2 = bcast(rs(0), ps2, bPS[1], [tmr])
        x1toks = []
        for a in range(NA):
            t1_ = P.op("v", lambda e, n, a=a: e.tensor_tensor(out=cs(5), in0=A[:, asl_(a)], in1=ps1[:],
                       op=ALU.mult), deps=bAs[a].rdeps() + [tb1] + bCs[5].wdeps())
            bCs[5].setw(t1_); bAs[a].addr(t1_); bPS[0].addr(t1_)
            t2_ = P.op("v", lambda e, n: e.tensor_tensor(out=cs(5), in0=cs(5), in1=ps2[:],
                       op=ALU.subtract), deps=bCs[5].rdeps() + [tb2] + bCs[5].wdeps())
            bCs[5].setw(t2_); bPS[1].addr(t2_)
            t3_ = P.op("v", lambda e, n, a=a: e.tensor_tensor(out=B[:, asl_(a)], in0=cs(5), in1=B[:, asl_(a)],
                       op=ALU.add), deps=bCs[5].rdeps() + bBs[a].rdeps() + bBs[a].wdeps())
            bBs[a].setw(t3_); bCs[5].addr(t3_)
            tok = P.dma("g", lambda e, n, a=a: e.dma_start(out=x1h[:, asl_(a)], in_=B[:, asl_(a)]),
                        deps=bBs[a].rdeps() + bX1H.wdeps())
            bBs[a].addr(tok); bX1H.setw(tok); x1toks.append(tok)
            sq = cs(4)
            for h in range(NHF):
                hsl = slice(h * FH, (h + 1) * FH)
                tm = P.op("t", lambda e, n, a=a, hsl=hsl: e.matmul(
                    out=ps4[0:1, hsl], lhsT=ONESCF[:], rhs=B[:, asl_(a)][:, hsl],
                    start=(a == 0), stop=(a == NA - 1)),
                    deps=bBs[a].rdeps() + bONESCF.rdeps() + (bPS[3].wdeps() if a == 0 else []))
                bBs[a].addr(tm)
                if a == NA - 1:
                    bPS[3].setw(tm)
            tsq = P.op("s", lambda e, n, a=a, sq=sq: e.activation(out=sq, in_=B[:, asl_(a)],
                       func=AF.Square), deps=bBs[a].rdeps() + bCs[4].wdeps())
            bCs[4].setw(tsq); bBs[a].addr(tsq)
            for h in range(NHF):
                hsl = slice(h * FH, (h + 1) * FH)
                tq = P.op("t", lambda e, n, hsl=hsl, a=a, sq=sq: e.matmul(
                    out=ps3[0:1, hsl], lhsT=ONESCF[:], rhs=sq[:, hsl],
                    start=(a == 0), stop=(a == NA - 1)),
                    deps=bCs[4].rdeps() + bONESCF.rdeps() + (bPS[2].wdeps() if a == 0 else []))
                bCs[4].addr(tq)
                if a == NA - 1:
                    bPS[2].setw(tq)
        if debug:
            tokd = P.dma("g", lambda e, n: e.dma_start(out=dbgX1[:], in_=x1h[:]), deps=x1toks)
            bX1H.addr(tokd)
        trs, tmr = ln_rows(ps4, ps3, bPS[3], bPS[2], 1.0 / D)
        tb1 = bcast(rs(1), ps1, bPS[0], [trs])
        tb2 = bcast(rs(0), ps2, bPS[1], [tmr])
        for a in range(NA):
            t1_ = P.op("v", lambda e, n, a=a: e.tensor_tensor(out=A[:, asl_(a)], in0=B[:, asl_(a)], in1=ps1[:],
                       op=ALU.mult), deps=bBs[a].rdeps() + [tb1] + bAs[a].wdeps())
            bAs[a].setw(t1_); bBs[a].addr(t1_); bPS[0].addr(t1_)
            t2_ = P.op("v", lambda e, n, a=a: e.tensor_tensor(out=A[:, asl_(a)], in0=A[:, asl_(a)], in1=ps2[:],
                       op=ALU.subtract), deps=bAs[a].rdeps() + [tb2] + bAs[a].wdeps())
            bAs[a].setw(t2_); bPS[1].addr(t2_)
            if a == 0:
                tok = P.op("v", lambda e, n: e.tensor_copy(out=cs(6), in_=A[:, asl_(0)]),
                           deps=bAs[0].rdeps() + bCs[6].wdeps())
                bCs[6].setw(tok); bAs[0].addr(tok)
                tok = P.op("v", lambda e, n: e.tensor_copy(out=cs(7), in_=A[:, asl_(0)]),
                           deps=bAs[0].rdeps() + bCs[7].wdeps())
                bCs[7].setw(tok); bAs[0].addr(tok)
            else:
                tok = P.op("v", lambda e, n, a=a: e.tensor_tensor(out=cs(6), in0=A[:, asl_(a)], in1=cs(6),
                           op=ALU.max), deps=bAs[a].rdeps() + bCs[6].rdeps() + bCs[6].wdeps())
                bCs[6].setw(tok); bAs[a].addr(tok)
                tok = P.op("v", lambda e, n, a=a: e.tensor_tensor(out=cs(7), in0=A[:, asl_(a)], in1=cs(7),
                           op=ALU.min), deps=bAs[a].rdeps() + bCs[7].rdeps() + bCs[7].wdeps())
                bCs[7].setw(tok); bAs[a].addr(tok)
        tok = P.op("v", lambda e, n: e.tensor_scalar(out=cs(7), in0=cs(7), scalar1=-1.0, scalar2=None,
                   op0=ALU.mult), deps=bCs[7].rdeps() + bCs[7].wdeps())
        bCs[7].setw(tok)
        tok = P.op("v", lambda e, n: e.tensor_tensor(out=cs(6), in0=cs(7), in1=cs(6), op=ALU.max),
                   deps=bCs[6].rdeps() + bCs[7].rdeps() + bCs[6].wdeps())
        bCs[6].setw(tok); bCs[7].addr(tok)
        tok = P.op("g", lambda e, n: e.partition_all_reduce(out_ap=cs(9), in_ap=cs(6),
                   channels=128, reduce_op=ReduceOp.max), deps=bCs[6].rdeps() + bCs[9].wdeps())
        bCs[9].setw(tok); bCs[6].addr(tok)
        tok = P.op("v", lambda e, n: e.tensor_scalar(out=cs(9), in0=cs(9), scalar1=1e-5, scalar2=None,
                   op0=ALU.max), deps=bCs[9].rdeps() + bCs[9].wdeps())
        bCs[9].setw(tok)
        tok = P.op("v", lambda e, n: e.tensor_scalar(out=cs(8), in0=cs(9), scalar1=1.0 / 127.0, scalar2=None,
                   op0=ALU.mult), deps=bCs[9].rdeps() + bCs[8].wdeps())
        bCs[8].setw(tok); bCs[9].addr(tok)
        tok = P.op("v", lambda e, n: e.reciprocal(out=cs(9), in_=cs(9)), deps=bCs[9].rdeps() + bCs[9].wdeps())
        bCs[9].setw(tok)
        tok = P.op("v", lambda e, n: e.tensor_scalar(out=cs(9), in0=cs(9), scalar1=127.0, scalar2=None,
                   op0=ALU.mult), deps=bCs[9].rdeps() + bCs[9].wdeps())
        bCs[9].setw(tok)
        QF = C[:, 4 * LC:4 * LC + LC // 2].bitcast(F16)
        for a in range(NA):
            t5_ = P.op("v", lambda e, n, a=a: e.tensor_tensor(out=cs(5), in0=A[:, asl_(a)], in1=cs(9),
                       op=ALU.mult), deps=bAs[a].rdeps() + bCs[9].rdeps() + bCs[5].wdeps())
            bCs[5].setw(t5_); bAs[a].addr(t5_); bCs[9].addr(t5_)
            t6_ = P.op("v", lambda e, n: e.tensor_scalar(out=QF[:], in0=cs(5), scalar1=12584448.0,
                       scalar2=-12582912.0, op0=ALU.add, op1=ALU.add), deps=bCs[5].rdeps() + bCs[4].wdeps())
            bCs[4].setw(t6_); bCs[5].addr(t6_)
            t7_ = P.op("v", lambda e, n, a=a: e.tensor_scalar(out=Ab[:, asl_(a)], in0=QF[:],
                       scalar1=-1536.0, scalar2=None, op0=ALU.add),
                       deps=bCs[4].rdeps() + bXQ.wdeps() + bAs[a // 2].wdeps())
            bXQ.setw(t7_); bCs[4].addr(t7_); bAs[a // 2].addr(t7_)

        def bsl(i):
            return B[:, i * LC:(i + 1) * LC]

        bF = [Bu() for _ in range(16)]
        fgateB = [t for b in bBs for t in b.wdeps()]
        hf_toks = []
        for ht in range(HT):
            par = ht % 2
            w1strip = C[:, (0 if par == 0 else 2) * LC:(0 if par == 0 else 2) * LC + 2048]
            w3strip = C[:, (4 if par == 0 else 6) * LC:(4 if par == 0 else 6) * LC + 2048]
            cw1 = bCs[0] if par == 0 else bCs[2]
            cw3 = bCs[4] if par == 0 else bCs[6]
            tok = P.dma("y", lambda e, n, ht=ht, w1strip=w1strip: e.dma_start(
                out=w1strip.rearrange("p (a c) -> p a c", a=16),
                in_=w1T.rearrange("(a p) h -> p a h", p=128)[:, :, ht * 128:(ht + 1) * 128]),
                deps=cw1.wdeps() + (fgateB if ht < 2 else []))
            cw1.setw(tok)
            tok = P.dma("s", lambda e, n, ht=ht, w3strip=w3strip: e.dma_start(
                out=w3strip.rearrange("p (a c) -> p a c", a=16),
                in_=w3T.rearrange("(a p) h -> p a h", p=128)[:, :, ht * 128:(ht + 1) * 128]),
                deps=cw3.wdeps() + (fgateB if ht < 2 else []))
            cw3.setw(tok)
            MG1 = B[:, par * LC:par * LC + 1024].bitcast(F16)
            MG3 = B[:, (2 + par) * LC:(2 + par) * LC + 1024].bitcast(F16)
            W1Q = B[:, (4 + par) * LC:(4 + par) * LC + 1024].bitcast(BF16)
            W3Q = B[:, (6 + par) * LC:(6 + par) * LC + 1024].bitcast(BF16)

            def wqchain(strip, cw, MG, WQ, bmg, bwq, m):
                t0 = P.op("v", lambda e, n, strip=strip, m=m: e.tensor_scalar(
                    out=strip, in0=strip, scalar1=swinv(m), scalar2=None, op0=ALU.mult),
                    deps=cw.rdeps() + bSWB.rdeps() + cw.wdeps() + (fgateB if ht < 2 else []))
                cw.setw(t0)
                t1 = P.op("v", lambda e, n, strip=strip: e.tensor_scalar(
                    out=strip, in0=strip, scalar1=-1.45, scalar2=1.45, op0=ALU.max, op1=ALU.min),
                    deps=cw.rdeps() + cw.wdeps())
                cw.setw(t1)
                t2 = P.op("v", lambda e, n, strip=strip, MG=MG: e.tensor_scalar(
                    out=MG[:], in0=strip, scalar1=12584448.0, scalar2=-12582912.0,
                    op0=ALU.add, op1=ALU.add), deps=cw.rdeps() + bmg.wdeps())
                bmg.setw(t2); cw.addr(t2)
                t3 = P.op("v", lambda e, n, MG=MG, WQ=WQ: e.tensor_scalar(out=WQ[:], in0=MG[:],
                          scalar1=-1536.0, scalar2=None, op0=ALU.add),
                          deps=bmg.rdeps() + bwq.wdeps())
                bwq.setw(t3); bmg.addr(t3)
                return t3
            tq1 = wqchain(w1strip, cw1, MG1, W1Q, bF[par], bF[4 + par], 0)
            tq3 = wqchain(w3strip, cw3, MG3, W3Q, bF[2 + par], bF[6 + par], 1)
            psb1 = bPS[0] if par == 0 else bPS[2]
            psb3 = bPS[1] if par == 0 else bPS[3]
            ps_1 = ps1 if par == 0 else ps3
            ps_3 = ps2 if par == 0 else ps4
            for a in range(NA):
                for h in range(NHF):
                    hsl = slice(h * FH, (h + 1) * FH)
                    xsl = slice(a * LC + h * FH, a * LC + (h + 1) * FH)
                    tok = P.op("t", lambda e, n, a=a, hsl=hsl, xsl=xsl, W1Q=W1Q, ps_1=ps_1: e.matmul(
                        out=ps_1[:, hsl], lhsT=W1Q[:, a * 128:(a + 1) * 128], rhs=Ab[:, xsl],
                        start=(a == 0), stop=(a == NA - 1)),
                        deps=[tq1] + bXQ.rdeps() + (psb1.wdeps() if a == 0 and h == 0 else []))
                    bXQ.addr(tok); bF[4 + par].addr(tok)
                    if a == NA - 1:
                        psb1.setw(tok)
            for a in range(NA):
                for h in range(NHF):
                    hsl = slice(h * FH, (h + 1) * FH)
                    xsl = slice(a * LC + h * FH, a * LC + (h + 1) * FH)
                    tok = P.op("t", lambda e, n, a=a, hsl=hsl, xsl=xsl, W3Q=W3Q, ps_3=ps_3: e.matmul(
                        out=ps_3[:, hsl], lhsT=W3Q[:, a * 128:(a + 1) * 128], rhs=Ab[:, xsl],
                        start=(a == 0), stop=(a == NA - 1)),
                        deps=[tq3] + bXQ.rdeps() + (psb3.wdeps() if a == 0 and h == 0 else []))
                    bXQ.addr(tok); bF[6 + par].addr(tok)
                    if a == NA - 1:
                        psb3.setw(tok)
            tt1 = P.op("v", lambda e, n, ps_1=ps_1: e.tensor_tensor(out=bsl(10), in0=ps_1[:], in1=cs(8),
                       op=ALU.mult), deps=psb1.rdeps() + bCs[8].rdeps() + bF[10].wdeps())
            bF[10].setw(tt1); psb1.addr(tt1); bCs[8].addr(tt1)
            ts1 = P.op("s", lambda e, n: e.activation(out=bsl(10), in_=bsl(10), func=AF.Silu,
                       scale=swmean(0)), deps=bF[10].rdeps() + bSWB.rdeps() + bF[10].wdeps())
            bF[10].setw(ts1)
            tt3 = P.op("v", lambda e, n, ps_3=ps_3: e.tensor_tensor(out=bsl(11), in0=ps_3[:], in1=cs(8),
                       op=ALU.mult), deps=psb3.rdeps() + bCs[8].rdeps() + bF[11].wdeps())
            bF[11].setw(tt3); psb3.addr(tt3); bCs[8].addr(tt3)
            thf = P.op("v", lambda e, n: e.scalar_tensor_tensor(out=bsl(12), in0=bsl(10), scalar=swmean(1),
                       in1=bsl(11), op0=ALU.mult, op1=ALU.mult),
                       deps=bF[10].rdeps() + bF[11].rdeps() + bSWB.rdeps() + bF[12].wdeps())
            bF[12].setw(thf); bF[10].addr(thf); bF[11].addr(thf)
            if ht == 0:
                tok = P.op("v", lambda e, n: e.tensor_copy(out=bsl(8), in_=bsl(12)),
                           deps=bF[12].rdeps() + bF[8].wdeps())
                bF[8].setw(tok); bF[12].addr(tok)
                tok = P.op("v", lambda e, n: e.tensor_copy(out=bsl(9), in_=bsl(12)),
                           deps=bF[12].rdeps() + bF[9].wdeps())
                bF[9].setw(tok); bF[12].addr(tok)
            else:
                tok = P.op("v", lambda e, n: e.tensor_tensor(out=bsl(8), in0=bsl(12), in1=bsl(8),
                           op=ALU.max), deps=bF[12].rdeps() + bF[8].rdeps() + bF[8].wdeps())
                bF[8].setw(tok); bF[12].addr(tok)
                tok = P.op("v", lambda e, n: e.tensor_tensor(out=bsl(9), in0=bsl(12), in1=bsl(9),
                           op=ALU.min), deps=bF[12].rdeps() + bF[9].rdeps() + bF[9].wdeps())
                bF[9].setw(tok); bF[12].addr(tok)
            tok = P.dma("g", lambda e, n, ht=ht: e.dma_start(out=hfh[:, ht * LC:(ht + 1) * LC],
                        in_=bsl(12)), deps=bF[12].rdeps() + bHFH.wdeps())
            bF[12].addr(tok); bHFH.setw(tok); hf_toks.append(tok)
        if debug:
            tokd = P.dma("g", lambda e, n: e.dma_start(out=dbgHF[:], in_=hfh[:]), deps=hf_toks)
            bHFH.addr(tokd)

        tok = P.op("v", lambda e, n: e.tensor_scalar(out=bsl(9), in0=bsl(9), scalar1=-1.0, scalar2=None,
                   op0=ALU.mult), deps=bF[9].rdeps() + bF[9].wdeps())
        bF[9].setw(tok)
        tok = P.op("v", lambda e, n: e.tensor_tensor(out=bsl(8), in0=bsl(9), in1=bsl(8), op=ALU.max),
                   deps=bF[8].rdeps() + bF[9].rdeps() + bF[8].wdeps())
        bF[8].setw(tok); bF[9].addr(tok)
        tok = P.op("g", lambda e, n: e.partition_all_reduce(out_ap=cs(9), in_ap=bsl(8),
                   channels=128, reduce_op=ReduceOp.max), deps=bF[8].rdeps() + bCs[9].wdeps() + bCs[9].rdeps())
        bCs[9].setw(tok); bF[8].addr(tok)
        tok = P.op("v", lambda e, n: e.tensor_scalar(out=cs(9), in0=cs(9), scalar1=1e-5, scalar2=None,
                   op0=ALU.max), deps=bCs[9].rdeps() + bCs[9].wdeps())
        bCs[9].setw(tok)
        tok = P.op("v", lambda e, n: e.tensor_scalar(out=cs(8), in0=cs(9), scalar1=1.0 / 127.0, scalar2=None,
                   op0=ALU.mult), deps=bCs[9].rdeps() + bCs[8].wdeps() + bCs[8].rdeps())
        bCs[8].setw(tok); bCs[9].addr(tok)
        tok = P.op("v", lambda e, n: e.reciprocal(out=cs(9), in_=cs(9)), deps=bCs[9].rdeps() + bCs[9].wdeps())
        bCs[9].setw(tok)
        tok = P.op("v", lambda e, n: e.tensor_scalar(out=cs(9), in0=cs(9), scalar1=127.0, scalar2=None,
                   op0=ALU.mult), deps=bCs[9].rdeps() + bCs[9].wdeps())
        bCs[9].setw(tok)

        def hq_ap(j):
            if j < 32:
                return Bb[:, j * LC:(j + 1) * LC]
            return Ab[:, (16 + (j - 32)) * LC:(17 + (j - 32)) * LC]

        bHFR = [Bu(), Bu()]
        QF2 = C[:, 4 * LC:4 * LC + LC // 2].bitcast(F16)
        allF1B = [t for b in bF for t in b.wdeps()] + [t for b in bAs for t in b.wdeps()]
        for j in range(HT):
            par = j % 2
            hfr = A[:, (14 + par) * LC:(15 + par) * LC]
            tok = P.dma("y", lambda e, n, j=j, hfr=hfr: e.dma_start(
                out=hfr, in_=hfh[:, j * LC:(j + 1) * LC]),
                deps=bHFR[par].wdeps() + [hf_toks[j]] +
                     ([t for b in bAs for t in b.wdeps()] if j < 2 else []))
            bHFR[par].setw(tok); bHFH.addr(tok)
            t5_ = P.op("v", lambda e, n, hfr=hfr: e.tensor_tensor(out=cs(5), in0=hfr, in1=cs(9),
                       op=ALU.mult), deps=bHFR[par].rdeps() + bCs[9].rdeps() + bCs[5].wdeps())
            bCs[5].setw(t5_); bHFR[par].addr(t5_); bCs[9].addr(t5_)
            t6_ = P.op("v", lambda e, n: e.tensor_scalar(out=QF2[:], in0=cs(5), scalar1=12584448.0,
                       scalar2=-12582912.0, op0=ALU.add, op1=ALU.add), deps=bCs[5].rdeps() + bCs[4].wdeps())
            bCs[4].setw(t6_); bCs[5].addr(t6_)
            t7_ = P.op("v", lambda e, n, j=j: e.tensor_scalar(out=hq_ap(j), in0=QF2[:],
                       scalar1=-1536.0, scalar2=None, op0=ALU.add),
                       deps=bCs[4].rdeps() + bHQ.wdeps() + (allF1B if j == 0 else []) + bXQ.wdeps())
            bHQ.setw(t7_); bCs[4].addr(t7_)

        def b2_ap(dout):
            if dout < 8:
                return A[:, asl_(dout)]
            return cs(dout - 8)

        bB2 = [Bu() for _ in range(NA)]
        bW2R = [Bu(), Bu()]
        bMG2 = [Bu(), Bu()]
        bW2Q = [Bu(), Bu()]
        for grp in range(4):
            for j in range(HT):
                par = j % 2
                w2raw = C[:, 9 * LC + par * 512:9 * LC + (par + 1) * 512]
                MG2 = MGW[:, par * 256:(par + 1) * 256].bitcast(F16)
                W2Q = MGW[:, 512 + par * 256:512 + (par + 1) * 256].bitcast(BF16)
                tok = P.dma("s", lambda e, n, j=j, grp=grp, w2raw=w2raw: e.dma_start(
                    out=w2raw, in_=w2T[j * 128:(j + 1) * 128, grp * 512:(grp + 1) * 512]),
                    deps=bW2R[par].wdeps() + (bCs[9].wdeps() if grp == 0 and j < 2 else []))
                bW2R[par].setw(tok)
                t0 = P.op("v", lambda e, n, w2raw=w2raw: e.tensor_scalar(
                    out=w2raw, in0=w2raw, scalar1=swinv(2), scalar2=None, op0=ALU.mult),
                    deps=bW2R[par].rdeps() + bSWB.rdeps() + bW2R[par].wdeps())
                bW2R[par].setw(t0)
                t1 = P.op("v", lambda e, n, w2raw=w2raw: e.tensor_scalar(
                    out=w2raw, in0=w2raw, scalar1=-1.45, scalar2=1.45, op0=ALU.max, op1=ALU.min),
                    deps=bW2R[par].rdeps() + bW2R[par].wdeps())
                bW2R[par].setw(t1)
                t1b = P.op("v", lambda e, n, w2raw=w2raw, MG2=MG2: e.tensor_scalar(
                    out=MG2[:], in0=w2raw, scalar1=12584448.0, scalar2=-12582912.0,
                    op0=ALU.add, op1=ALU.add), deps=bW2R[par].rdeps() + bMG2[par].wdeps())
                bMG2[par].setw(t1b); bW2R[par].addr(t1b)
                t2 = P.op("v", lambda e, n, MG2=MG2, W2Q=W2Q: e.tensor_scalar(out=W2Q[:], in0=MG2[:],
                          scalar1=-1536.0, scalar2=None, op0=ALU.add),
                          deps=bMG2[par].rdeps() + bW2Q[par].wdeps())
                bW2Q[par].setw(t2); bMG2[par].addr(t2)
                for dg in range(4):
                    for h in range(NHF):
                        hsl = slice(h * FH, (h + 1) * FH)
                        psd = bPS[dg]
                        pst = [ps1, ps2, ps3, ps4][dg]
                        tok = P.op("t", lambda e, n, dg=dg, hsl=hsl, j=j, W2Q=W2Q, pst=pst: e.matmul(
                            out=pst[:, hsl], lhsT=W2Q[:, dg * 128:(dg + 1) * 128],
                            rhs=hq_ap(j)[:, hsl], start=(j == 0), stop=(j == HT - 1)),
                            deps=bW2Q[par].rdeps() + bHQ.rdeps() +
                                 (psd.wdeps() if j == 0 and h == 0 else []))
                        bW2Q[par].addr(tok); bHQ.addr(tok)
                        if j == HT - 1:
                            psd.setw(tok)
            for dg in range(4):
                dout = grp * 4 + dg
                tok = P.op("v", lambda e, n, dg=dg, dout=dout: e.scalar_tensor_tensor(
                    out=b2_ap(dout), in0=[ps1, ps2, ps3, ps4][dg][:], scalar=swmean(2), in1=cs(8),
                    op0=ALU.mult, op1=ALU.mult),
                    deps=bPS[dg].rdeps() + bSWB.rdeps() + bCs[8].rdeps() + bB2[dout].wdeps() +
                         (bXQ.wdeps() if dout < 8 else []) + (bCs[dout - 8].wdeps() if dout >= 8 else []))
                bB2[dout].setw(tok); bPS[dg].addr(tok); bCs[8].addr(tok)
                if dout >= 8:
                    bCs[dout - 8].setw(tok)

        bCs[9].rtoks += bW2R[0].wdeps() + bW2R[1].wdeps()
        for a in range(NA):
            sq = cs(9)
            for h in range(NHF):
                hsl = slice(h * FH, (h + 1) * FH)
                tm = P.op("t", lambda e, n, a=a, hsl=hsl: e.matmul(
                    out=ps1[0:1, hsl], lhsT=ONESCF[:], rhs=b2_ap(a)[:, hsl],
                    start=(a == 0), stop=(a == NA - 1)),
                    deps=bB2[a].rdeps() + bONESCF.rdeps() + (bPS[0].wdeps() if a == 0 else []))
                bB2[a].addr(tm)
                if a == NA - 1:
                    bPS[0].setw(tm)
            tsq = P.op("s", lambda e, n, a=a, sq=sq: e.activation(out=sq, in_=b2_ap(a), func=AF.Square),
                       deps=bB2[a].rdeps() + bCs[9].wdeps() + bCs[9].rdeps())
            bCs[9].setw(tsq); bB2[a].addr(tsq)
            for h in range(NHF):
                hsl = slice(h * FH, (h + 1) * FH)
                tq = P.op("t", lambda e, n, hsl=hsl, a=a, sq=sq: e.matmul(
                    out=ps2[0:1, hsl], lhsT=ONESCF[:], rhs=sq[:, hsl],
                    start=(a == 0), stop=(a == NA - 1)),
                    deps=bCs[9].rdeps() + bONESCF.rdeps() + (bPS[1].wdeps() if a == 0 else []))
                bCs[9].addr(tq)
                if a == NA - 1:
                    bPS[1].setw(tq)
        trs, tmr = ln_rows(ps1, ps2, bPS[0], bPS[1], 1.0 / D)
        tb1 = bcast(rs(1), ps3, bPS[2], [trs])
        tb2 = bcast(rs(0), ps4, bPS[3], [tmr])
        out_toks = []
        bOUT = [Bu(), Bu()]
        for a in range(NA):
            par = a % 2
            x1r = A[:, (14 + par) * LC:(15 + par) * LC]
            oscr = cs(9) if par == 0 else MGW[:, 0:LC]
            bo = bOUT[par]
            bx = bHFR[par]
            tok = P.dma("y", lambda e, n, a=a, x1r=x1r: e.dma_start(out=x1r, in_=x1h[:, asl_(a)]),
                        deps=bx.wdeps() + [x1toks[a]])
            bx.setw(tok); bX1H.addr(tok)
            t1_ = P.op("v", lambda e, n, a=a, oscr=oscr: e.tensor_tensor(out=oscr, in0=b2_ap(a), in1=ps3[:],
                       op=ALU.mult), deps=bB2[a].rdeps() + [tb1] + bo.wdeps() +
                       (bCs[9].rdeps() + bCs[9].wdeps() if par == 0 else bMGW.rdeps() + bMGW.wdeps()))
            bo.setw(t1_); bB2[a].addr(t1_); bPS[2].addr(t1_)
            if par == 0:
                bCs[9].setw(t1_)
            else:
                bMGW.setw(t1_)
            t2_ = P.op("v", lambda e, n, oscr=oscr: e.tensor_tensor(out=oscr, in0=oscr, in1=ps4[:],
                       op=ALU.subtract), deps=bo.rdeps() + [tb2] + bo.wdeps())
            bo.setw(t2_); bPS[3].addr(t2_)
            t3_ = P.op("v", lambda e, n, oscr=oscr, x1r=x1r: e.tensor_tensor(out=oscr, in0=oscr, in1=x1r,
                       op=ALU.add), deps=bo.rdeps() + bx.rdeps() + bo.wdeps())
            bo.setw(t3_); bx.addr(t3_)
            tok = P.dma("g", lambda e, n, a=a, oscr=oscr: e.dma_start(
                out=outT[a * 128:(a + 1) * 128, :], in_=oscr), deps=bo.rdeps())
            bo.addr(tok); out_toks.append(tok)
        P.barrier("g", out_toks)
        P.barrier("v", out_toks)

        with nc.Block() as block:
            P.build_blocks(block)
    finalize(nc)
    return nc


LCK = 1024

def _build_inmaps(L0_attn_x, L0_attn_W_xproj, L0_attn_W_dt, L0_attn_b_dt,
                  L0_attn_A_log, L0_attn_D_param, L0_ffn_w1, L0_ffn_w2, L0_ffn_w3):
    import ml_dtypes
    x = np.asarray(L0_attn_x[0], np.float32)
    xT = np.ascontiguousarray(x.T)
    sel = np.zeros((32, 32 * 128), np.float32)
    for k in range(32):
        sel[k, k * 128:(k + 1) * 128] = 1.0
    common = dict(
        wxp=np.ascontiguousarray(np.asarray(L0_attn_W_xproj, np.float32).T),
        wdt=np.ascontiguousarray(np.asarray(L0_attn_W_dt, np.float32).T),
        bdt=np.ascontiguousarray(np.asarray(L0_attn_b_dt, np.float32).reshape(16, 128).T),
        alog=np.ascontiguousarray(np.asarray(L0_attn_A_log, np.float32).reshape(16, 128, 16)
                                  .transpose(1, 0, 2).reshape(128, 256)),
        dpar=np.ascontiguousarray(np.asarray(L0_attn_D_param, np.float32).reshape(16, 128).T),
        w1T=np.ascontiguousarray(np.asarray(L0_ffn_w1, np.float32).T),
        w3T=np.ascontiguousarray(np.asarray(L0_ffn_w3, np.float32).T),
        w2T=np.ascontiguousarray(np.asarray(L0_ffn_w2, np.float32).T),
        ident=np.eye(128, dtype=np.float32),
        sel32=sel.astype(ml_dtypes.bfloat16),
        ones1=np.ones((1, 128), np.float32),
        onescf=np.ones((128, 1), np.float32),
        consts=np.tile(np.array([1e-5, 1.0, 0, 0, 0, 0, 0, 0], np.float32), (128, 1)),
    )
    in_maps = []
    for i in range(8):
        m = dict(common)
        m["xTf"] = np.ascontiguousarray(xT[:, i * LCK:(i + 1) * LCK])
        in_maps.append(m)
    return in_maps


def kernel(L0_attn_x, L0_attn_W_xproj, L0_attn_W_dt, L0_attn_b_dt, L0_attn_A_log,
           L0_attn_D_param, L0_ffn_w1, L0_ffn_w2, L0_ffn_w3):
    from concourse.bass_utils import run_bass_kernel_spmd
    nc = build2(LC=LCK, debug=False)
    in_maps = _build_inmaps(L0_attn_x, L0_attn_W_xproj, L0_attn_W_dt, L0_attn_b_dt,
                            L0_attn_A_log, L0_attn_D_param, L0_ffn_w1, L0_ffn_w2, L0_ffn_w3)
    r = run_bass_kernel_spmd(nc, in_maps, core_ids=list(range(8)))
    out = np.concatenate([np.asarray(r.results[i]["outT"]).T for i in range(8)], 0)
    return out[None, :, :].astype(np.float32)
